# version 20
# baseline (speedup 1.0000x reference)
import numpy as np
import concourse.bass as bass
import concourse.mybir as mybir
from concourse.bass_utils import run_bass_kernel_spmd
from contextlib import ExitStack

F32 = mybir.dt.float32
BF16 = mybir.dt.bfloat16
AF = mybir.ActivationFunctionType
ALU = mybir.AluOpType

P = 128
B2 = 2
LL = 2048
DM = 2048
NB = 8
NST = 16
LCA = 512
NCA = LL // LCA
LCB = 256
NCB = LL // LCB
TQ = 512
NIC = 11
EPS = 1e-5
DQ = 1024


def build_nc():
    nc = bass.Bass("TRN2", target_bir_lowering=False)

    hid = nc.dram_tensor("hid", [B2, P, 16 * LL], F32, kind="ExternalInput").ap()
    hidq = nc.dram_tensor("hidq", [B2, P, 16 * TQ], F32, kind="ExternalInput").ap()
    winT = nc.dram_tensor("winT", [P, 16 * DM], F32, kind="ExternalInput").ap()
    wxT = nc.dram_tensor("wxT", [P, NB * 160], F32, kind="ExternalInput").ap()
    wdtT = nc.dram_tensor("wdtT", [P, DQ], F32, kind="ExternalInput").ap()
    woutT = nc.dram_tensor("woutT", [P, NB * DM], F32, kind="ExternalInput").ap()
    convw = nc.dram_tensor("convw", [P, NB * 4], F32, kind="ExternalInput").ap()
    convb = nc.dram_tensor("convb", [P, NB], F32, kind="ExternalInput").ap()
    dtb = nc.dram_tensor("dtb", [P, NB], F32, kind="ExternalInput").ap()
    a_mat = nc.dram_tensor("a_mat", [P, NB * NST], F32, kind="ExternalInput").ap()
    d_vec = nc.dram_tensor("d_vec", [P, NB], F32, kind="ExternalInput").ap()
    gateT = nc.dram_tensor("gateT", [P, 16 * 5632], F32, kind="ExternalInput").ap()
    upT = nc.dram_tensor("upT", [P, 16 * 5632], F32, kind="ExternalInput").ap()
    downT = nc.dram_tensor("downT", [P, 44 * DM], F32, kind="ExternalInput").ap()
    gs = nc.dram_tensor("gs", [P, 44], F32, kind="ExternalInput").ap()
    us = nc.dram_tensor("us", [P, 44], F32, kind="ExternalInput").ap()
    ds = nc.dram_tensor("ds", [P, 16], F32, kind="ExternalInput").ap()

    outy = nc.dram_tensor("outy", [B2, P, 16 * TQ], F32, kind="ExternalOutput").ap()

    xc_buf = nc.dram_tensor("xc_buf", [B2 * P, NB * LL], BF16)
    z_buf = nc.dram_tensor("z_buf", [B2 * P, NB * LL], BF16)
    ssm_p = [nc.dram_tensor(f"ssm_p{j}", [160, LL], F32) for j in range(B2)]
    ssm_r = [nc.dram_tensor(f"ssm_r{j}", [160, LL], F32) for j in range(B2)]
    outp = [nc.dram_tensor(f"outp{j}", [4, DM * TQ], F32) for j in range(B2)]
    outr = [nc.dram_tensor(f"outr{j}", [1, DM * TQ], F32) for j in range(B2)]

    GROUPS = [[0, 1, 2, 3], [4, 5, 6, 7]]
    NT_A = B2 * NCA
    NT_B = B2 * NCB

    with ExitStack() as ctx0:
        a_sb = ctx0.enter_context(nc.sbuf_tensor("a_sb", [P, NB * NST], F32))
        convw_sb = ctx0.enter_context(nc.sbuf_tensor("convw_sb", [P, NB * 4], F32))
        convb_sb = ctx0.enter_context(nc.sbuf_tensor("convb_sb", [P, NB], F32))
        dtb_sb = ctx0.enter_context(nc.sbuf_tensor("dtb_sb", [P, NB], F32))
        d_sb = ctx0.enter_context(nc.sbuf_tensor("d_sb", [P, NB], F32))
        gs_sb = ctx0.enter_context(nc.sbuf_tensor("gs_sb", [P, 44], F32))
        us_sb = ctx0.enter_context(nc.sbuf_tensor("us_sb", [P, 44], F32))
        ds_sb = ctx0.enter_context(nc.sbuf_tensor("ds_sb", [P, 16], F32))
        wxT_sb = ctx0.enter_context(nc.sbuf_tensor("wxT_sb", [P, NB * 160], BF16))
        wdtT_sb = ctx0.enter_context(nc.sbuf_tensor("wdtT_sb", [P, DQ], BF16))
        ones_c = ctx0.enter_context(nc.sbuf_tensor("ones_c", [P, 1], F32))
        ones_r = ctx0.enter_context(nc.sbuf_tensor("ones_r", [1, P], F32))
        carry = ctx0.enter_context(nc.sbuf_tensor("carry", [P, NST * NB], F32))
        halo = ctx0.enter_context(nc.sbuf_tensor("halo", [P, NB * 3], BF16))
        tmp8 = ctx0.enter_context(nc.sbuf_tensor("tmp8", [P, NB], F32))
        epsv = ctx0.enter_context(nc.sbuf_tensor("epsv", [P, 1], F32))

        S = {}
        for nm in ["wl", "vw", "ha", "va", "mmA", "mmS", "evA", "evS", "cvA",
                   "siA", "stA", "ccS", "phA",
                   "wb", "vwb", "hb", "vb", "mb_", "eb_", "spb", "vn", "sn",
                   "vg", "mo", "eo", "sb_", "ccO", "phB",
                   "hc", "vx1", "sqc", "tql", "tL", "eL", "vL", "vln",
                   "vgc", "tic", "egc", "vh", "vdc", "toc", "edc", "vp",
                   "vout", "soc", "vsf", "vym", "pya"]:
            S[nm] = ctx0.enter_context(nc.semaphore(nm))

        a3 = a_sb[:, :].rearrange("p (b n) -> p b n", b=NB)
        hid_v = hid.rearrange("j p (b l) -> j p b l", b=16)
        xcb_v = xc_buf.ap().rearrange("(j p) (b l) -> j p b l", j=B2, b=NB)
        zb_v = z_buf.ap().rearrange("(j p) (b l) -> j p b l", j=B2, b=NB)
        outp_v = [o.ap().rearrange("q (mb p l) -> q p mb l", mb=16, p=P) for o in outp]
        outr_v = [o.ap().rearrange("o (mb p l) -> o p mb l", mb=16, p=P) for o in outr]

        with ExitStack() as ctxA:
            winT_sb = ctxA.enter_context(nc.sbuf_tensor("winT_sb", [P, 16 * DM], BF16))
            hidf = ctxA.enter_context(nc.sbuf_tensor("hidf", [P, 16 * LCA], F32))
            hidb = ctxA.enter_context(nc.sbuf_tensor("hidb", [P, 16 * LCA], BF16))
            xpre = ctxA.enter_context(nc.sbuf_tensor("xpre", [P, NB * (LCA + 3)], BF16))
            xc_t = ctxA.enter_context(nc.sbuf_tensor("xc_t", [P, NB * LCA], BF16))
            zs_t = ctxA.enter_context(nc.sbuf_tensor("zs_t", [P, NB * LCA], BF16))
            accv = ctxA.enter_context(nc.sbuf_tensor("accv", [P, NB * LCA], F32))
            sst0 = ctxA.enter_context(nc.sbuf_tensor("sst0", [P, LCA], F32))
            sst1 = ctxA.enter_context(nc.sbuf_tensor("sst1", [32, LCA], F32))
            pa0 = ctxA.enter_context(nc.psum_tensor("pa0", [P, LCA], F32))
            pa1 = ctxA.enter_context(nc.psum_tensor("pa1", [P, LCA], F32))
            psS0 = ctxA.enter_context(nc.psum_tensor("psS0", [P, LCA], F32))
            psS1 = ctxA.enter_context(nc.psum_tensor("psS1", [32, LCA], F32))
            pa = [pa0, pa1]

            xpre3 = xpre[:, :].rearrange("p (b l) -> p b l", b=NB)
            hidf3 = hidf[:, :].rearrange("p (b l) -> p b l", b=16)
            hidb3 = hidb[:, :].rearrange("p (b l) -> p b l", b=16)
            xc3 = xc_t[:, :].rearrange("p (b l) -> p b l", b=NB)
            zs3 = zs_t[:, :].rearrange("p (b l) -> p b l", b=NB)
            acc3 = accv[:, :].rearrange("p (b l) -> p b l", b=NB)
            winT3 = winT_sb[:, :].rearrange("p (b e) -> p b e", b=16)
            wxT3 = wxT_sb[:, :].rearrange("p (b e) -> p b e", b=NB)
            halo3 = halo[:, :].rearrange("p (b k) -> p b k", b=NB)
            winT_v = winT.rearrange("p (b e) -> p b e", b=16)

            with nc.Block() as blk:
                @blk.sync
                def _(sync):
                    for dst, src in [(a_sb, a_mat), (convw_sb, convw), (convb_sb, convb),
                                     (dtb_sb, dtb), (d_sb, d_vec), (gs_sb, gs),
                                     (us_sb, us), (ds_sb, ds)]:
                        sync.dma_start(out=dst[:, :], in_=src).then_inc(S["wl"], 16)
                    sync.dma_start(out=hidf[:, 0:DQ], in_=wdtT).then_inc(S["wl"], 16)
                    sync.wait_ge(S["vw"], 1)
                    sync.dma_start(out=hidf[:, 0:NB * 160], in_=wxT).then_inc(S["wl"], 16)
                    for wc in range(4):
                        sync.wait_ge(S["vw"], 2 + wc)
                        sync.dma_start(out=hidf3[:, :, 0:512],
                                       in_=winT_v[:, :, wc * 512:(wc + 1) * 512]
                                       ).then_inc(S["wl"], 16)
                    for t in range(NT_A):
                        j, Ci = t // NCA, t % NCA
                        if t > 0:
                            sync.wait_ge(S["mmA"], 16 * t)
                        else:
                            sync.wait_ge(S["vw"], 6)
                        sync.dma_start(out=hidf3[:, :, :],
                                       in_=hid_v[j, :, :, Ci * LCA:(Ci + 1) * LCA]
                                       ).then_inc(S["ha"], 16)
                    sync.wait_ge(S["stA"], 64 * NT_A)
                    sync.sem_inc(S["phA"], 1)

                @blk.vector
                def _(vector):
                    vector.memset(ones_c[:, :], 1.0)
                    vector.memset(ones_r[:, :], 1.0)
                    vector.memset(epsv[:, :], EPS)
                    vector.wait_ge(S["wl"], 144)
                    vector.tensor_copy(wdtT_sb[:, :], hidf[:, 0:DQ]).then_inc(S["vw"], 1)
                    vector.wait_ge(S["wl"], 160)
                    vector.tensor_copy(wxT_sb[:, :], hidf[:, 0:NB * 160]).then_inc(S["vw"], 1)
                    for wc in range(4):
                        vector.wait_ge(S["wl"], 176 + 16 * wc)
                        vector.tensor_copy(winT3[:, :, wc * 512:(wc + 1) * 512],
                                           hidf3[:, :, 0:512]).then_inc(S["vw"], 1)
                    for t in range(NT_A):
                        j, Ci = t // NCA, t % NCA
                        vector.wait_ge(S["ha"], 16 * (t + 1))
                        vector.tensor_copy(hidb[:, :], hidf[:, :]).then_inc(S["va"], 1)
                        if Ci == 0:
                            vector.memset(halo[:, :], 0.0)
                        vector.wait_ge(S["evA"], 16 * t + 8)
                        vector.tensor_copy(xpre3[:, :, 0:3], halo3[:, :, :])
                        for db in range(NB):
                            vector.tensor_scalar(
                                out=acc3[:, db, :], in0=xpre3[:, db, 0:LCA],
                                scalar1=convw_sb[:, 4 * db:4 * db + 1],
                                scalar2=convb_sb[:, db:db + 1],
                                op0=ALU.mult, op1=ALU.add)
                        for k in range(1, 4):
                            for db in range(NB):
                                vector.scalar_tensor_tensor(
                                    out=acc3[:, db, :], in0=xpre3[:, db, k:k + LCA],
                                    scalar=convw_sb[:, 4 * db + k:4 * db + k + 1],
                                    in1=acc3[:, db, :], op0=ALU.mult, op1=ALU.add)
                        vector.tensor_copy(halo3[:, :, :], xpre3[:, :, LCA:LCA + 3]
                                           ).then_inc(S["cvA"], 1)
                    vector.wait_ge(S["stA"], 64 * NT_A)
                    vector.sem_inc(S["phA"], 1)

                @blk.tensor
                def _(tensor):
                    for t in range(NT_A):
                        tensor.wait_ge(S["va"], t + 1)
                        tensor.wait_ge(S["evA"], 16 * t)
                        for eb in range(16):
                            if eb >= 2:
                                tensor.wait_ge(S["evA"], 16 * t + eb - 1)
                            for kb in range(16):
                                mmi = tensor.matmul(
                                    pa[eb % 2][:, :],
                                    winT3[:, kb, eb * 128:(eb + 1) * 128],
                                    hidb3[:, kb, :],
                                    start=(kb == 0), stop=(kb == 15))
                                if kb == 15:
                                    mmi.then_inc(S["mmA"], 1)
                        tensor.wait_ge(S["siA"], t + 1)
                        tensor.wait_ge(S["evS"], 2 * t)
                        for kb in range(NB):
                            m0 = tensor.matmul(psS0[:, :], wxT3[:, kb, 0:128],
                                               xc3[:, kb, :], start=(kb == 0), stop=(kb == 7))
                            m1 = tensor.matmul(psS1[:, :], wxT3[:, kb, 128:160],
                                               xc3[:, kb, :], start=(kb == 0), stop=(kb == 7))
                            if kb == 7:
                                m0.then_inc(S["mmS"], 1)
                                m1.then_inc(S["mmS"], 1)
                    tensor.wait_ge(S["stA"], 64 * NT_A)
                    tensor.sem_inc(S["phA"], 1)

                @blk.scalar
                def _(scalar):
                    for t in range(NT_A):
                        scalar.wait_ge(S["cvA"], t)
                        scalar.wait_ge(S["stA"], 64 * t)
                        for eb in range(16):
                            scalar.wait_ge(S["mmA"], 16 * t + eb + 1)
                            if eb < 8:
                                scalar.activation(out=xpre3[:, eb, 3:LCA + 3],
                                                  in_=pa[eb % 2][:, :], func=AF.Copy
                                                  ).then_inc(S["evA"], 1)
                            else:
                                scalar.activation(out=zs3[:, eb - 8, :],
                                                  in_=pa[eb % 2][:, :], func=AF.Silu
                                                  ).then_inc(S["evA"], 1)
                        scalar.wait_ge(S["cvA"], t + 1)
                        scalar.activation(out=xc_t[:, :], in_=accv[:, :], func=AF.Silu
                                          ).then_inc(S["siA"], 1)
                        scalar.wait_ge(S["mmS"], 2 * t + 2)
                        scalar.copy(out=sst0[:, :], in_=psS0[:, :]).then_inc(S["evS"], 1)
                        scalar.copy(out=sst1[:, :], in_=psS1[:, :]).then_inc(S["evS"], 1)
                    scalar.wait_ge(S["stA"], 64 * NT_A)
                    scalar.sem_inc(S["phA"], 1)

                @blk.gpsimd
                def _(gpsimd):
                    for t in range(NT_A):
                        j, Ci = t // NCA, t % NCA
                        ls = slice(Ci * LCA, (Ci + 1) * LCA)
                        gpsimd.wait_ge(S["siA"], t + 1)
                        gpsimd.dma_start(out=xcb_v[j, :, :, ls], in_=xc3[:, :, :]
                                         ).then_inc(S["stA"], 16)
                        gpsimd.wait_ge(S["evA"], 16 * (t + 1))
                        gpsimd.dma_start(out=zb_v[j, :, :, ls], in_=zs3[:, :, :]
                                         ).then_inc(S["stA"], 16)
                        gpsimd.wait_ge(S["evS"], 2 * t + 1)
                        gpsimd.dma_start(out=ssm_p[j].ap()[0:128, ls],
                                         in_=sst0[:, :]).then_inc(S["stA"], 16)
                        gpsimd.wait_ge(S["evS"], 2 * t + 2)
                        gpsimd.dma_start(out=ssm_p[j].ap()[128:160, ls],
                                         in_=sst1[:, :]).then_inc(S["stA"], 16)
                        if Ci == NCA - 1:
                            gpsimd.wait_ge(S["stA"], 64 * (t + 1))
                            gpsimd.collective_compute(
                                "AllReduce", ALU.add, replica_groups=GROUPS,
                                ins=[ssm_p[j].ap().opt()], outs=[ssm_r[j].ap().opt()],
                            ).then_inc(S["ccS"], 1)
                    gpsimd.wait_ge(S["stA"], 64 * NT_A)
                    gpsimd.sem_inc(S["phA"], 1)

        with ExitStack() as ctxB:
            woutT_sb = ctxB.enter_context(nc.sbuf_tensor("woutT_sb", [P, NB * DM], BF16))
            wstageB = ctxB.enter_context(nc.sbuf_tensor("wstageB", [P, NB * 512], F32))
            dtr_f = [ctxB.enter_context(nc.sbuf_tensor(f"dtr_f{i}", [P, LCB], F32))
                     for i in range(2)]
            dtr_b = [ctxB.enter_context(nc.sbuf_tensor(f"dtr_b{i}", [P, LCB], BF16))
                     for i in range(2)]
            b_rep = [ctxB.enter_context(nc.sbuf_tensor(f"b_rep{i}", [P, NST * LCB], F32))
                     for i in range(2)]
            c_rep = ctxB.enter_context(nc.sbuf_tensor("c_rep", [P, NST * LCB], F32))

            delta = [ctxB.enter_context(nc.sbuf_tensor(f"delta{i}", [P, NB * LCB], F32))
                     for i in range(2)]
            sps = ctxB.enter_context(nc.sbuf_tensor("sps", [P, NB * LCB], F32))
            at = ctxB.enter_context(nc.sbuf_tensor("at", [P, NB * LCB], F32))
            bt = ctxB.enter_context(nc.sbuf_tensor("bt", [P, NB * LCB], F32))
            st_ = ctxB.enter_context(nc.sbuf_tensor("st_", [P, NB * LCB], F32))
            du_b = ctxB.enter_context(nc.sbuf_tensor("du_b", [P, NB * LCB], BF16))
            yac = ctxB.enter_context(nc.sbuf_tensor("yac", [P, NB * LCB], F32))
            xcB = [ctxB.enter_context(nc.sbuf_tensor(f"xcB{i}", [P, NB * LCB], BF16))
                   for i in range(2)]
            zsB = ctxB.enter_context(nc.sbuf_tensor("zsB", [P, NB * LCB], BF16))
            yg = ctxB.enter_context(nc.sbuf_tensor("yg", [P, NB * LCB], BF16))
            wo_st = ctxB.enter_context(nc.sbuf_tensor("wo_st", [P, 16 * LCB], F32))
            pd0 = ctxB.enter_context(nc.psum_tensor("pd0", [P, LCB], F32))
            pd1 = ctxB.enter_context(nc.psum_tensor("pd1", [P, LCB], F32))
            po0 = ctxB.enter_context(nc.psum_tensor("po0", [P, LCB], F32))
            po1 = ctxB.enter_context(nc.psum_tensor("po1", [P, LCB], F32))
            pd = [pd0, pd1]
            po = [po0, po1]

            woutT3 = woutT_sb[:, :].rearrange("p (b m) -> p b m", b=NB)
            wstage3 = wstageB[:, :].rearrange("p (b m) -> p b m", b=NB)
            woutT_v = woutT.rearrange("p (b m) -> p b m", b=NB)
            delta3 = [d[:, :].rearrange("p (b l) -> p b l", b=NB) for d in delta]
            sps3 = sps[:, :].rearrange("p (b l) -> p b l", b=NB)
            at2f = wstageB[:, 0:NB * LCB]
            ytmp = wstageB[:, NB * LCB:2 * NB * LCB]
            atv = [at[:, :], at2f]
            atv3 = [v.rearrange("p (b l) -> p b l", b=NB) for v in atv]
            at3 = atv3[0]
            bt3 = bt[:, :].rearrange("p (b l) -> p b l", b=NB)
            st3 = st_[:, :].rearrange("p (b l) -> p b l", b=NB)
            du3 = du_b[:, :].rearrange("p (b l) -> p b l", b=NB)
            yac3 = yac[:, :].rearrange("p (b l) -> p b l", b=NB)
            xcB3 = [x[:, :].rearrange("p (b l) -> p b l", b=NB) for x in xcB]
            zsB3 = zsB[:, :].rearrange("p (b l) -> p b l", b=NB)
            yg3 = yg[:, :].rearrange("p (b l) -> p b l", b=NB)
            wo3 = wo_st[:, :].rearrange("p (b l) -> p b l", b=16)

            with nc.Block() as blk:
                @blk.sync
                def _(sync):
                    sync.wait_ge(S["phA"], 5)
                    for wc in range(4):
                        if wc > 0:
                            sync.wait_ge(S["vwb"], wc)
                        sync.dma_start(out=wstage3[:, :, 0:512],
                                       in_=woutT_v[:, :, wc * 512:(wc + 1) * 512]
                                       ).then_inc(S["wb"], 16)
                    for t in range(NT_B):
                        j, Ci = t // NCB, t % NCB
                        pt = t % 2
                        ls = slice(Ci * LCB, (Ci + 1) * LCB)
                        if Ci == 0:
                            sync.wait_ge(S["ccS"], j + 1)
                        if t >= 2:
                            sync.wait_ge(S["vg"], t - 1)
                        sync.dma_start(out=xcB3[pt][:, :, :], in_=xcb_v[j, :, :, ls]
                                       ).then_inc(S["hb"], 16)
                        sync.dma_start(out=dtr_f[pt][:, :],
                                       in_=ssm_r[j].ap()[0:128, ls]
                                       ).then_inc(S["hb"], 16)
                        sync.dma_start(
                            out=b_rep[pt][:, :].rearrange("p (n l) -> p n l", n=NST),
                            in_=ssm_r[j].ap()[128:144, ls
                                              ].unsqueeze(0).broadcast_to([P, NST, LCB])
                        ).then_inc(S["hb"], 16)
                        if t >= 1:
                            sync.wait_ge(S["vg"], t)
                        sync.dma_start(
                            out=c_rep[:, :].rearrange("p (n l) -> p n l", n=NST),
                            in_=ssm_r[j].ap()[144:160, ls
                                              ].unsqueeze(0).broadcast_to([P, NST, LCB])
                        ).then_inc(S["hb"], 16)
                        sync.dma_start(out=zsB3[:, :, :], in_=zb_v[j, :, :, ls]
                                       ).then_inc(S["hb"], 16)
                    sync.wait_ge(S["sb_"], 16 * NT_B)
                    sync.sem_inc(S["phB"], 1)

                @blk.vector
                def _(vector):
                    vector.wait_ge(S["phA"], 5)
                    for wc in range(4):
                        vector.wait_ge(S["wb"], 16 * (wc + 1))
                        vector.tensor_copy(woutT3[:, :, wc * 512:(wc + 1) * 512],
                                           wstage3[:, :, 0:512]).then_inc(S["vwb"], 1)
                    for t in range(NT_B):
                        j, Ci = t // NCB, t % NCB
                        pt = t % 2
                        dp = delta[pt]
                        if t == 0:
                            vector.wait_ge(S["hb"], 48)
                            vector.tensor_copy(dtr_b[0][:, :], dtr_f[0][:, :]
                                               ).then_inc(S["vb"], 1)
                        vector.wait_ge(S["hb"], 80 * (t + 1))
                        if Ci == 0:
                            vector.memset(carry[:, :], 0.0)
                        vector.wait_ge(S["spb"], t + 1)
                        vector.tensor_add(dp[:, :], dp[:, :], sps[:, :]
                                          ).then_inc(S["vn"], 1)
                        vector.tensor_tensor(out=du_b[:, :], in0=dp[:, :],
                                             in1=xcB[pt][:, :], op=ALU.mult)
                        if t + 1 < NT_B:
                            vector.tensor_copy(dtr_b[(t + 1) % 2][:, :],
                                               dtr_f[(t + 1) % 2][:, :]
                                               ).then_inc(S["vb"], 1)
                        for n in range(NST):
                            ap3 = atv3[n % 2]
                            vector.wait_ge(S["sn"], NST * t + n + 1)
                            b_bc = b_rep[pt][:, n * LCB:(n + 1) * LCB].unsqueeze(1
                                        ).broadcast_to([P, NB, LCB])
                            vector.tensor_tensor(out=bt3, in0=du3, in1=b_bc, op=ALU.mult)
                            cu = carry[:, n * NB:(n + 1) * NB].unsqueeze(2)
                            for db in range(NB):
                                sci = vector.tensor_tensor_scan(
                                    out=st3[:, db, :], data0=ap3[:, db, :],
                                    data1=bt3[:, db, :],
                                    initial=carry[:, n * NB + db:n * NB + db + 1],
                                    op0=ALU.mult, op1=ALU.add)
                                if db == NB - 1:
                                    sci.then_inc(S["vsf"], 1)
                            vector.wait_ge(S["vsf"], NST * t + n + 1)
                            vector.tensor_copy(cu, st3[:, :, LCB - 1:LCB])
                            c_bc = c_rep[:, n * LCB:(n + 1) * LCB].unsqueeze(1
                                        ).broadcast_to([P, NB, LCB])
                            if n == 0:
                                vector.tensor_tensor(out=yac3, in0=st3, in1=c_bc,
                                                     op=ALU.mult)
                            else:
                                vector.tensor_tensor(
                                    out=ytmp[:, :].rearrange("p (b l) -> p b l", b=NB),
                                    in0=st3, in1=c_bc, op=ALU.mult)
                                vector.tensor_add(yac[:, :], yac[:, :], ytmp[:, :])
                        for db in range(NB):
                            vector.scalar_tensor_tensor(
                                out=yac3[:, db, :], in0=xcB3[pt][:, db, :],
                                scalar=d_sb[:, db:db + 1], in1=yac3[:, db, :],
                                op0=ALU.mult, op1=ALU.add)
                        vector.wait_ge(S["mo"], 16 * t)
                        vector.tensor_tensor(out=yg[:, :], in0=yac[:, :],
                                             in1=zsB[:, :],
                                             op=ALU.mult).then_inc(S["vg"], 1)
                    vector.wait_ge(S["sb_"], 16 * NT_B)
                    vector.sem_inc(S["phB"], 1)

                @blk.tensor
                def _(tensor):
                    tensor.wait_ge(S["phA"], 5)
                    for t in range(NT_B):
                        tensor.wait_ge(S["vb"], t + 1)
                        tensor.wait_ge(S["eb_"], 8 * t)
                        for db in range(NB):
                            if db >= 2:
                                tensor.wait_ge(S["eb_"], 8 * t + db - 1)
                            tensor.matmul(pd[db % 2][:, :],
                                          wdtT_sb[:, db * 128:(db + 1) * 128],
                                          dtr_b[t % 2][:, :], start=True, stop=True
                                          ).then_inc(S["mb_"], 1)
                        tensor.wait_ge(S["vg"], t + 1)
                        tensor.wait_ge(S["eo"], 16 * t)
                        if t == 0:
                            tensor.wait_ge(S["vwb"], 4)
                        for mb in range(16):
                            if mb >= 2:
                                tensor.wait_ge(S["eo"], 16 * t + mb - 1)
                            for kb in range(NB):
                                mmi = tensor.matmul(po[mb % 2][:, :],
                                                    woutT3[:, kb, mb * 128:(mb + 1) * 128],
                                                    yg3[:, kb, :],
                                                    start=(kb == 0), stop=(kb == 7))
                                if kb == 7:
                                    mmi.then_inc(S["mo"], 1)
                    tensor.wait_ge(S["sb_"], 16 * NT_B)
                    tensor.sem_inc(S["phB"], 1)

                @blk.scalar
                def _(scalar):
                    scalar.wait_ge(S["phA"], 5)

                    def softplus(scalar, k):
                        dk3 = delta3[k % 2]
                        for db in range(NB):
                            scalar.wait_ge(S["mb_"], 8 * k + db + 1)
                            scalar.activation(out=sps3[:, db, :], in_=pd[db % 2][:, :],
                                              func=AF.Abs, bias=dtb_sb[:, db:db + 1])
                            scalar.activation(out=dk3[:, db, :], in_=pd[db % 2][:, :],
                                              func=AF.Relu, bias=dtb_sb[:, db:db + 1]
                                              ).then_inc(S["eb_"], 1)
                        scalar.activation(out=sps[:, :], in_=sps[:, :], func=AF.Exp,
                                          scale=-1.0)
                        scalar.activation(out=sps[:, :], in_=sps[:, :], func=AF.Ln,
                                          bias=1.0).then_inc(S["spb"], 1)

                    softplus(scalar, 0)
                    for t in range(NT_B):
                        dp3s = delta3[t % 2]
                        scalar.wait_ge(S["vn"], t + 1)
                        for n in range(NST):
                            if n >= 2:
                                scalar.wait_ge(S["vsf"], NST * t + n - 1)
                            for db in range(NB):
                                ai = scalar.activation(
                                    out=atv3[n % 2][:, db, :], in_=dp3s[:, db, :],
                                    func=AF.Exp,
                                    scale=a_sb[:, db * NST + n:db * NST + n + 1])
                                if db == NB - 1:
                                    ai.then_inc(S["sn"], 1)
                        scalar.wait_ge(S["sb_"], 16 * t)
                        for mb in range(16):
                            scalar.wait_ge(S["mo"], 16 * t + mb + 1)
                            scalar.copy(out=wo3[:, mb, :], in_=po[mb % 2][:, :]
                                        ).then_inc(S["eo"], 1)
                        if t + 1 < NT_B:
                            softplus(scalar, t + 1)
                    scalar.wait_ge(S["sb_"], 16 * NT_B)
                    scalar.sem_inc(S["phB"], 1)

                @blk.gpsimd
                def _(gpsimd):
                    gpsimd.wait_ge(S["phA"], 5)
                    for t in range(NT_B):
                        j, Ci = t // NCB, t % NCB
                        qc, oc = Ci // 2, (Ci % 2) * LCB
                        gpsimd.wait_ge(S["eo"], 16 * (t + 1))
                        gpsimd.dma_start(out=outp_v[j][qc][:, :, oc:oc + LCB],
                                         in_=wo3[:, :, :]).then_inc(S["sb_"], 16)
                    gpsimd.wait_ge(S["sb_"], 16 * NT_B)
                    gpsimd.sem_inc(S["phB"], 1)
                    for j in range(B2):
                        gpsimd.collective_compute(
                            "ReduceScatter", ALU.add, replica_groups=GROUPS,
                            ins=[outp[j].ap().opt()], outs=[outr[j].ap().opt()],
                        ).then_inc(S["ccO"], 1)

        with ExitStack() as ctxC:
            x1 = ctxC.enter_context(nc.sbuf_tensor("x1", [P, 16 * TQ], F32))
            bufA = ctxC.enter_context(nc.sbuf_tensor("bufA", [P, 16 * TQ], F32))
            bufBF = ctxC.enter_context(nc.sbuf_tensor("bufBF", [P, 24576], BF16))
            h_all = ctxC.enter_context(nc.sbuf_tensor("h_all", [P, 44 * TQ], BF16))
            pre2 = ctxC.enter_context(nc.sbuf_tensor("pre2", [P, 16 * TQ], F32))
            sq1 = ctxC.enter_context(nc.sbuf_tensor("sq1", [P, TQ], F32))
            rows = ctxC.enter_context(nc.sbuf_tensor("rows", [1, 2 * TQ], F32))
            mu_bc = ctxC.enter_context(nc.sbuf_tensor("mu_bc", [P, TQ], F32))
            rs_bc = ctxC.enter_context(nc.sbuf_tensor("rs_bc", [P, TQ], F32))
            gsil = ctxC.enter_context(nc.sbuf_tensor("gsil", [P, TQ], BF16))
            uscb = ctxC.enter_context(nc.sbuf_tensor("uscb", [P, TQ], BF16))
            ffn_t = sq1
            psG = ctxC.enter_context(nc.psum_tensor("psG", [P, TQ], F32))
            psU = ctxC.enter_context(nc.psum_tensor("psU", [P, TQ], F32))
            psL0 = ctxC.enter_context(nc.psum_tensor("psL0", [1, TQ], F32))
            psL1 = ctxC.enter_context(nc.psum_tensor("psL1", [1, TQ], F32))

            x1_3 = x1[:, :].rearrange("p (b l) -> p b l", b=16)
            bufA3 = bufA[:, :].rearrange("p (b l) -> p b l", b=16)
            pre2_3 = pre2[:, :].rearrange("p (b l) -> p b l", b=16)
            dst11 = bufA[:, 0:11 * 512].rearrange("p (g m) -> p g m", g=11)
            gbf = bufBF[:, 0:8192].rearrange("p (b i) -> p b i", b=16)
            ubf = bufBF[:, 8192:16384].rearrange("p (b i) -> p b i", b=16)
            xlnb = bufBF[:, 16384:24576].rearrange("p (b l) -> p b l", b=16)
            dbf = bufBF[:, 0:22528].rearrange("p (b m) -> p b m", b=44)
            h3 = h_all[:, :].rearrange("p (b l) -> p b l", b=44)
            out3 = bufA3
            gateT_v = gateT.rearrange("p (b i) -> p b i", b=16)
            upT_v = upT.rearrange("p (b i) -> p b i", b=16)
            downT_v = downT.rearrange("p (b m) -> p b m", b=44)
            hidq_v = hidq.rearrange("j p (b l) -> j p b l", b=16)
            outy_v = outy.rearrange("j p (b l) -> j p b l", b=16)
            r_m, r_s = rows[:, 0:TQ], rows[:, TQ:2 * TQ]
            r_mu, r_v = mu_bc[0:1, :], rs_bc[0:1, :]

            SQ_J = 32
            TQL_J = 32
            TL_J = 4
            EL_J = 10
            VL_J = 4
            HC_J = 32 + NIC * 128 + 16 * 16
            VGC_J = 8 * NIC
            TIC_J = 8 * NIC
            EGC_J = 8 * NIC
            VH_J = 4 * NIC
            VDC_J = 16
            TOC_J = 16
            EDC_J = 16
            VP_J = 16

            def ln_scalar(scalar, xt3, b_sq, b_tql, b_tL, b_eL, b_vL):
                for kb in range(16):
                    scalar.wait_ge(S["tql"], b_tql + kb)
                    scalar.activation(out=sq1[:, :], in_=xt3[:, kb, :],
                                      func=AF.Square).then_inc(S["sqc"], 1)
                scalar.wait_ge(S["tql"], b_tql + 16)
                scalar.copy(out=r_m, in_=psL0[:, :]).then_inc(S["eL"], 1)
                scalar.copy(out=r_s, in_=psL1[:, :]).then_inc(S["eL"], 1)
                scalar.wait_ge(S["vL"], b_vL + 1)
                scalar.activation(out=r_v, in_=r_v, func=AF.Sqrt,
                                  bias=epsv[0:1, 0:1]).then_inc(S["eL"], 1)
                scalar.wait_ge(S["tL"], b_tL + 2)
                scalar.copy(out=mu_bc[:, :], in_=psG[:, :]).then_inc(S["eL"], 1)
                scalar.copy(out=rs_bc[:, :], in_=psU[:, :]).then_inc(S["eL"], 1)

            def ln_tensor(tensor, xt3, b_sq, b_tql, b_tL, b_eL, b_vL):
                for kb in range(16):
                    tensor.matmul(psL0[:, :], ones_c[:, 0:1], xt3[:, kb, :],
                                  start=(kb == 0), stop=(kb == 15))
                for kb in range(16):
                    tensor.wait_ge(S["sqc"], b_sq + kb + 1)
                    tensor.matmul(psL1[:, :], ones_c[:, 0:1], sq1[:, :],
                                  start=(kb == 0), stop=(kb == 15)
                                  ).then_inc(S["tql"], 1)
                tensor.wait_ge(S["vL"], b_vL + 2)
                tensor.matmul(psG[:, :], ones_r[0:1, :], r_mu, start=True, stop=True
                              ).then_inc(S["tL"], 1)
                tensor.matmul(psU[:, :], ones_r[0:1, :], r_v, start=True, stop=True
                              ).then_inc(S["tL"], 1)

            def ln_vector(vector, xt3, outt3, b_sq, b_tql, b_tL, b_eL, b_vL,
                          inc=None):
                vector.wait_ge(S["eL"], b_eL + 2)
                vector.tensor_scalar_mul(r_mu, r_m, 1.0 / DM)
                vector.tensor_tensor(out=r_v, in0=r_mu, in1=r_mu, op=ALU.mult)
                vector.scalar_tensor_tensor(out=r_v, in0=r_s, scalar=1.0 / DM,
                                            in1=r_v, op0=ALU.mult,
                                            op1=ALU.subtract).then_inc(S["vL"], 1)
                vector.wait_ge(S["eL"], b_eL + 3)
                vector.reciprocal(r_v, r_v).then_inc(S["vL"], 1)
                vector.wait_ge(S["eL"], b_eL + 5)
                mu16 = mu_bc[:, :].unsqueeze(1).broadcast_to([P, 16, TQ])
                rs16 = rs_bc[:, :].unsqueeze(1).broadcast_to([P, 16, TQ])
                vector.tensor_tensor(out=outt3, in0=xt3, in1=mu16, op=ALU.subtract)
                ti = vector.tensor_tensor(out=outt3, in0=outt3, in1=rs16, op=ALU.mult)
                if inc is not None:
                    ti.then_inc(*inc)

            with nc.Block() as blk:
                @blk.sync
                def _(sync):
                    sync.wait_ge(S["phB"], 5)
                    for j in range(B2):
                        hcb = j * HC_J
                        sync.wait_ge(S["ccO"], j + 1)
                        if j > 0:
                            sync.wait_ge(S["vout"], j)
                            sync.wait_ge(S["soc"], 16 * j)
                        sync.dma_start(out=x1_3[:, :, :], in_=hidq_v[j, :, :, :]
                                       ).then_inc(S["hc"], 16)
                        sync.dma_start(out=bufA3[:, :, :], in_=outr_v[j][0]
                                       ).then_inc(S["hc"], 16)
                        for ic in range(NIC):
                            if ic == 0:
                                sync.wait_ge(S["vx1"], j + 1)
                            for gi in range(8):
                                wsrc = gateT_v if gi < 4 else upT_v
                                sub = gi % 4
                                kbs = slice(4 * sub, 4 * sub + 4)
                                slot = gi % 4
                                g_all = 8 * ic + gi
                                if g_all >= 4:
                                    sync.wait_ge(S["vgc"], j * VGC_J + g_all - 3)
                                sync.dma_start(
                                    out=bufA3[:, 4 * slot:4 * slot + 4, :],
                                    in_=wsrc[:, kbs, ic * 512:(ic + 1) * 512]
                                    ).then_inc(S["hc"], 16)
                        for mc in range(4):
                            for sub in range(4):
                                if mc == 0 and sub == 0:
                                    sync.wait_ge(S["vgc"], (j + 1) * VGC_J)
                                sync.wait_ge(S["vdc"], j * VDC_J + 4 * mc + sub)
                                sync.dma_start(
                                    out=dst11,
                                    in_=downT_v[:, sub * 11:(sub + 1) * 11,
                                                mc * 512:(mc + 1) * 512]
                                    ).then_inc(S["hc"], 16)

                @blk.vector
                def _(vector):
                    vector.wait_ge(S["phB"], 5)
                    for j in range(B2):
                        hcb = j * HC_J
                        vector.wait_ge(S["hc"], hcb + 32)
                        vector.tensor_add(x1[:, :], x1[:, :], bufA[:, :]
                                          ).then_inc(S["vx1"], 1)
                        ln_vector(vector, x1_3, x1_3, j * SQ_J, j * TQL_J, j * TL_J,
                                  j * EL_J, j * VL_J)
                        if j > 0:
                            vector.wait_ge(S["toc"], j * TOC_J)
                        vector.tensor_copy(xlnb, x1_3).then_inc(S["vln"], 1)
                        for ic in range(NIC):
                            for gi in range(8):
                                g_all = 8 * ic + gi
                                if g_all >= 8:
                                    vector.wait_ge(S["tic"], j * TIC_J + 8 * ic)
                                vector.wait_ge(S["hc"], hcb + 32 + 16 * (g_all + 1))
                                dst = gbf if gi < 4 else ubf
                                sub = gi % 4
                                slot = gi % 4
                                vector.tensor_copy(dst[:, 4 * sub:4 * sub + 4, :],
                                                   bufA3[:, 4 * slot:4 * slot + 4, :]
                                                   ).then_inc(S["vgc"], 1)
                            for ib in range(4):
                                vector.wait_ge(S["egc"],
                                               j * EGC_J + 8 * ic + 2 * (ib + 1))
                                vector.tensor_tensor(
                                    out=h3[:, ic * 4 + ib, :], in0=gsil[:, :],
                                    in1=uscb[:, :], op=ALU.mult).then_inc(S["vh"], 1)
                        for mc in range(4):
                            for sub in range(4):
                                if mc == 0 and sub == 0:
                                    vector.wait_ge(S["tic"], (j + 1) * TIC_J)
                                else:
                                    vector.wait_ge(S["toc"], j * TOC_J + 4 * mc)
                                vector.wait_ge(S["hc"],
                                               hcb + 32 + 128 * NIC + 16 * (4 * mc + sub + 1))
                                vector.tensor_copy(dbf[:, sub * 11:(sub + 1) * 11, :],
                                                   dst11).then_inc(S["vdc"], 1)
                            for mb in range(4):
                                idx = 4 * mc + mb
                                vector.wait_ge(S["edc"], j * EDC_J + idx + 1)
                                vector.tensor_add(pre2_3[:, idx, :], ffn_t[:, :],
                                                  x1_3[:, idx, :]).then_inc(S["vp"], 1)
                        ln_vector(vector, pre2_3, out3, j * SQ_J + 16, j * TQL_J + 16,
                                  j * TL_J + 2, j * EL_J + 5, j * VL_J + 2,
                                  inc=(S["vout"], 1))

                @blk.tensor
                def _(tensor):
                    tensor.wait_ge(S["phB"], 5)
                    for j in range(B2):
                        tensor.wait_ge(S["vx1"], j + 1)
                        ln_tensor(tensor, x1_3, j * SQ_J, j * TQL_J, j * TL_J,
                                  j * EL_J, j * VL_J)
                        tensor.wait_ge(S["vln"], j + 1)
                        for ic in range(NIC):
                            for ib in range(4):
                                tensor.wait_ge(S["egc"], j * EGC_J + 8 * ic + 2 * ib)
                                for kb in range(16):
                                    if ib == 0 and kb % 4 == 0:
                                        tensor.wait_ge(S["vgc"],
                                                       j * VGC_J + 8 * ic + kb // 4 + 1)
                                    mmi = tensor.matmul(psG[:, :],
                                                        gbf[:, kb, ib * 128:(ib + 1) * 128],
                                                        xlnb[:, kb, :],
                                                        start=(kb == 0), stop=(kb == 15))
                                    if kb == 15:
                                        mmi.then_inc(S["tic"], 1)
                                for kb in range(16):
                                    if ib == 0 and kb % 4 == 0:
                                        tensor.wait_ge(S["vgc"],
                                                       j * VGC_J + 8 * ic + 4 + kb // 4 + 1)
                                    mmi = tensor.matmul(psU[:, :],
                                                        ubf[:, kb, ib * 128:(ib + 1) * 128],
                                                        xlnb[:, kb, :],
                                                        start=(kb == 0), stop=(kb == 15))
                                    if kb == 15:
                                        mmi.then_inc(S["tic"], 1)
                        tensor.wait_ge(S["vh"], (j + 1) * VH_J)
                        tensor.wait_ge(S["egc"], (j + 1) * EGC_J)
                        for mc in range(4):
                            for mb in range(4):
                                idx = 4 * mc + mb
                                if idx >= 2:
                                    tensor.wait_ge(S["edc"], j * EDC_J + idx - 1)
                                pot = psG if idx % 2 == 0 else psU
                                for ib in range(44):
                                    if mb == 0 and ib % 11 == 0:
                                        tensor.wait_ge(S["vdc"],
                                                       j * VDC_J + 4 * mc + ib // 11 + 1)
                                    mmi = tensor.matmul(pot[:, :],
                                                        dbf[:, ib, mb * 128:(mb + 1) * 128],
                                                        h3[:, ib, :],
                                                        start=(ib == 0), stop=(ib == 43))
                                    if ib == 43:
                                        mmi.then_inc(S["toc"], 1)
                        tensor.wait_ge(S["vp"], (j + 1) * VP_J)
                        ln_tensor(tensor, pre2_3, j * SQ_J + 16, j * TQL_J + 16,
                                  j * TL_J + 2, j * EL_J + 5, j * VL_J + 2)

                @blk.scalar
                def _(scalar):
                    scalar.wait_ge(S["phB"], 5)
                    for j in range(B2):
                        scalar.wait_ge(S["vx1"], j + 1)
                        ln_scalar(scalar, x1_3, j * SQ_J, j * TQL_J, j * TL_J,
                                  j * EL_J, j * VL_J)
                        for ic in range(NIC):
                            for ib in range(4):
                                scalar.wait_ge(S["vh"], j * VH_J + 4 * ic + ib)
                                scalar.wait_ge(S["tic"], j * TIC_J + 8 * ic + 2 * ib + 1)
                                scalar.activation(out=gsil[:, :], in_=psG[:, :],
                                                  func=AF.Silu,
                                                  scale=gs_sb[:, ic * 4 + ib:
                                                              ic * 4 + ib + 1])
                                scalar.wait_ge(S["tic"], j * TIC_J + 8 * ic + 2 * ib + 2)
                                scalar.activation(out=uscb[:, :], in_=psU[:, :],
                                                  func=AF.Copy,
                                                  scale=us_sb[:, ic * 4 + ib:
                                                              ic * 4 + ib + 1]
                                                  ).then_inc(S["egc"], 2)
                        for mc in range(4):
                            for mb in range(4):
                                idx = 4 * mc + mb
                                scalar.wait_ge(S["toc"], j * TOC_J + idx + 1)
                                if idx > 0:
                                    scalar.wait_ge(S["vp"], j * VP_J + idx)
                                pot = psG if idx % 2 == 0 else psU
                                scalar.activation(out=ffn_t[:, :], in_=pot[:, :],
                                                  func=AF.Copy,
                                                  scale=ds_sb[:, idx:idx + 1]
                                                  ).then_inc(S["edc"], 1)
                        scalar.wait_ge(S["vp"], (j + 1) * VP_J)
                        ln_scalar(scalar, pre2_3, j * SQ_J + 16, j * TQL_J + 16,
                                  j * TL_J + 2, j * EL_J + 5, j * VL_J + 2)

                @blk.gpsimd
                def _(gpsimd):
                    gpsimd.wait_ge(S["phB"], 5)
                    for j in range(B2):
                        gpsimd.wait_ge(S["vout"], j + 1)
                        gpsimd.dma_start(out=outy_v[j, :, :, :], in_=out3[:, :, :]
                                         ).then_inc(S["soc"], 16)
                    gpsimd.wait_ge(S["soc"], 16 * B2)

    return nc


def kernel(**inp):
    f32 = np.float32

    def pshape(a, nblk):
        x = np.asarray(a)
        rest = x.shape[1:]
        return np.ascontiguousarray(
            x.reshape(nblk, 128, *rest).transpose(1, 0, *range(2, 2 + len(rest)))
            .reshape(128, -1))

    hid_all = np.asarray(inp["L0_attn_hidden_states"])
    w_in = np.asarray(inp["L0_attn_W_in"])
    conv_w = np.asarray(inp["L0_attn_conv_w"])
    conv_b = np.asarray(inp["L0_attn_conv_b"])
    w_x = np.asarray(inp["L0_attn_W_x"])
    w_dt = np.asarray(inp["L0_attn_W_dt"])
    dt_bias = np.asarray(inp["L0_attn_dt_bias"])
    a_m = np.asarray(inp["L0_attn_A"])
    d_v = np.asarray(inp["L0_attn_D"])
    w_out = np.asarray(inp["L0_attn_W_out"])
    gate_w = np.asarray(inp["L0_ffn_gate_w"])
    up_w = np.asarray(inp["L0_ffn_up_w"])
    down_w = np.asarray(inp["L0_ffn_down_w"])
    gate_s = np.asarray(inp["L0_ffn_gate_s"])
    up_s = np.asarray(inp["L0_ffn_up_s"])
    down_s = np.asarray(inp["L0_ffn_down_s"])

    gateT_h = pshape(np.ascontiguousarray(gate_w.T), 16).astype(f32)
    upT_h = pshape(np.ascontiguousarray(up_w.T), 16).astype(f32)
    downT_h = pshape(np.ascontiguousarray(down_w.T), 44).astype(f32)
    gs_h = pshape(gate_s.reshape(-1, 1), 44).astype(f32)
    us_h = pshape(up_s.reshape(-1, 1), 44).astype(f32)
    ds_h = pshape(down_s.reshape(-1, 1), 16).astype(f32)

    in_maps = []
    for c in range(8):
        p, q = c // 4, c % 4
        dq = slice(1024 * q, 1024 * (q + 1))
        hid_fm = np.ascontiguousarray(hid_all[2 * p:2 * p + 2].transpose(0, 2, 1))
        hid_h = np.stack([pshape(hid_fm[jj], 16) for jj in range(2)])
        hidq_h = np.stack([pshape(np.ascontiguousarray(
            hid_fm[jj][:, 512 * q:512 * (q + 1)]), 16) for jj in range(2)])
        winT_q = np.ascontiguousarray(
            np.concatenate([w_in[dq], w_in[4096 + 1024 * q:4096 + 1024 * (q + 1)]], 0).T)
        im = {
            "hid": hid_h.astype(f32),
            "hidq": hidq_h.astype(f32),
            "winT": pshape(winT_q, 16).astype(f32),
            "wxT": pshape(np.ascontiguousarray(w_x[:, dq].T), 8).astype(f32),
            "wdtT": np.ascontiguousarray(w_dt[dq].T).astype(f32),
            "woutT": pshape(np.ascontiguousarray(w_out[:, dq].T), 8).astype(f32),
            "convw": pshape(conv_w[dq], 8).astype(f32),
            "convb": pshape(conv_b[dq].reshape(-1, 1), 8).astype(f32),
            "dtb": pshape(dt_bias[dq].reshape(-1, 1), 8).astype(f32),
            "a_mat": pshape(a_m[dq], 8).astype(f32),
            "d_vec": pshape(d_v[dq].reshape(-1, 1), 8).astype(f32),
            "gateT": gateT_h, "upT": upT_h, "downT": downT_h,
            "gs": gs_h, "us": us_h, "ds": ds_h,
        }
        in_maps.append(im)

    nc = build_nc()
    res = run_bass_kernel_spmd(nc, in_maps, core_ids=list(range(8)))

    out = np.empty((4, 2048, 2048), f32)
    for c in range(8):
        p, q = c // 4, c % 4
        oc = np.asarray(res.results[c]["outy"])
        for jj in range(2):
            om = oc[jj].reshape(128, 16, 512).transpose(1, 0, 2).reshape(2048, 512)
            out[2 * p + jj, 512 * q:512 * (q + 1), :] = om.T
    return out


# version 21
# speedup vs baseline: 1.0058x; 1.0058x over previous
import numpy as np
import concourse.bass as bass
import concourse.mybir as mybir
from concourse.bass_utils import run_bass_kernel_spmd
from contextlib import ExitStack

F32 = mybir.dt.float32
BF16 = mybir.dt.bfloat16
AF = mybir.ActivationFunctionType
ALU = mybir.AluOpType

P = 128
B2 = 2
LL = 2048
DM = 2048
NB = 8
NST = 16
LCA = 512
NCA = LL // LCA
LCB = 256
NCB = LL // LCB
TQ = 512
NIC = 11
EPS = 1e-5
DQ = 1024


def build_nc():
    nc = bass.Bass("TRN2", target_bir_lowering=False)

    hid = nc.dram_tensor("hid", [B2, P, 16 * LL], F32, kind="ExternalInput").ap()
    hidq = nc.dram_tensor("hidq", [B2, P, 16 * TQ], F32, kind="ExternalInput").ap()
    winT = nc.dram_tensor("winT", [P, 16 * DM], F32, kind="ExternalInput").ap()
    wxT = nc.dram_tensor("wxT", [P, NB * 160], F32, kind="ExternalInput").ap()
    wdtT = nc.dram_tensor("wdtT", [P, DQ], F32, kind="ExternalInput").ap()
    woutT = nc.dram_tensor("woutT", [P, NB * DM], F32, kind="ExternalInput").ap()
    convw = nc.dram_tensor("convw", [P, NB * 4], F32, kind="ExternalInput").ap()
    convb = nc.dram_tensor("convb", [P, NB], F32, kind="ExternalInput").ap()
    dtb = nc.dram_tensor("dtb", [P, NB], F32, kind="ExternalInput").ap()
    a_mat = nc.dram_tensor("a_mat", [P, NB * NST], F32, kind="ExternalInput").ap()
    d_vec = nc.dram_tensor("d_vec", [P, NB], F32, kind="ExternalInput").ap()
    gateT = nc.dram_tensor("gateT", [P, 16 * 5632], F32, kind="ExternalInput").ap()
    upT = nc.dram_tensor("upT", [P, 16 * 5632], F32, kind="ExternalInput").ap()
    downT = nc.dram_tensor("downT", [P, 44 * DM], F32, kind="ExternalInput").ap()
    gs = nc.dram_tensor("gs", [P, 44], F32, kind="ExternalInput").ap()
    us = nc.dram_tensor("us", [P, 44], F32, kind="ExternalInput").ap()
    ds = nc.dram_tensor("ds", [P, 16], F32, kind="ExternalInput").ap()

    outy = nc.dram_tensor("outy", [B2, P, 16 * TQ], F32, kind="ExternalOutput").ap()

    xc_buf = nc.dram_tensor("xc_buf", [B2 * P, NB * LL], BF16)
    z_buf = nc.dram_tensor("z_buf", [B2 * P, NB * LL], BF16)
    ssm_p = [nc.dram_tensor(f"ssm_p{j}", [160, LL], F32) for j in range(B2)]
    ssm_r = [nc.dram_tensor(f"ssm_r{j}", [160, LL], F32) for j in range(B2)]
    outp = [nc.dram_tensor(f"outp{j}", [4, DM * TQ], F32) for j in range(B2)]
    outr = [nc.dram_tensor(f"outr{j}", [1, DM * TQ], F32) for j in range(B2)]

    GROUPS = [[0, 1, 2, 3], [4, 5, 6, 7]]
    NT_A = B2 * NCA
    NT_B = B2 * NCB

    with ExitStack() as ctx0:
        a_sb = ctx0.enter_context(nc.sbuf_tensor("a_sb", [P, NB * NST], F32))
        convw_sb = ctx0.enter_context(nc.sbuf_tensor("convw_sb", [P, NB * 4], F32))
        convb_sb = ctx0.enter_context(nc.sbuf_tensor("convb_sb", [P, NB], F32))
        dtb_sb = ctx0.enter_context(nc.sbuf_tensor("dtb_sb", [P, NB], F32))
        d_sb = ctx0.enter_context(nc.sbuf_tensor("d_sb", [P, NB], F32))
        gs_sb = ctx0.enter_context(nc.sbuf_tensor("gs_sb", [P, 44], F32))
        us_sb = ctx0.enter_context(nc.sbuf_tensor("us_sb", [P, 44], F32))
        ds_sb = ctx0.enter_context(nc.sbuf_tensor("ds_sb", [P, 16], F32))
        wxT_sb = ctx0.enter_context(nc.sbuf_tensor("wxT_sb", [P, NB * 160], BF16))
        wdtT_sb = ctx0.enter_context(nc.sbuf_tensor("wdtT_sb", [P, DQ], BF16))
        ones_c = ctx0.enter_context(nc.sbuf_tensor("ones_c", [P, 1], F32))
        ones_r = ctx0.enter_context(nc.sbuf_tensor("ones_r", [1, P], F32))
        carry = ctx0.enter_context(nc.sbuf_tensor("carry", [P, NST * NB], F32))
        halo = ctx0.enter_context(nc.sbuf_tensor("halo", [P, NB * 3], BF16))
        tmp8 = ctx0.enter_context(nc.sbuf_tensor("tmp8", [P, NB], F32))
        epsv = ctx0.enter_context(nc.sbuf_tensor("epsv", [P, 1], F32))

        S = {}
        for nm in ["wl", "vw", "ha", "va", "mmA", "mmS", "evA", "evS", "cvA",
                   "siA", "stA", "ccS", "phA",
                   "wb", "vwb", "hb", "vb", "mb_", "eb_", "spb", "vn", "sn",
                   "vg", "mo", "eo", "sb_", "ccO", "phB",
                   "hc", "vx1", "sqc", "tql", "tL", "eL", "vL", "vln",
                   "vgc", "tic", "egc", "vh", "vdc", "toc", "edc", "vp",
                   "vout", "soc", "vsf", "vym", "pya"]:
            S[nm] = ctx0.enter_context(nc.semaphore(nm))

        a3 = a_sb[:, :].rearrange("p (b n) -> p b n", b=NB)
        hid_v = hid.rearrange("j p (b l) -> j p b l", b=16)
        xcb_v = xc_buf.ap().rearrange("(j p) (b l) -> j p b l", j=B2, b=NB)
        zb_v = z_buf.ap().rearrange("(j p) (b l) -> j p b l", j=B2, b=NB)
        outp_v = [o.ap().rearrange("q (mb p l) -> q p mb l", mb=16, p=P) for o in outp]
        outr_v = [o.ap().rearrange("o (mb p l) -> o p mb l", mb=16, p=P) for o in outr]

        with ExitStack() as ctxA:
            winT_sb = ctxA.enter_context(nc.sbuf_tensor("winT_sb", [P, 16 * DM], BF16))
            hidf = ctxA.enter_context(nc.sbuf_tensor("hidf", [P, 16 * LCA], F32))
            hidb = ctxA.enter_context(nc.sbuf_tensor("hidb", [P, 16 * LCA], BF16))
            xpre = ctxA.enter_context(nc.sbuf_tensor("xpre", [P, NB * (LCA + 3)], BF16))
            xc_t = ctxA.enter_context(nc.sbuf_tensor("xc_t", [P, NB * LCA], BF16))
            zs_t = ctxA.enter_context(nc.sbuf_tensor("zs_t", [P, NB * LCA], BF16))
            accv = ctxA.enter_context(nc.sbuf_tensor("accv", [P, NB * LCA], F32))
            sst0 = ctxA.enter_context(nc.sbuf_tensor("sst0", [P, LCA], F32))
            sst1 = ctxA.enter_context(nc.sbuf_tensor("sst1", [32, LCA], F32))
            pa0 = ctxA.enter_context(nc.psum_tensor("pa0", [P, LCA], F32))
            pa1 = ctxA.enter_context(nc.psum_tensor("pa1", [P, LCA], F32))
            psS0 = ctxA.enter_context(nc.psum_tensor("psS0", [P, LCA], F32))
            psS1 = ctxA.enter_context(nc.psum_tensor("psS1", [32, LCA], F32))
            pa = [pa0, pa1]

            xpre3 = xpre[:, :].rearrange("p (b l) -> p b l", b=NB)
            hidf3 = hidf[:, :].rearrange("p (b l) -> p b l", b=16)
            hidb3 = hidb[:, :].rearrange("p (b l) -> p b l", b=16)
            xc3 = xc_t[:, :].rearrange("p (b l) -> p b l", b=NB)
            zs3 = zs_t[:, :].rearrange("p (b l) -> p b l", b=NB)
            acc3 = accv[:, :].rearrange("p (b l) -> p b l", b=NB)
            winT3 = winT_sb[:, :].rearrange("p (b e) -> p b e", b=16)
            wxT3 = wxT_sb[:, :].rearrange("p (b e) -> p b e", b=NB)
            halo3 = halo[:, :].rearrange("p (b k) -> p b k", b=NB)
            winT_v = winT.rearrange("p (b e) -> p b e", b=16)

            with nc.Block() as blk:
                @blk.sync
                def _(sync):
                    for dst, src in [(a_sb, a_mat), (convw_sb, convw), (convb_sb, convb),
                                     (dtb_sb, dtb), (d_sb, d_vec), (gs_sb, gs),
                                     (us_sb, us), (ds_sb, ds)]:
                        sync.dma_start(out=dst[:, :], in_=src).then_inc(S["wl"], 16)
                    sync.dma_start(out=hidf[:, 0:DQ], in_=wdtT).then_inc(S["wl"], 16)
                    sync.wait_ge(S["vw"], 1)
                    sync.dma_start(out=hidf[:, 0:NB * 160], in_=wxT).then_inc(S["wl"], 16)
                    for wc in range(4):
                        sync.wait_ge(S["vw"], 2 + wc)
                        sync.dma_start(out=hidf3[:, :, 0:512],
                                       in_=winT_v[:, :, wc * 512:(wc + 1) * 512]
                                       ).then_inc(S["wl"], 16)
                    for t in range(NT_A):
                        j, Ci = t // NCA, t % NCA
                        if t > 0:
                            sync.wait_ge(S["mmA"], 16 * t)
                        else:
                            sync.wait_ge(S["vw"], 6)
                        sync.dma_start(out=hidf3[:, :, :],
                                       in_=hid_v[j, :, :, Ci * LCA:(Ci + 1) * LCA]
                                       ).then_inc(S["ha"], 16)
                    sync.wait_ge(S["stA"], 64 * NT_A)
                    sync.sem_inc(S["phA"], 1)

                @blk.vector
                def _(vector):
                    vector.memset(ones_c[:, :], 1.0)
                    vector.memset(ones_r[:, :], 1.0)
                    vector.memset(epsv[:, :], EPS)
                    vector.wait_ge(S["wl"], 144)
                    vector.tensor_copy(wdtT_sb[:, :], hidf[:, 0:DQ]).then_inc(S["vw"], 1)
                    vector.wait_ge(S["wl"], 160)
                    vector.tensor_copy(wxT_sb[:, :], hidf[:, 0:NB * 160]).then_inc(S["vw"], 1)
                    for wc in range(4):
                        vector.wait_ge(S["wl"], 176 + 16 * wc)
                        vector.tensor_copy(winT3[:, :, wc * 512:(wc + 1) * 512],
                                           hidf3[:, :, 0:512]).then_inc(S["vw"], 1)
                    for t in range(NT_A):
                        j, Ci = t // NCA, t % NCA
                        vector.wait_ge(S["ha"], 16 * (t + 1))
                        vector.tensor_copy(hidb[:, :], hidf[:, :]).then_inc(S["va"], 1)
                        if Ci == 0:
                            vector.memset(halo[:, :], 0.0)
                        vector.wait_ge(S["evA"], 16 * t + 8)
                        vector.tensor_copy(xpre3[:, :, 0:3], halo3[:, :, :])
                        for db in range(NB):
                            vector.tensor_scalar(
                                out=acc3[:, db, :], in0=xpre3[:, db, 0:LCA],
                                scalar1=convw_sb[:, 4 * db:4 * db + 1],
                                scalar2=convb_sb[:, db:db + 1],
                                op0=ALU.mult, op1=ALU.add)
                        for k in range(1, 4):
                            for db in range(NB):
                                vector.scalar_tensor_tensor(
                                    out=acc3[:, db, :], in0=xpre3[:, db, k:k + LCA],
                                    scalar=convw_sb[:, 4 * db + k:4 * db + k + 1],
                                    in1=acc3[:, db, :], op0=ALU.mult, op1=ALU.add)
                        vector.tensor_copy(halo3[:, :, :], xpre3[:, :, LCA:LCA + 3]
                                           ).then_inc(S["cvA"], 1)
                    vector.wait_ge(S["stA"], 64 * NT_A)
                    vector.sem_inc(S["phA"], 1)

                @blk.tensor
                def _(tensor):
                    for t in range(NT_A):
                        tensor.wait_ge(S["va"], t + 1)
                        tensor.wait_ge(S["evA"], 16 * t)
                        for eb in range(16):
                            if eb >= 2:
                                tensor.wait_ge(S["evA"], 16 * t + eb - 1)
                            for kb in range(16):
                                mmi = tensor.matmul(
                                    pa[eb % 2][:, :],
                                    winT3[:, kb, eb * 128:(eb + 1) * 128],
                                    hidb3[:, kb, :],
                                    start=(kb == 0), stop=(kb == 15))
                                if kb == 15:
                                    mmi.then_inc(S["mmA"], 1)
                        tensor.wait_ge(S["siA"], t + 1)
                        tensor.wait_ge(S["evS"], 2 * t)
                        for kb in range(NB):
                            m0 = tensor.matmul(psS0[:, :], wxT3[:, kb, 0:128],
                                               xc3[:, kb, :], start=(kb == 0), stop=(kb == 7))
                            m1 = tensor.matmul(psS1[:, :], wxT3[:, kb, 128:160],
                                               xc3[:, kb, :], start=(kb == 0), stop=(kb == 7))
                            if kb == 7:
                                m0.then_inc(S["mmS"], 1)
                                m1.then_inc(S["mmS"], 1)
                    tensor.wait_ge(S["stA"], 64 * NT_A)
                    tensor.sem_inc(S["phA"], 1)

                @blk.scalar
                def _(scalar):
                    for t in range(NT_A):
                        scalar.wait_ge(S["cvA"], t)
                        scalar.wait_ge(S["stA"], 64 * t)
                        for eb in range(16):
                            scalar.wait_ge(S["mmA"], 16 * t + eb + 1)
                            if eb < 8:
                                scalar.activation(out=xpre3[:, eb, 3:LCA + 3],
                                                  in_=pa[eb % 2][:, :], func=AF.Copy
                                                  ).then_inc(S["evA"], 1)
                            else:
                                scalar.activation(out=zs3[:, eb - 8, :],
                                                  in_=pa[eb % 2][:, :], func=AF.Silu
                                                  ).then_inc(S["evA"], 1)
                        scalar.wait_ge(S["cvA"], t + 1)
                        scalar.activation(out=xc_t[:, :], in_=accv[:, :], func=AF.Silu
                                          ).then_inc(S["siA"], 1)
                        scalar.wait_ge(S["mmS"], 2 * t + 2)
                        scalar.copy(out=sst0[:, :], in_=psS0[:, :]).then_inc(S["evS"], 1)
                        scalar.copy(out=sst1[:, :], in_=psS1[:, :]).then_inc(S["evS"], 1)
                    scalar.wait_ge(S["stA"], 64 * NT_A)
                    scalar.sem_inc(S["phA"], 1)

                @blk.gpsimd
                def _(gpsimd):
                    for t in range(NT_A):
                        j, Ci = t // NCA, t % NCA
                        ls = slice(Ci * LCA, (Ci + 1) * LCA)
                        gpsimd.wait_ge(S["siA"], t + 1)
                        gpsimd.dma_start(out=xcb_v[j, :, :, ls], in_=xc3[:, :, :]
                                         ).then_inc(S["stA"], 16)
                        gpsimd.wait_ge(S["evA"], 16 * (t + 1))
                        gpsimd.dma_start(out=zb_v[j, :, :, ls], in_=zs3[:, :, :]
                                         ).then_inc(S["stA"], 16)
                        gpsimd.wait_ge(S["evS"], 2 * t + 1)
                        gpsimd.dma_start(out=ssm_p[j].ap()[0:128, ls],
                                         in_=sst0[:, :]).then_inc(S["stA"], 16)
                        gpsimd.wait_ge(S["evS"], 2 * t + 2)
                        gpsimd.dma_start(out=ssm_p[j].ap()[128:160, ls],
                                         in_=sst1[:, :]).then_inc(S["stA"], 16)
                        if Ci == NCA - 1:
                            gpsimd.wait_ge(S["stA"], 64 * (t + 1))
                            gpsimd.collective_compute(
                                "AllReduce", ALU.add, replica_groups=GROUPS,
                                ins=[ssm_p[j].ap().opt()], outs=[ssm_r[j].ap().opt()],
                            ).then_inc(S["ccS"], 1)
                    gpsimd.wait_ge(S["stA"], 64 * NT_A)
                    gpsimd.sem_inc(S["phA"], 1)

        with ExitStack() as ctxB:
            woutT_sb = ctxB.enter_context(nc.sbuf_tensor("woutT_sb", [P, NB * DM], BF16))
            wstageB = ctxB.enter_context(nc.sbuf_tensor("wstageB", [P, NB * 512], F32))
            dtr_f = [ctxB.enter_context(nc.sbuf_tensor(f"dtr_f{i}", [P, LCB], F32))
                     for i in range(2)]
            dtr_b = [ctxB.enter_context(nc.sbuf_tensor(f"dtr_b{i}", [P, LCB], BF16))
                     for i in range(2)]
            b_rep = [ctxB.enter_context(nc.sbuf_tensor(f"b_rep{i}", [P, NST * LCB], F32))
                     for i in range(2)]
            c_rep = ctxB.enter_context(nc.sbuf_tensor("c_rep", [P, NST * LCB], F32))

            delta = [ctxB.enter_context(nc.sbuf_tensor(f"delta{i}", [P, NB * LCB], F32))
                     for i in range(2)]
            sps = ctxB.enter_context(nc.sbuf_tensor("sps", [P, NB * LCB], F32))
            at = ctxB.enter_context(nc.sbuf_tensor("at", [P, NB * LCB], F32))
            bt = ctxB.enter_context(nc.sbuf_tensor("bt", [P, NB * LCB], F32))
            st_ = ctxB.enter_context(nc.sbuf_tensor("st_", [P, NB * LCB], F32))
            du_b = ctxB.enter_context(nc.sbuf_tensor("du_b", [P, NB * LCB], BF16))
            yac = ctxB.enter_context(nc.sbuf_tensor("yac", [P, NB * LCB], F32))
            xcB = [ctxB.enter_context(nc.sbuf_tensor(f"xcB{i}", [P, NB * LCB], BF16))
                   for i in range(2)]
            zsB = ctxB.enter_context(nc.sbuf_tensor("zsB", [P, NB * LCB], BF16))
            yg = ctxB.enter_context(nc.sbuf_tensor("yg", [P, NB * LCB], BF16))
            wo_st = ctxB.enter_context(nc.sbuf_tensor("wo_st", [P, 16 * LCB], F32))
            pd0 = ctxB.enter_context(nc.psum_tensor("pd0", [P, LCB], F32))
            pd1 = ctxB.enter_context(nc.psum_tensor("pd1", [P, LCB], F32))
            po0 = ctxB.enter_context(nc.psum_tensor("po0", [P, LCB], F32))
            po1 = ctxB.enter_context(nc.psum_tensor("po1", [P, LCB], F32))
            pd = [pd0, pd1]
            po = [po0, po1]

            woutT3 = woutT_sb[:, :].rearrange("p (b m) -> p b m", b=NB)
            wstage3 = wstageB[:, :].rearrange("p (b m) -> p b m", b=NB)
            woutT_v = woutT.rearrange("p (b m) -> p b m", b=NB)
            delta3 = [d[:, :].rearrange("p (b l) -> p b l", b=NB) for d in delta]
            sps3 = sps[:, :].rearrange("p (b l) -> p b l", b=NB)
            at2f = wstageB[:, 0:NB * LCB]
            ytmp = wstageB[:, NB * LCB:2 * NB * LCB]
            atv = [at[:, :], at2f]
            atv3 = [v.rearrange("p (b l) -> p b l", b=NB) for v in atv]
            at3 = atv3[0]
            bt3 = bt[:, :].rearrange("p (b l) -> p b l", b=NB)
            st3 = st_[:, :].rearrange("p (b l) -> p b l", b=NB)
            du3 = du_b[:, :].rearrange("p (b l) -> p b l", b=NB)
            yac3 = yac[:, :].rearrange("p (b l) -> p b l", b=NB)
            xcB3 = [x[:, :].rearrange("p (b l) -> p b l", b=NB) for x in xcB]
            zsB3 = zsB[:, :].rearrange("p (b l) -> p b l", b=NB)
            yg3 = yg[:, :].rearrange("p (b l) -> p b l", b=NB)
            wo3 = wo_st[:, :].rearrange("p (b l) -> p b l", b=16)

            with nc.Block() as blk:
                @blk.sync
                def _(sync):
                    sync.wait_ge(S["phA"], 5)
                    for wc in range(4):
                        if wc > 0:
                            sync.wait_ge(S["vwb"], wc)
                        sync.dma_start(out=wstage3[:, :, 0:512],
                                       in_=woutT_v[:, :, wc * 512:(wc + 1) * 512]
                                       ).then_inc(S["wb"], 16)
                    for t in range(NT_B):
                        j, Ci = t // NCB, t % NCB
                        pt = t % 2
                        ls = slice(Ci * LCB, (Ci + 1) * LCB)
                        if Ci == 0:
                            sync.wait_ge(S["ccS"], j + 1)
                        if t >= 2:
                            sync.wait_ge(S["vg"], t - 1)
                        sync.dma_start(out=xcB3[pt][:, :, :], in_=xcb_v[j, :, :, ls]
                                       ).then_inc(S["hb"], 16)
                        sync.dma_start(out=dtr_f[pt][:, :],
                                       in_=ssm_r[j].ap()[0:128, ls]
                                       ).then_inc(S["hb"], 16)
                        sync.dma_start(
                            out=b_rep[pt][:, :].rearrange("p (n l) -> p n l", n=NST),
                            in_=ssm_r[j].ap()[128:144, ls
                                              ].unsqueeze(0).broadcast_to([P, NST, LCB])
                        ).then_inc(S["hb"], 16)
                        if t >= 1:
                            sync.wait_ge(S["vg"], t)
                        sync.dma_start(
                            out=c_rep[:, :].rearrange("p (n l) -> p n l", n=NST),
                            in_=ssm_r[j].ap()[144:160, ls
                                              ].unsqueeze(0).broadcast_to([P, NST, LCB])
                        ).then_inc(S["hb"], 16)
                        sync.dma_start(out=zsB3[:, :, :], in_=zb_v[j, :, :, ls]
                                       ).then_inc(S["hb"], 16)
                    sync.wait_ge(S["sb_"], 16 * NT_B)
                    sync.sem_inc(S["phB"], 1)

                @blk.vector
                def _(vector):
                    vector.wait_ge(S["phA"], 5)
                    for wc in range(4):
                        vector.wait_ge(S["wb"], 16 * (wc + 1))
                        vector.tensor_copy(woutT3[:, :, wc * 512:(wc + 1) * 512],
                                           wstage3[:, :, 0:512]).then_inc(S["vwb"], 1)
                    for t in range(NT_B):
                        j, Ci = t // NCB, t % NCB
                        pt = t % 2
                        dp = delta[pt]
                        if t == 0:
                            vector.wait_ge(S["hb"], 48)
                            vector.tensor_copy(dtr_b[0][:, :], dtr_f[0][:, :]
                                               ).then_inc(S["vb"], 1)
                        vector.wait_ge(S["hb"], 80 * (t + 1))
                        if Ci == 0:
                            vector.memset(carry[:, :], 0.0)
                        vector.wait_ge(S["spb"], t + 1)
                        vector.tensor_add(dp[:, :], dp[:, :], sps[:, :]
                                          ).then_inc(S["vn"], 1)
                        vector.tensor_tensor(out=du_b[:, :], in0=dp[:, :],
                                             in1=xcB[pt][:, :], op=ALU.mult)
                        if t + 1 < NT_B:
                            vector.wait_ge(S["hb"], 80 * (t + 1) + 32)
                            vector.tensor_copy(dtr_b[(t + 1) % 2][:, :],
                                               dtr_f[(t + 1) % 2][:, :]
                                               ).then_inc(S["vb"], 1)
                        for n in range(NST):
                            ap3 = atv3[n % 2]
                            vector.wait_ge(S["sn"], NST * t + n + 1)
                            b_bc = b_rep[pt][:, n * LCB:(n + 1) * LCB].unsqueeze(1
                                        ).broadcast_to([P, NB, LCB])
                            vector.tensor_tensor(out=bt3, in0=du3, in1=b_bc, op=ALU.mult)
                            cu = carry[:, n * NB:(n + 1) * NB].unsqueeze(2)
                            for db in range(NB):
                                sci = vector.tensor_tensor_scan(
                                    out=st3[:, db, :], data0=ap3[:, db, :],
                                    data1=bt3[:, db, :],
                                    initial=carry[:, n * NB + db:n * NB + db + 1],
                                    op0=ALU.mult, op1=ALU.add)
                                if db == NB - 1:
                                    sci.then_inc(S["vsf"], 1)
                            vector.wait_ge(S["vsf"], NST * t + n + 1)
                            vector.tensor_copy(cu, st3[:, :, LCB - 1:LCB])
                            c_bc = c_rep[:, n * LCB:(n + 1) * LCB].unsqueeze(1
                                        ).broadcast_to([P, NB, LCB])
                            if n == 0:
                                vector.tensor_tensor(out=yac3, in0=st3, in1=c_bc,
                                                     op=ALU.mult)
                            else:
                                vector.tensor_tensor(
                                    out=ytmp[:, :].rearrange("p (b l) -> p b l", b=NB),
                                    in0=st3, in1=c_bc, op=ALU.mult)
                                vector.tensor_add(yac[:, :], yac[:, :], ytmp[:, :])
                        for db in range(NB):
                            vector.scalar_tensor_tensor(
                                out=yac3[:, db, :], in0=xcB3[pt][:, db, :],
                                scalar=d_sb[:, db:db + 1], in1=yac3[:, db, :],
                                op0=ALU.mult, op1=ALU.add)
                        vector.wait_ge(S["mo"], 16 * t)
                        vector.tensor_tensor(out=yg[:, :], in0=yac[:, :],
                                             in1=zsB[:, :],
                                             op=ALU.mult).then_inc(S["vg"], 1)
                    vector.wait_ge(S["sb_"], 16 * NT_B)
                    vector.sem_inc(S["phB"], 1)

                @blk.tensor
                def _(tensor):
                    tensor.wait_ge(S["phA"], 5)

                    def delta_mm(tensor, k):
                        tensor.wait_ge(S["vb"], k + 1)
                        tensor.wait_ge(S["eb_"], 8 * k)
                        for db in range(NB):
                            if db >= 2:
                                tensor.wait_ge(S["eb_"], 8 * k + db - 1)
                            tensor.matmul(pd[db % 2][:, :],
                                          wdtT_sb[:, db * 128:(db + 1) * 128],
                                          dtr_b[k % 2][:, :], start=True, stop=True
                                          ).then_inc(S["mb_"], 1)

                    delta_mm(tensor, 0)
                    for t in range(NT_B):
                        if t + 1 < NT_B:
                            delta_mm(tensor, t + 1)
                        tensor.wait_ge(S["vg"], t + 1)
                        tensor.wait_ge(S["eo"], 16 * t)
                        if t == 0:
                            tensor.wait_ge(S["vwb"], 4)
                        for mb in range(16):
                            if mb >= 2:
                                tensor.wait_ge(S["eo"], 16 * t + mb - 1)
                            for kb in range(NB):
                                mmi = tensor.matmul(po[mb % 2][:, :],
                                                    woutT3[:, kb, mb * 128:(mb + 1) * 128],
                                                    yg3[:, kb, :],
                                                    start=(kb == 0), stop=(kb == 7))
                                if kb == 7:
                                    mmi.then_inc(S["mo"], 1)
                    tensor.wait_ge(S["sb_"], 16 * NT_B)
                    tensor.sem_inc(S["phB"], 1)

                @blk.scalar
                def _(scalar):
                    scalar.wait_ge(S["phA"], 5)

                    def softplus(scalar, k):
                        dk3 = delta3[k % 2]
                        for db in range(NB):
                            scalar.wait_ge(S["mb_"], 8 * k + db + 1)
                            scalar.activation(out=sps3[:, db, :], in_=pd[db % 2][:, :],
                                              func=AF.Abs, bias=dtb_sb[:, db:db + 1])
                            scalar.activation(out=dk3[:, db, :], in_=pd[db % 2][:, :],
                                              func=AF.Relu, bias=dtb_sb[:, db:db + 1]
                                              ).then_inc(S["eb_"], 1)
                        scalar.activation(out=sps[:, :], in_=sps[:, :], func=AF.Exp,
                                          scale=-1.0)
                        scalar.activation(out=sps[:, :], in_=sps[:, :], func=AF.Ln,
                                          bias=1.0).then_inc(S["spb"], 1)

                    softplus(scalar, 0)
                    for t in range(NT_B):
                        dp3s = delta3[t % 2]
                        scalar.wait_ge(S["vn"], t + 1)
                        for n in range(NST):
                            if n >= 2:
                                scalar.wait_ge(S["vsf"], NST * t + n - 1)
                            for db in range(NB):
                                ai = scalar.activation(
                                    out=atv3[n % 2][:, db, :], in_=dp3s[:, db, :],
                                    func=AF.Exp,
                                    scale=a_sb[:, db * NST + n:db * NST + n + 1])
                                if db == NB - 1:
                                    ai.then_inc(S["sn"], 1)
                        if t + 1 < NT_B:
                            softplus(scalar, t + 1)
                        scalar.wait_ge(S["sb_"], 16 * t)
                        for mb in range(16):
                            scalar.wait_ge(S["mo"], 16 * t + mb + 1)
                            scalar.copy(out=wo3[:, mb, :], in_=po[mb % 2][:, :]
                                        ).then_inc(S["eo"], 1)
                    scalar.wait_ge(S["sb_"], 16 * NT_B)
                    scalar.sem_inc(S["phB"], 1)

                @blk.gpsimd
                def _(gpsimd):
                    gpsimd.wait_ge(S["phA"], 5)
                    for t in range(NT_B):
                        j, Ci = t // NCB, t % NCB
                        qc, oc = Ci // 2, (Ci % 2) * LCB
                        gpsimd.wait_ge(S["eo"], 16 * (t + 1))
                        gpsimd.dma_start(out=outp_v[j][qc][:, :, oc:oc + LCB],
                                         in_=wo3[:, :, :]).then_inc(S["sb_"], 16)
                    gpsimd.wait_ge(S["sb_"], 16 * NT_B)
                    gpsimd.sem_inc(S["phB"], 1)
                    for j in range(B2):
                        gpsimd.collective_compute(
                            "ReduceScatter", ALU.add, replica_groups=GROUPS,
                            ins=[outp[j].ap().opt()], outs=[outr[j].ap().opt()],
                        ).then_inc(S["ccO"], 1)

        with ExitStack() as ctxC:
            x1 = ctxC.enter_context(nc.sbuf_tensor("x1", [P, 16 * TQ], F32))
            bufA = ctxC.enter_context(nc.sbuf_tensor("bufA", [P, 16 * TQ], F32))
            bufBF = ctxC.enter_context(nc.sbuf_tensor("bufBF", [P, 24576], BF16))
            h_all = ctxC.enter_context(nc.sbuf_tensor("h_all", [P, 44 * TQ], BF16))
            pre2 = ctxC.enter_context(nc.sbuf_tensor("pre2", [P, 16 * TQ], F32))
            sq1 = ctxC.enter_context(nc.sbuf_tensor("sq1", [P, TQ], F32))
            rows = ctxC.enter_context(nc.sbuf_tensor("rows", [1, 2 * TQ], F32))
            mu_bc = ctxC.enter_context(nc.sbuf_tensor("mu_bc", [P, TQ], F32))
            rs_bc = ctxC.enter_context(nc.sbuf_tensor("rs_bc", [P, TQ], F32))
            gsil = ctxC.enter_context(nc.sbuf_tensor("gsil", [P, TQ], BF16))
            uscb = ctxC.enter_context(nc.sbuf_tensor("uscb", [P, TQ], BF16))
            ffn_t = sq1
            psG = ctxC.enter_context(nc.psum_tensor("psG", [P, TQ], F32))
            psU = ctxC.enter_context(nc.psum_tensor("psU", [P, TQ], F32))
            psL0 = ctxC.enter_context(nc.psum_tensor("psL0", [1, TQ], F32))
            psL1 = ctxC.enter_context(nc.psum_tensor("psL1", [1, TQ], F32))

            x1_3 = x1[:, :].rearrange("p (b l) -> p b l", b=16)
            bufA3 = bufA[:, :].rearrange("p (b l) -> p b l", b=16)
            pre2_3 = pre2[:, :].rearrange("p (b l) -> p b l", b=16)
            dst11 = bufA[:, 0:11 * 512].rearrange("p (g m) -> p g m", g=11)
            gbf = bufBF[:, 0:8192].rearrange("p (b i) -> p b i", b=16)
            ubf = bufBF[:, 8192:16384].rearrange("p (b i) -> p b i", b=16)
            xlnb = bufBF[:, 16384:24576].rearrange("p (b l) -> p b l", b=16)
            dbf = bufBF[:, 0:22528].rearrange("p (b m) -> p b m", b=44)
            h3 = h_all[:, :].rearrange("p (b l) -> p b l", b=44)
            out3 = bufA3
            gateT_v = gateT.rearrange("p (b i) -> p b i", b=16)
            upT_v = upT.rearrange("p (b i) -> p b i", b=16)
            downT_v = downT.rearrange("p (b m) -> p b m", b=44)
            hidq_v = hidq.rearrange("j p (b l) -> j p b l", b=16)
            outy_v = outy.rearrange("j p (b l) -> j p b l", b=16)
            r_m, r_s = rows[:, 0:TQ], rows[:, TQ:2 * TQ]
            r_mu, r_v = mu_bc[0:1, :], rs_bc[0:1, :]

            SQ_J = 32
            TQL_J = 32
            TL_J = 4
            EL_J = 10
            VL_J = 4
            HC_J = 32 + NIC * 128 + 16 * 16
            VGC_J = 8 * NIC
            TIC_J = 8 * NIC
            EGC_J = 8 * NIC
            VH_J = 4 * NIC
            VDC_J = 16
            TOC_J = 16
            EDC_J = 16
            VP_J = 16

            def ln_scalar(scalar, xt3, b_sq, b_tql, b_tL, b_eL, b_vL):
                for kb in range(16):
                    scalar.wait_ge(S["tql"], b_tql + kb)
                    scalar.activation(out=sq1[:, :], in_=xt3[:, kb, :],
                                      func=AF.Square).then_inc(S["sqc"], 1)
                scalar.wait_ge(S["tql"], b_tql + 16)
                scalar.copy(out=r_m, in_=psL0[:, :]).then_inc(S["eL"], 1)
                scalar.copy(out=r_s, in_=psL1[:, :]).then_inc(S["eL"], 1)
                scalar.wait_ge(S["vL"], b_vL + 1)
                scalar.activation(out=r_v, in_=r_v, func=AF.Sqrt,
                                  bias=epsv[0:1, 0:1]).then_inc(S["eL"], 1)
                scalar.wait_ge(S["tL"], b_tL + 2)
                scalar.copy(out=mu_bc[:, :], in_=psG[:, :]).then_inc(S["eL"], 1)
                scalar.copy(out=rs_bc[:, :], in_=psU[:, :]).then_inc(S["eL"], 1)

            def ln_tensor(tensor, xt3, b_sq, b_tql, b_tL, b_eL, b_vL):
                for kb in range(16):
                    tensor.matmul(psL0[:, :], ones_c[:, 0:1], xt3[:, kb, :],
                                  start=(kb == 0), stop=(kb == 15))
                for kb in range(16):
                    tensor.wait_ge(S["sqc"], b_sq + kb + 1)
                    tensor.matmul(psL1[:, :], ones_c[:, 0:1], sq1[:, :],
                                  start=(kb == 0), stop=(kb == 15)
                                  ).then_inc(S["tql"], 1)
                tensor.wait_ge(S["vL"], b_vL + 2)
                tensor.matmul(psG[:, :], ones_r[0:1, :], r_mu, start=True, stop=True
                              ).then_inc(S["tL"], 1)
                tensor.matmul(psU[:, :], ones_r[0:1, :], r_v, start=True, stop=True
                              ).then_inc(S["tL"], 1)

            def ln_vector(vector, xt3, outt3, b_sq, b_tql, b_tL, b_eL, b_vL,
                          inc=None):
                vector.wait_ge(S["eL"], b_eL + 2)
                vector.tensor_scalar_mul(r_mu, r_m, 1.0 / DM)
                vector.tensor_tensor(out=r_v, in0=r_mu, in1=r_mu, op=ALU.mult)
                vector.scalar_tensor_tensor(out=r_v, in0=r_s, scalar=1.0 / DM,
                                            in1=r_v, op0=ALU.mult,
                                            op1=ALU.subtract).then_inc(S["vL"], 1)
                vector.wait_ge(S["eL"], b_eL + 3)
                vector.reciprocal(r_v, r_v).then_inc(S["vL"], 1)
                vector.wait_ge(S["eL"], b_eL + 5)
                mu16 = mu_bc[:, :].unsqueeze(1).broadcast_to([P, 16, TQ])
                rs16 = rs_bc[:, :].unsqueeze(1).broadcast_to([P, 16, TQ])
                vector.tensor_tensor(out=outt3, in0=xt3, in1=mu16, op=ALU.subtract)
                ti = vector.tensor_tensor(out=outt3, in0=outt3, in1=rs16, op=ALU.mult)
                if inc is not None:
                    ti.then_inc(*inc)

            with nc.Block() as blk:
                @blk.sync
                def _(sync):
                    sync.wait_ge(S["phB"], 5)
                    for j in range(B2):
                        hcb = j * HC_J
                        sync.wait_ge(S["ccO"], j + 1)
                        if j > 0:
                            sync.wait_ge(S["vout"], j)
                            sync.wait_ge(S["soc"], 16 * j)
                        sync.dma_start(out=x1_3[:, :, :], in_=hidq_v[j, :, :, :]
                                       ).then_inc(S["hc"], 16)
                        sync.dma_start(out=bufA3[:, :, :], in_=outr_v[j][0]
                                       ).then_inc(S["hc"], 16)
                        for ic in range(NIC):
                            if ic == 0:
                                sync.wait_ge(S["vx1"], j + 1)
                            for gi in range(8):
                                wsrc = gateT_v if gi < 4 else upT_v
                                sub = gi % 4
                                kbs = slice(4 * sub, 4 * sub + 4)
                                slot = gi % 4
                                g_all = 8 * ic + gi
                                if g_all >= 4:
                                    sync.wait_ge(S["vgc"], j * VGC_J + g_all - 3)
                                sync.dma_start(
                                    out=bufA3[:, 4 * slot:4 * slot + 4, :],
                                    in_=wsrc[:, kbs, ic * 512:(ic + 1) * 512]
                                    ).then_inc(S["hc"], 16)
                        for mc in range(4):
                            for sub in range(4):
                                if mc == 0 and sub == 0:
                                    sync.wait_ge(S["vgc"], (j + 1) * VGC_J)
                                sync.wait_ge(S["vdc"], j * VDC_J + 4 * mc + sub)
                                sync.dma_start(
                                    out=dst11,
                                    in_=downT_v[:, sub * 11:(sub + 1) * 11,
                                                mc * 512:(mc + 1) * 512]
                                    ).then_inc(S["hc"], 16)

                @blk.vector
                def _(vector):
                    vector.wait_ge(S["phB"], 5)
                    for j in range(B2):
                        hcb = j * HC_J
                        vector.wait_ge(S["hc"], hcb + 32)
                        vector.tensor_add(x1[:, :], x1[:, :], bufA[:, :]
                                          ).then_inc(S["vx1"], 1)
                        ln_vector(vector, x1_3, x1_3, j * SQ_J, j * TQL_J, j * TL_J,
                                  j * EL_J, j * VL_J)
                        if j > 0:
                            vector.wait_ge(S["toc"], j * TOC_J)
                        vector.tensor_copy(xlnb, x1_3).then_inc(S["vln"], 1)
                        for ic in range(NIC):
                            for gi in range(8):
                                g_all = 8 * ic + gi
                                if g_all >= 8:
                                    vector.wait_ge(S["tic"], j * TIC_J + 8 * ic)
                                vector.wait_ge(S["hc"], hcb + 32 + 16 * (g_all + 1))
                                dst = gbf if gi < 4 else ubf
                                sub = gi % 4
                                slot = gi % 4
                                vector.tensor_copy(dst[:, 4 * sub:4 * sub + 4, :],
                                                   bufA3[:, 4 * slot:4 * slot + 4, :]
                                                   ).then_inc(S["vgc"], 1)
                            for ib in range(4):
                                vector.wait_ge(S["egc"],
                                               j * EGC_J + 8 * ic + 2 * (ib + 1))
                                vector.tensor_tensor(
                                    out=h3[:, ic * 4 + ib, :], in0=gsil[:, :],
                                    in1=uscb[:, :], op=ALU.mult).then_inc(S["vh"], 1)
                        for mc in range(4):
                            for sub in range(4):
                                if mc == 0 and sub == 0:
                                    vector.wait_ge(S["tic"], (j + 1) * TIC_J)
                                else:
                                    vector.wait_ge(S["toc"], j * TOC_J + 4 * mc)
                                vector.wait_ge(S["hc"],
                                               hcb + 32 + 128 * NIC + 16 * (4 * mc + sub + 1))
                                vector.tensor_copy(dbf[:, sub * 11:(sub + 1) * 11, :],
                                                   dst11).then_inc(S["vdc"], 1)
                            for mb in range(4):
                                idx = 4 * mc + mb
                                vector.wait_ge(S["edc"], j * EDC_J + idx + 1)
                                vector.tensor_add(pre2_3[:, idx, :], ffn_t[:, :],
                                                  x1_3[:, idx, :]).then_inc(S["vp"], 1)
                        ln_vector(vector, pre2_3, out3, j * SQ_J + 16, j * TQL_J + 16,
                                  j * TL_J + 2, j * EL_J + 5, j * VL_J + 2,
                                  inc=(S["vout"], 1))

                @blk.tensor
                def _(tensor):
                    tensor.wait_ge(S["phB"], 5)
                    for j in range(B2):
                        tensor.wait_ge(S["vx1"], j + 1)
                        ln_tensor(tensor, x1_3, j * SQ_J, j * TQL_J, j * TL_J,
                                  j * EL_J, j * VL_J)
                        tensor.wait_ge(S["vln"], j + 1)
                        for ic in range(NIC):
                            for ib in range(4):
                                tensor.wait_ge(S["egc"], j * EGC_J + 8 * ic + 2 * ib)
                                for kb in range(16):
                                    if ib == 0 and kb % 4 == 0:
                                        tensor.wait_ge(S["vgc"],
                                                       j * VGC_J + 8 * ic + kb // 4 + 1)
                                    mmi = tensor.matmul(psG[:, :],
                                                        gbf[:, kb, ib * 128:(ib + 1) * 128],
                                                        xlnb[:, kb, :],
                                                        start=(kb == 0), stop=(kb == 15))
                                    if kb == 15:
                                        mmi.then_inc(S["tic"], 1)
                                for kb in range(16):
                                    if ib == 0 and kb % 4 == 0:
                                        tensor.wait_ge(S["vgc"],
                                                       j * VGC_J + 8 * ic + 4 + kb // 4 + 1)
                                    mmi = tensor.matmul(psU[:, :],
                                                        ubf[:, kb, ib * 128:(ib + 1) * 128],
                                                        xlnb[:, kb, :],
                                                        start=(kb == 0), stop=(kb == 15))
                                    if kb == 15:
                                        mmi.then_inc(S["tic"], 1)
                        tensor.wait_ge(S["vh"], (j + 1) * VH_J)
                        tensor.wait_ge(S["egc"], (j + 1) * EGC_J)
                        for mc in range(4):
                            for mb in range(4):
                                idx = 4 * mc + mb
                                if idx >= 2:
                                    tensor.wait_ge(S["edc"], j * EDC_J + idx - 1)
                                pot = psG if idx % 2 == 0 else psU
                                for ib in range(44):
                                    if mb == 0 and ib % 11 == 0:
                                        tensor.wait_ge(S["vdc"],
                                                       j * VDC_J + 4 * mc + ib // 11 + 1)
                                    mmi = tensor.matmul(pot[:, :],
                                                        dbf[:, ib, mb * 128:(mb + 1) * 128],
                                                        h3[:, ib, :],
                                                        start=(ib == 0), stop=(ib == 43))
                                    if ib == 43:
                                        mmi.then_inc(S["toc"], 1)
                        tensor.wait_ge(S["vp"], (j + 1) * VP_J)
                        ln_tensor(tensor, pre2_3, j * SQ_J + 16, j * TQL_J + 16,
                                  j * TL_J + 2, j * EL_J + 5, j * VL_J + 2)

                @blk.scalar
                def _(scalar):
                    scalar.wait_ge(S["phB"], 5)
                    for j in range(B2):
                        scalar.wait_ge(S["vx1"], j + 1)
                        ln_scalar(scalar, x1_3, j * SQ_J, j * TQL_J, j * TL_J,
                                  j * EL_J, j * VL_J)
                        for ic in range(NIC):
                            for ib in range(4):
                                scalar.wait_ge(S["vh"], j * VH_J + 4 * ic + ib)
                                scalar.wait_ge(S["tic"], j * TIC_J + 8 * ic + 2 * ib + 1)
                                scalar.activation(out=gsil[:, :], in_=psG[:, :],
                                                  func=AF.Silu,
                                                  scale=gs_sb[:, ic * 4 + ib:
                                                              ic * 4 + ib + 1])
                                scalar.wait_ge(S["tic"], j * TIC_J + 8 * ic + 2 * ib + 2)
                                scalar.activation(out=uscb[:, :], in_=psU[:, :],
                                                  func=AF.Copy,
                                                  scale=us_sb[:, ic * 4 + ib:
                                                              ic * 4 + ib + 1]
                                                  ).then_inc(S["egc"], 2)
                        for mc in range(4):
                            for mb in range(4):
                                idx = 4 * mc + mb
                                scalar.wait_ge(S["toc"], j * TOC_J + idx + 1)
                                if idx > 0:
                                    scalar.wait_ge(S["vp"], j * VP_J + idx)
                                pot = psG if idx % 2 == 0 else psU
                                scalar.activation(out=ffn_t[:, :], in_=pot[:, :],
                                                  func=AF.Copy,
                                                  scale=ds_sb[:, idx:idx + 1]
                                                  ).then_inc(S["edc"], 1)
                        scalar.wait_ge(S["vp"], (j + 1) * VP_J)
                        ln_scalar(scalar, pre2_3, j * SQ_J + 16, j * TQL_J + 16,
                                  j * TL_J + 2, j * EL_J + 5, j * VL_J + 2)

                @blk.gpsimd
                def _(gpsimd):
                    gpsimd.wait_ge(S["phB"], 5)
                    for j in range(B2):
                        gpsimd.wait_ge(S["vout"], j + 1)
                        gpsimd.dma_start(out=outy_v[j, :, :, :], in_=out3[:, :, :]
                                         ).then_inc(S["soc"], 16)
                    gpsimd.wait_ge(S["soc"], 16 * B2)

    return nc


def kernel(**inp):
    f32 = np.float32

    def pshape(a, nblk):
        x = np.asarray(a)
        rest = x.shape[1:]
        return np.ascontiguousarray(
            x.reshape(nblk, 128, *rest).transpose(1, 0, *range(2, 2 + len(rest)))
            .reshape(128, -1))

    hid_all = np.asarray(inp["L0_attn_hidden_states"])
    w_in = np.asarray(inp["L0_attn_W_in"])
    conv_w = np.asarray(inp["L0_attn_conv_w"])
    conv_b = np.asarray(inp["L0_attn_conv_b"])
    w_x = np.asarray(inp["L0_attn_W_x"])
    w_dt = np.asarray(inp["L0_attn_W_dt"])
    dt_bias = np.asarray(inp["L0_attn_dt_bias"])
    a_m = np.asarray(inp["L0_attn_A"])
    d_v = np.asarray(inp["L0_attn_D"])
    w_out = np.asarray(inp["L0_attn_W_out"])
    gate_w = np.asarray(inp["L0_ffn_gate_w"])
    up_w = np.asarray(inp["L0_ffn_up_w"])
    down_w = np.asarray(inp["L0_ffn_down_w"])
    gate_s = np.asarray(inp["L0_ffn_gate_s"])
    up_s = np.asarray(inp["L0_ffn_up_s"])
    down_s = np.asarray(inp["L0_ffn_down_s"])

    gateT_h = pshape(np.ascontiguousarray(gate_w.T), 16).astype(f32)
    upT_h = pshape(np.ascontiguousarray(up_w.T), 16).astype(f32)
    downT_h = pshape(np.ascontiguousarray(down_w.T), 44).astype(f32)
    gs_h = pshape(gate_s.reshape(-1, 1), 44).astype(f32)
    us_h = pshape(up_s.reshape(-1, 1), 44).astype(f32)
    ds_h = pshape(down_s.reshape(-1, 1), 16).astype(f32)

    in_maps = []
    for c in range(8):
        p, q = c // 4, c % 4
        dq = slice(1024 * q, 1024 * (q + 1))
        hid_fm = np.ascontiguousarray(hid_all[2 * p:2 * p + 2].transpose(0, 2, 1))
        hid_h = np.stack([pshape(hid_fm[jj], 16) for jj in range(2)])
        hidq_h = np.stack([pshape(np.ascontiguousarray(
            hid_fm[jj][:, 512 * q:512 * (q + 1)]), 16) for jj in range(2)])
        winT_q = np.ascontiguousarray(
            np.concatenate([w_in[dq], w_in[4096 + 1024 * q:4096 + 1024 * (q + 1)]], 0).T)
        im = {
            "hid": hid_h.astype(f32),
            "hidq": hidq_h.astype(f32),
            "winT": pshape(winT_q, 16).astype(f32),
            "wxT": pshape(np.ascontiguousarray(w_x[:, dq].T), 8).astype(f32),
            "wdtT": np.ascontiguousarray(w_dt[dq].T).astype(f32),
            "woutT": pshape(np.ascontiguousarray(w_out[:, dq].T), 8).astype(f32),
            "convw": pshape(conv_w[dq], 8).astype(f32),
            "convb": pshape(conv_b[dq].reshape(-1, 1), 8).astype(f32),
            "dtb": pshape(dt_bias[dq].reshape(-1, 1), 8).astype(f32),
            "a_mat": pshape(a_m[dq], 8).astype(f32),
            "d_vec": pshape(d_v[dq].reshape(-1, 1), 8).astype(f32),
            "gateT": gateT_h, "upT": upT_h, "downT": downT_h,
            "gs": gs_h, "us": us_h, "ds": ds_h,
        }
        in_maps.append(im)

    nc = build_nc()
    res = run_bass_kernel_spmd(nc, in_maps, core_ids=list(range(8)))

    out = np.empty((4, 2048, 2048), f32)
    for c in range(8):
        p, q = c // 4, c % 4
        oc = np.asarray(res.results[c]["outy"])
        for jj in range(2):
            om = oc[jj].reshape(128, 16, 512).transpose(1, 0, 2).reshape(2048, 512)
            out[2 * p + jj, 512 * q:512 * (q + 1), :] = om.T
    return out


# version 22
# speedup vs baseline: 1.0161x; 1.0102x over previous
import numpy as np
import concourse.bass as bass
import concourse.mybir as mybir
from concourse.bass_utils import run_bass_kernel_spmd
from contextlib import ExitStack

F32 = mybir.dt.float32
BF16 = mybir.dt.bfloat16
AF = mybir.ActivationFunctionType
ALU = mybir.AluOpType

P = 128
B2 = 2
LL = 2048
DM = 2048
NB = 8
NST = 16
LCA = 512
NCA = LL // LCA
LCB = 256
NCB = LL // LCB
TQ = 512
NIC = 11
EPS = 1e-5
DQ = 1024


def build_nc():
    nc = bass.Bass("TRN2", target_bir_lowering=False)

    hid = nc.dram_tensor("hid", [B2, P, 16 * LL], F32, kind="ExternalInput").ap()
    hidq = nc.dram_tensor("hidq", [B2, P, 16 * TQ], F32, kind="ExternalInput").ap()
    winT = nc.dram_tensor("winT", [P, 16 * DM], F32, kind="ExternalInput").ap()
    wxT = nc.dram_tensor("wxT", [P, NB * 160], F32, kind="ExternalInput").ap()
    wdtT = nc.dram_tensor("wdtT", [P, DQ], F32, kind="ExternalInput").ap()
    woutT = nc.dram_tensor("woutT", [P, NB * DM], F32, kind="ExternalInput").ap()
    convw = nc.dram_tensor("convw", [P, NB * 4], F32, kind="ExternalInput").ap()
    convb = nc.dram_tensor("convb", [P, NB], F32, kind="ExternalInput").ap()
    dtb = nc.dram_tensor("dtb", [P, NB], F32, kind="ExternalInput").ap()
    a_mat = nc.dram_tensor("a_mat", [P, NB * NST], F32, kind="ExternalInput").ap()
    d_vec = nc.dram_tensor("d_vec", [P, NB], F32, kind="ExternalInput").ap()
    gateT = nc.dram_tensor("gateT", [P, 16 * 5632], F32, kind="ExternalInput").ap()
    upT = nc.dram_tensor("upT", [P, 16 * 5632], F32, kind="ExternalInput").ap()
    downT = nc.dram_tensor("downT", [P, 44 * DM], F32, kind="ExternalInput").ap()
    gs = nc.dram_tensor("gs", [P, 44], F32, kind="ExternalInput").ap()
    us = nc.dram_tensor("us", [P, 44], F32, kind="ExternalInput").ap()
    ds = nc.dram_tensor("ds", [P, 16], F32, kind="ExternalInput").ap()

    outy = nc.dram_tensor("outy", [B2, P, 16 * TQ], F32, kind="ExternalOutput").ap()

    xc_buf = nc.dram_tensor("xc_buf", [B2 * P, NB * LL], BF16)
    z_buf = nc.dram_tensor("z_buf", [B2 * P, NB * LL], BF16)
    ssm_p = [nc.dram_tensor(f"ssm_p{j}", [160, LL], F32) for j in range(B2)]
    ssm_r = [nc.dram_tensor(f"ssm_r{j}", [160, LL], F32) for j in range(B2)]
    outp = [nc.dram_tensor(f"outp{j}", [4, DM * TQ], F32) for j in range(B2)]
    outr = [nc.dram_tensor(f"outr{j}", [1, DM * TQ], F32) for j in range(B2)]

    GROUPS = [[0, 1, 2, 3], [4, 5, 6, 7]]
    NT_A = B2 * NCA
    NT_B = B2 * NCB

    with ExitStack() as ctx0:
        a_sb = ctx0.enter_context(nc.sbuf_tensor("a_sb", [P, NB * NST], F32))
        convw_sb = ctx0.enter_context(nc.sbuf_tensor("convw_sb", [P, NB * 4], F32))
        convb_sb = ctx0.enter_context(nc.sbuf_tensor("convb_sb", [P, NB], F32))
        dtb_sb = ctx0.enter_context(nc.sbuf_tensor("dtb_sb", [P, NB], F32))
        d_sb = ctx0.enter_context(nc.sbuf_tensor("d_sb", [P, NB], F32))
        gs_sb = ctx0.enter_context(nc.sbuf_tensor("gs_sb", [P, 44], F32))
        us_sb = ctx0.enter_context(nc.sbuf_tensor("us_sb", [P, 44], F32))
        ds_sb = ctx0.enter_context(nc.sbuf_tensor("ds_sb", [P, 16], F32))
        wxT_sb = ctx0.enter_context(nc.sbuf_tensor("wxT_sb", [P, NB * 160], BF16))
        wdtT_sb = ctx0.enter_context(nc.sbuf_tensor("wdtT_sb", [P, DQ], BF16))
        ones_c = ctx0.enter_context(nc.sbuf_tensor("ones_c", [P, 1], F32))
        ones_r = ctx0.enter_context(nc.sbuf_tensor("ones_r", [1, P], F32))
        carry = ctx0.enter_context(nc.sbuf_tensor("carry", [P, NST * NB], F32))
        halo = ctx0.enter_context(nc.sbuf_tensor("halo", [P, NB * 3], BF16))
        tmp8 = ctx0.enter_context(nc.sbuf_tensor("tmp8", [P, NB], F32))
        epsv = ctx0.enter_context(nc.sbuf_tensor("epsv", [P, 1], F32))

        S = {}
        for nm in ["wl", "vw", "ha", "va", "mmA", "mmS", "evA", "evS", "cvA",
                   "siA", "stA", "ccS", "phA",
                   "wb", "vwb", "hb", "vb", "mb_", "eb_", "spb", "vn", "sn",
                   "vg", "mo", "eo", "sb_", "ccO", "phB",
                   "hc", "vx1", "sqc", "tql", "tL", "eL", "vL", "vln",
                   "vgc", "tic", "egc", "vh", "vdc", "toc", "edc", "vp",
                   "vout", "soc", "vsf", "vym", "pya"]:
            S[nm] = ctx0.enter_context(nc.semaphore(nm))

        a3 = a_sb[:, :].rearrange("p (b n) -> p b n", b=NB)
        hid_v = hid.rearrange("j p (b l) -> j p b l", b=16)
        xcb_v = xc_buf.ap().rearrange("(j p) (b l) -> j p b l", j=B2, b=NB)
        zb_v = z_buf.ap().rearrange("(j p) (b l) -> j p b l", j=B2, b=NB)
        outp_v = [o.ap().rearrange("q (mb p l) -> q p mb l", mb=16, p=P) for o in outp]
        outr_v = [o.ap().rearrange("o (mb p l) -> o p mb l", mb=16, p=P) for o in outr]

        with ExitStack() as ctxA:
            winT_sb = ctxA.enter_context(nc.sbuf_tensor("winT_sb", [P, 16 * DM], BF16))
            hidf = ctxA.enter_context(nc.sbuf_tensor("hidf", [P, 16 * LCA], F32))
            hidb = ctxA.enter_context(nc.sbuf_tensor("hidb", [P, 16 * LCA], BF16))
            xpre = ctxA.enter_context(nc.sbuf_tensor("xpre", [P, NB * (LCA + 3)], BF16))
            xc_t = ctxA.enter_context(nc.sbuf_tensor("xc_t", [P, NB * LCA], BF16))
            zs_t = ctxA.enter_context(nc.sbuf_tensor("zs_t", [P, NB * LCA], BF16))
            accv = ctxA.enter_context(nc.sbuf_tensor("accv", [P, NB * LCA], F32))
            sst0 = ctxA.enter_context(nc.sbuf_tensor("sst0", [P, LCA], F32))
            sst1 = ctxA.enter_context(nc.sbuf_tensor("sst1", [32, LCA], F32))
            pa0 = ctxA.enter_context(nc.psum_tensor("pa0", [P, LCA], F32))
            pa1 = ctxA.enter_context(nc.psum_tensor("pa1", [P, LCA], F32))
            psS0 = ctxA.enter_context(nc.psum_tensor("psS0", [P, LCA], F32))
            psS1 = ctxA.enter_context(nc.psum_tensor("psS1", [32, LCA], F32))
            pa = [pa0, pa1]

            xpre3 = xpre[:, :].rearrange("p (b l) -> p b l", b=NB)
            hidf3 = hidf[:, :].rearrange("p (b l) -> p b l", b=16)
            hidb3 = hidb[:, :].rearrange("p (b l) -> p b l", b=16)
            xc3 = xc_t[:, :].rearrange("p (b l) -> p b l", b=NB)
            zs3 = zs_t[:, :].rearrange("p (b l) -> p b l", b=NB)
            acc3 = accv[:, :].rearrange("p (b l) -> p b l", b=NB)
            winT3 = winT_sb[:, :].rearrange("p (b e) -> p b e", b=16)
            wxT3 = wxT_sb[:, :].rearrange("p (b e) -> p b e", b=NB)
            halo3 = halo[:, :].rearrange("p (b k) -> p b k", b=NB)
            winT_v = winT.rearrange("p (b e) -> p b e", b=16)

            with nc.Block() as blk:
                @blk.sync
                def _(sync):
                    for dst, src in [(a_sb, a_mat), (convw_sb, convw), (convb_sb, convb),
                                     (dtb_sb, dtb), (d_sb, d_vec), (gs_sb, gs),
                                     (us_sb, us), (ds_sb, ds)]:
                        sync.dma_start(out=dst[:, :], in_=src).then_inc(S["wl"], 16)
                    sync.dma_start(out=hidf[:, 0:DQ], in_=wdtT).then_inc(S["wl"], 16)
                    sync.wait_ge(S["vw"], 1)
                    sync.dma_start(out=hidf[:, 0:NB * 160], in_=wxT).then_inc(S["wl"], 16)
                    for wc in range(4):
                        sync.wait_ge(S["vw"], 2 + wc)
                        sync.dma_start(out=hidf3[:, :, 0:512],
                                       in_=winT_v[:, :, wc * 512:(wc + 1) * 512]
                                       ).then_inc(S["wl"], 16)
                    for t in range(NT_A):
                        j, Ci = t // NCA, t % NCA
                        if t > 0:
                            sync.wait_ge(S["mmA"], 16 * t)
                        else:
                            sync.wait_ge(S["vw"], 6)
                        sync.dma_start(out=hidf3[:, :, :],
                                       in_=hid_v[j, :, :, Ci * LCA:(Ci + 1) * LCA]
                                       ).then_inc(S["ha"], 16)
                    sync.wait_ge(S["stA"], 64 * NT_A)
                    sync.sem_inc(S["phA"], 1)

                @blk.vector
                def _(vector):
                    vector.memset(ones_c[:, :], 1.0)
                    vector.memset(ones_r[:, :], 1.0)
                    vector.memset(epsv[:, :], EPS)
                    vector.wait_ge(S["wl"], 144)
                    vector.tensor_copy(wdtT_sb[:, :], hidf[:, 0:DQ]).then_inc(S["vw"], 1)
                    vector.wait_ge(S["wl"], 160)
                    vector.tensor_copy(wxT_sb[:, :], hidf[:, 0:NB * 160]).then_inc(S["vw"], 1)
                    for wc in range(4):
                        vector.wait_ge(S["wl"], 176 + 16 * wc)
                        vector.tensor_copy(winT3[:, :, wc * 512:(wc + 1) * 512],
                                           hidf3[:, :, 0:512]).then_inc(S["vw"], 1)
                    for t in range(NT_A):
                        j, Ci = t // NCA, t % NCA
                        vector.wait_ge(S["ha"], 16 * (t + 1))
                        vector.tensor_copy(hidb[:, :], hidf[:, :]).then_inc(S["va"], 1)
                        if Ci == 0:
                            vector.memset(halo[:, :], 0.0)
                        vector.wait_ge(S["evA"], 16 * t + 8)
                        vector.tensor_copy(xpre3[:, :, 0:3], halo3[:, :, :])
                        for db in range(NB):
                            vector.tensor_scalar(
                                out=acc3[:, db, :], in0=xpre3[:, db, 0:LCA],
                                scalar1=convw_sb[:, 4 * db:4 * db + 1],
                                scalar2=convb_sb[:, db:db + 1],
                                op0=ALU.mult, op1=ALU.add)
                        for k in range(1, 4):
                            for db in range(NB):
                                vector.scalar_tensor_tensor(
                                    out=acc3[:, db, :], in0=xpre3[:, db, k:k + LCA],
                                    scalar=convw_sb[:, 4 * db + k:4 * db + k + 1],
                                    in1=acc3[:, db, :], op0=ALU.mult, op1=ALU.add)
                        vector.tensor_copy(halo3[:, :, :], xpre3[:, :, LCA:LCA + 3]
                                           ).then_inc(S["cvA"], 1)
                    vector.wait_ge(S["stA"], 64 * NT_A)
                    vector.sem_inc(S["phA"], 1)

                @blk.tensor
                def _(tensor):
                    for t in range(NT_A):
                        tensor.wait_ge(S["va"], t + 1)
                        tensor.wait_ge(S["evA"], 16 * t)
                        for eb in range(16):
                            if eb >= 2:
                                tensor.wait_ge(S["evA"], 16 * t + eb - 1)
                            for kb in range(16):
                                mmi = tensor.matmul(
                                    pa[eb % 2][:, :],
                                    winT3[:, kb, eb * 128:(eb + 1) * 128],
                                    hidb3[:, kb, :],
                                    start=(kb == 0), stop=(kb == 15))
                                if kb == 15:
                                    mmi.then_inc(S["mmA"], 1)
                        tensor.wait_ge(S["siA"], t + 1)
                        tensor.wait_ge(S["evS"], 2 * t)
                        for kb in range(NB):
                            m0 = tensor.matmul(psS0[:, :], wxT3[:, kb, 0:128],
                                               xc3[:, kb, :], start=(kb == 0), stop=(kb == 7))
                            m1 = tensor.matmul(psS1[:, :], wxT3[:, kb, 128:160],
                                               xc3[:, kb, :], start=(kb == 0), stop=(kb == 7))
                            if kb == 7:
                                m0.then_inc(S["mmS"], 1)
                                m1.then_inc(S["mmS"], 1)
                    tensor.wait_ge(S["stA"], 64 * NT_A)
                    tensor.sem_inc(S["phA"], 1)

                @blk.scalar
                def _(scalar):
                    for t in range(NT_A):
                        scalar.wait_ge(S["cvA"], t)
                        scalar.wait_ge(S["stA"], 64 * t)
                        for eb in range(16):
                            scalar.wait_ge(S["mmA"], 16 * t + eb + 1)
                            if eb < 8:
                                scalar.activation(out=xpre3[:, eb, 3:LCA + 3],
                                                  in_=pa[eb % 2][:, :], func=AF.Copy
                                                  ).then_inc(S["evA"], 1)
                            else:
                                scalar.activation(out=zs3[:, eb - 8, :],
                                                  in_=pa[eb % 2][:, :], func=AF.Silu
                                                  ).then_inc(S["evA"], 1)
                        scalar.wait_ge(S["cvA"], t + 1)
                        scalar.activation(out=xc_t[:, :], in_=accv[:, :], func=AF.Silu
                                          ).then_inc(S["siA"], 1)
                        scalar.wait_ge(S["mmS"], 2 * t + 2)
                        scalar.copy(out=sst0[:, :], in_=psS0[:, :]).then_inc(S["evS"], 1)
                        scalar.copy(out=sst1[:, :], in_=psS1[:, :]).then_inc(S["evS"], 1)
                    scalar.wait_ge(S["stA"], 64 * NT_A)
                    scalar.sem_inc(S["phA"], 1)

                @blk.gpsimd
                def _(gpsimd):
                    for t in range(NT_A):
                        j, Ci = t // NCA, t % NCA
                        ls = slice(Ci * LCA, (Ci + 1) * LCA)
                        gpsimd.wait_ge(S["siA"], t + 1)
                        gpsimd.dma_start(out=xcb_v[j, :, :, ls], in_=xc3[:, :, :]
                                         ).then_inc(S["stA"], 16)
                        gpsimd.wait_ge(S["evA"], 16 * (t + 1))
                        gpsimd.dma_start(out=zb_v[j, :, :, ls], in_=zs3[:, :, :]
                                         ).then_inc(S["stA"], 16)
                        gpsimd.wait_ge(S["evS"], 2 * t + 1)
                        gpsimd.dma_start(out=ssm_p[j].ap()[0:128, ls],
                                         in_=sst0[:, :]).then_inc(S["stA"], 16)
                        gpsimd.wait_ge(S["evS"], 2 * t + 2)
                        gpsimd.dma_start(out=ssm_p[j].ap()[128:160, ls],
                                         in_=sst1[:, :]).then_inc(S["stA"], 16)
                        if Ci == NCA - 1:
                            gpsimd.wait_ge(S["stA"], 64 * (t + 1))
                            gpsimd.collective_compute(
                                "AllReduce", ALU.add, replica_groups=GROUPS,
                                ins=[ssm_p[j].ap().opt()], outs=[ssm_r[j].ap().opt()],
                            ).then_inc(S["ccS"], 1)
                    gpsimd.wait_ge(S["stA"], 64 * NT_A)
                    gpsimd.sem_inc(S["phA"], 1)

        with ExitStack() as ctxB:
            woutT_sb = ctxB.enter_context(nc.sbuf_tensor("woutT_sb", [P, NB * DM], BF16))
            wstageB = ctxB.enter_context(nc.sbuf_tensor("wstageB", [P, NB * 512], F32))
            dtr_f = [ctxB.enter_context(nc.sbuf_tensor(f"dtr_f{i}", [P, LCB], F32))
                     for i in range(2)]
            dtr_b = [ctxB.enter_context(nc.sbuf_tensor(f"dtr_b{i}", [P, LCB], BF16))
                     for i in range(2)]
            b_rep = [ctxB.enter_context(nc.sbuf_tensor(f"b_rep{i}", [P, NST * LCB], F32))
                     for i in range(2)]
            c_rep = [ctxB.enter_context(nc.sbuf_tensor(f"c_rep{i}", [P, NST * LCB], F32))
                     for i in range(2)]

            delta = ctxB.enter_context(nc.sbuf_tensor("delta", [P, NB * LCB], F32))
            at = ctxB.enter_context(nc.sbuf_tensor("at", [P, NB * LCB], F32))
            bt = ctxB.enter_context(nc.sbuf_tensor("bt", [P, NB * LCB], F32))
            st_ = ctxB.enter_context(nc.sbuf_tensor("st_", [P, NB * LCB], F32))
            du_b = ctxB.enter_context(nc.sbuf_tensor("du_b", [P, NB * LCB], BF16))
            yac = ctxB.enter_context(nc.sbuf_tensor("yac", [P, NB * LCB], F32))
            xcB = [ctxB.enter_context(nc.sbuf_tensor(f"xcB{i}", [P, NB * LCB], BF16))
                   for i in range(2)]
            zsB = [ctxB.enter_context(nc.sbuf_tensor(f"zsB{i}", [P, NB * LCB], BF16))
                   for i in range(2)]
            yg = ctxB.enter_context(nc.sbuf_tensor("yg", [P, NB * LCB], BF16))
            wo_st = ctxB.enter_context(nc.sbuf_tensor("wo_st", [P, 16 * LCB], F32))
            pd0 = ctxB.enter_context(nc.psum_tensor("pd0", [P, LCB], F32))
            pd1 = ctxB.enter_context(nc.psum_tensor("pd1", [P, LCB], F32))
            po0 = ctxB.enter_context(nc.psum_tensor("po0", [P, LCB], F32))
            po1 = ctxB.enter_context(nc.psum_tensor("po1", [P, LCB], F32))
            pd = [pd0, pd1]
            po = [po0, po1]

            woutT3 = woutT_sb[:, :].rearrange("p (b m) -> p b m", b=NB)
            wstage3 = wstageB[:, :].rearrange("p (b m) -> p b m", b=NB)
            woutT_v = woutT.rearrange("p (b m) -> p b m", b=NB)
            delta3 = delta[:, :].rearrange("p (b l) -> p b l", b=NB)
            at2f = wstageB[:, 0:NB * LCB]
            ytmp = wstageB[:, NB * LCB:2 * NB * LCB]
            atv = [at[:, :], at2f]
            atv3 = [v.rearrange("p (b l) -> p b l", b=NB) for v in atv]
            at3 = atv3[0]
            bt3 = bt[:, :].rearrange("p (b l) -> p b l", b=NB)
            st3 = st_[:, :].rearrange("p (b l) -> p b l", b=NB)
            du3 = du_b[:, :].rearrange("p (b l) -> p b l", b=NB)
            yac3 = yac[:, :].rearrange("p (b l) -> p b l", b=NB)
            xcB3 = [x[:, :].rearrange("p (b l) -> p b l", b=NB) for x in xcB]
            zsB3 = [x[:, :].rearrange("p (b l) -> p b l", b=NB) for x in zsB]
            yg3 = yg[:, :].rearrange("p (b l) -> p b l", b=NB)
            wo3 = wo_st[:, :].rearrange("p (b l) -> p b l", b=16)

            with nc.Block() as blk:
                @blk.sync
                def _(sync):
                    sync.wait_ge(S["phA"], 5)
                    for wc in range(4):
                        if wc > 0:
                            sync.wait_ge(S["vwb"], wc)
                        sync.dma_start(out=wstage3[:, :, 0:512],
                                       in_=woutT_v[:, :, wc * 512:(wc + 1) * 512]
                                       ).then_inc(S["wb"], 16)
                    for t in range(NT_B):
                        j, Ci = t // NCB, t % NCB
                        pt = t % 2
                        ls = slice(Ci * LCB, (Ci + 1) * LCB)
                        if Ci == 0:
                            sync.wait_ge(S["ccS"], j + 1)
                        if t >= 2:
                            sync.wait_ge(S["vg"], t - 1)
                        sync.dma_start(out=xcB3[pt][:, :, :], in_=xcb_v[j, :, :, ls]
                                       ).then_inc(S["hb"], 16)
                        sync.dma_start(out=zsB3[pt][:, :, :], in_=zb_v[j, :, :, ls]
                                       ).then_inc(S["hb"], 16)
                        sync.dma_start(out=dtr_f[pt][:, :],
                                       in_=ssm_r[j].ap()[0:128, ls]
                                       ).then_inc(S["hb"], 16)
                        sync.dma_start(
                            out=b_rep[pt][:, :].rearrange("p (n l) -> p n l", n=NST),
                            in_=ssm_r[j].ap()[128:144, ls
                                              ].unsqueeze(0).broadcast_to([P, NST, LCB])
                        ).then_inc(S["hb"], 16)
                        sync.dma_start(
                            out=c_rep[pt][:, :].rearrange("p (n l) -> p n l", n=NST),
                            in_=ssm_r[j].ap()[144:160, ls
                                              ].unsqueeze(0).broadcast_to([P, NST, LCB])
                        ).then_inc(S["hb"], 16)
                    sync.wait_ge(S["sb_"], 16 * NT_B)
                    sync.sem_inc(S["phB"], 1)

                @blk.vector
                def _(vector):
                    vector.wait_ge(S["phA"], 5)
                    for wc in range(4):
                        vector.wait_ge(S["wb"], 16 * (wc + 1))
                        vector.tensor_copy(woutT3[:, :, wc * 512:(wc + 1) * 512],
                                           wstage3[:, :, 0:512]).then_inc(S["vwb"], 1)
                    for t in range(NT_B):
                        j, Ci = t // NCB, t % NCB
                        pt = t % 2
                        vector.wait_ge(S["hb"], 80 * (t + 1))
                        if Ci == 0:
                            vector.memset(carry[:, :], 0.0)
                        vector.tensor_copy(dtr_b[pt][:, :], dtr_f[pt][:, :]
                                           ).then_inc(S["vb"], 1)
                        vector.wait_ge(S["spb"], t + 1)
                        vector.tensor_add(delta[:, :], delta[:, :], at[:, :]
                                          ).then_inc(S["vn"], 1)
                        vector.tensor_tensor(out=du_b[:, :], in0=delta[:, :],
                                             in1=xcB[pt][:, :], op=ALU.mult)
                        for n in range(NST):
                            ap3 = atv3[n % 2]
                            vector.wait_ge(S["sn"], NST * t + n + 1)
                            b_bc = b_rep[pt][:, n * LCB:(n + 1) * LCB].unsqueeze(1
                                        ).broadcast_to([P, NB, LCB])
                            vector.tensor_tensor(out=bt3, in0=du3, in1=b_bc, op=ALU.mult)
                            cu = carry[:, n * NB:(n + 1) * NB].unsqueeze(2)
                            for db in range(NB):
                                sci = vector.tensor_tensor_scan(
                                    out=st3[:, db, :], data0=ap3[:, db, :],
                                    data1=bt3[:, db, :],
                                    initial=carry[:, n * NB + db:n * NB + db + 1],
                                    op0=ALU.mult, op1=ALU.add)
                                if db == NB - 1:
                                    sci.then_inc(S["vsf"], 1)
                            vector.wait_ge(S["vsf"], NST * t + n + 1)
                            vector.tensor_copy(cu, st3[:, :, LCB - 1:LCB])
                            c_bc = c_rep[pt][:, n * LCB:(n + 1) * LCB].unsqueeze(1
                                        ).broadcast_to([P, NB, LCB])
                            if n == 0:
                                vector.tensor_tensor(out=yac3, in0=st3, in1=c_bc,
                                                     op=ALU.mult)
                            else:
                                vector.tensor_tensor(
                                    out=ytmp[:, :].rearrange("p (b l) -> p b l", b=NB),
                                    in0=st3, in1=c_bc, op=ALU.mult)
                                vector.tensor_add(yac[:, :], yac[:, :], ytmp[:, :])
                        for db in range(NB):
                            vector.scalar_tensor_tensor(
                                out=yac3[:, db, :], in0=xcB3[pt][:, db, :],
                                scalar=d_sb[:, db:db + 1], in1=yac3[:, db, :],
                                op0=ALU.mult, op1=ALU.add)
                        vector.wait_ge(S["mo"], 16 * t)
                        vector.tensor_tensor(out=yg[:, :], in0=yac[:, :],
                                             in1=zsB[pt][:, :],
                                             op=ALU.mult).then_inc(S["vg"], 1)
                    vector.wait_ge(S["sb_"], 16 * NT_B)
                    vector.sem_inc(S["phB"], 1)

                @blk.tensor
                def _(tensor):
                    tensor.wait_ge(S["phA"], 5)
                    for t in range(NT_B):
                        tensor.wait_ge(S["vb"], t + 1)
                        tensor.wait_ge(S["eb_"], 8 * t)
                        for db in range(NB):
                            if db >= 2:
                                tensor.wait_ge(S["eb_"], 8 * t + db - 1)
                            tensor.matmul(pd[db % 2][:, :],
                                          wdtT_sb[:, db * 128:(db + 1) * 128],
                                          dtr_b[t % 2][:, :], start=True, stop=True
                                          ).then_inc(S["mb_"], 1)
                        tensor.wait_ge(S["vg"], t + 1)
                        tensor.wait_ge(S["eo"], 16 * t)
                        if t == 0:
                            tensor.wait_ge(S["vwb"], 4)
                        for mb in range(16):
                            if mb >= 2:
                                tensor.wait_ge(S["eo"], 16 * t + mb - 1)
                            for kb in range(NB):
                                mmi = tensor.matmul(po[mb % 2][:, :],
                                                    woutT3[:, kb, mb * 128:(mb + 1) * 128],
                                                    yg3[:, kb, :],
                                                    start=(kb == 0), stop=(kb == 7))
                                if kb == 7:
                                    mmi.then_inc(S["mo"], 1)
                    tensor.wait_ge(S["sb_"], 16 * NT_B)
                    tensor.sem_inc(S["phB"], 1)

                @blk.scalar
                def _(scalar):
                    scalar.wait_ge(S["phA"], 5)
                    for t in range(NT_B):
                        for db in range(NB):
                            scalar.wait_ge(S["mb_"], 8 * t + db + 1)
                            scalar.activation(out=at3[:, db, :], in_=pd[db % 2][:, :],
                                              func=AF.Abs, bias=dtb_sb[:, db:db + 1])
                            scalar.activation(out=delta3[:, db, :], in_=pd[db % 2][:, :],
                                              func=AF.Relu, bias=dtb_sb[:, db:db + 1]
                                              ).then_inc(S["eb_"], 1)
                        scalar.activation(out=at[:, :], in_=at[:, :], func=AF.Exp,
                                          scale=-1.0)
                        scalar.activation(out=at[:, :], in_=at[:, :], func=AF.Ln,
                                          bias=1.0).then_inc(S["spb"], 1)
                        scalar.wait_ge(S["vn"], t + 1)
                        for n in range(NST):
                            if n >= 2:
                                scalar.wait_ge(S["vsf"], NST * t + n - 1)
                            for db in range(NB):
                                ai = scalar.activation(
                                    out=atv3[n % 2][:, db, :], in_=delta3[:, db, :],
                                    func=AF.Exp,
                                    scale=a_sb[:, db * NST + n:db * NST + n + 1])
                                if db == NB - 1:
                                    ai.then_inc(S["sn"], 1)
                        scalar.wait_ge(S["sb_"], 16 * t)
                        for mb in range(16):
                            scalar.wait_ge(S["mo"], 16 * t + mb + 1)
                            scalar.copy(out=wo3[:, mb, :], in_=po[mb % 2][:, :]
                                        ).then_inc(S["eo"], 1)
                    scalar.wait_ge(S["sb_"], 16 * NT_B)
                    scalar.sem_inc(S["phB"], 1)

                @blk.gpsimd
                def _(gpsimd):
                    gpsimd.wait_ge(S["phA"], 5)
                    for t in range(NT_B):
                        j, Ci = t // NCB, t % NCB
                        qc, oc = Ci // 2, (Ci % 2) * LCB
                        gpsimd.wait_ge(S["eo"], 16 * (t + 1))
                        gpsimd.dma_start(out=outp_v[j][qc][:, :, oc:oc + LCB],
                                         in_=wo3[:, :, :]).then_inc(S["sb_"], 16)
                    gpsimd.wait_ge(S["sb_"], 16 * NT_B)
                    gpsimd.sem_inc(S["phB"], 1)
                    for j in range(B2):
                        gpsimd.collective_compute(
                            "ReduceScatter", ALU.add, replica_groups=GROUPS,
                            ins=[outp[j].ap().opt()], outs=[outr[j].ap().opt()],
                        ).then_inc(S["ccO"], 1)

        with ExitStack() as ctxC:
            x1 = ctxC.enter_context(nc.sbuf_tensor("x1", [P, 16 * TQ], F32))
            bufA = ctxC.enter_context(nc.sbuf_tensor("bufA", [P, 16 * TQ], F32))
            bufBF = ctxC.enter_context(nc.sbuf_tensor("bufBF", [P, 24576], BF16))
            h_all = ctxC.enter_context(nc.sbuf_tensor("h_all", [P, 44 * TQ], BF16))
            pre2 = ctxC.enter_context(nc.sbuf_tensor("pre2", [P, 16 * TQ], F32))
            sq1 = ctxC.enter_context(nc.sbuf_tensor("sq1", [P, TQ], F32))
            rows = ctxC.enter_context(nc.sbuf_tensor("rows", [1, 2 * TQ], F32))
            mu_bc = ctxC.enter_context(nc.sbuf_tensor("mu_bc", [P, TQ], F32))
            rs_bc = ctxC.enter_context(nc.sbuf_tensor("rs_bc", [P, TQ], F32))
            gsil = ctxC.enter_context(nc.sbuf_tensor("gsil", [P, TQ], BF16))
            uscb = ctxC.enter_context(nc.sbuf_tensor("uscb", [P, TQ], BF16))
            ffn_t = sq1
            psG = ctxC.enter_context(nc.psum_tensor("psG", [P, TQ], F32))
            psU = ctxC.enter_context(nc.psum_tensor("psU", [P, TQ], F32))
            psG2 = ctxC.enter_context(nc.psum_tensor("psG2", [P, TQ], F32))
            psU2 = ctxC.enter_context(nc.psum_tensor("psU2", [P, TQ], F32))
            psL0 = ctxC.enter_context(nc.psum_tensor("psL0", [1, TQ], F32))
            psL1 = ctxC.enter_context(nc.psum_tensor("psL1", [1, TQ], F32))

            x1_3 = x1[:, :].rearrange("p (b l) -> p b l", b=16)
            bufA3 = bufA[:, :].rearrange("p (b l) -> p b l", b=16)
            pre2_3 = pre2[:, :].rearrange("p (b l) -> p b l", b=16)
            dst11 = bufA[:, 0:11 * 512].rearrange("p (g m) -> p g m", g=11)
            gbf = bufBF[:, 0:8192].rearrange("p (b i) -> p b i", b=16)
            ubf = bufBF[:, 8192:16384].rearrange("p (b i) -> p b i", b=16)
            xlnb = bufBF[:, 16384:24576].rearrange("p (b l) -> p b l", b=16)
            dbf = bufBF[:, 0:22528].rearrange("p (b m) -> p b m", b=44)
            h3 = h_all[:, :].rearrange("p (b l) -> p b l", b=44)
            out3 = bufA3
            gateT_v = gateT.rearrange("p (b i) -> p b i", b=16)
            upT_v = upT.rearrange("p (b i) -> p b i", b=16)
            downT_v = downT.rearrange("p (b m) -> p b m", b=44)
            hidq_v = hidq.rearrange("j p (b l) -> j p b l", b=16)
            outy_v = outy.rearrange("j p (b l) -> j p b l", b=16)
            r_m, r_s = rows[:, 0:TQ], rows[:, TQ:2 * TQ]
            r_mu, r_v = mu_bc[0:1, :], rs_bc[0:1, :]

            SQ_J = 32
            TQL_J = 32
            TL_J = 4
            EL_J = 10
            VL_J = 4
            HC_J = 32 + NIC * 128 + 16 * 16
            VGC_J = 8 * NIC
            TIC_J = 8 * NIC
            EGC_J = 8 * NIC
            VH_J = 4 * NIC
            VDC_J = 16
            TOC_J = 16
            EDC_J = 16
            VP_J = 16

            def ln_scalar(scalar, xt3, b_sq, b_tql, b_tL, b_eL, b_vL):
                for kb in range(16):
                    scalar.wait_ge(S["tql"], b_tql + kb)
                    scalar.activation(out=sq1[:, :], in_=xt3[:, kb, :],
                                      func=AF.Square).then_inc(S["sqc"], 1)
                scalar.wait_ge(S["tql"], b_tql + 16)
                scalar.copy(out=r_m, in_=psL0[:, :]).then_inc(S["eL"], 1)
                scalar.copy(out=r_s, in_=psL1[:, :]).then_inc(S["eL"], 1)
                scalar.wait_ge(S["vL"], b_vL + 1)
                scalar.activation(out=r_v, in_=r_v, func=AF.Sqrt,
                                  bias=epsv[0:1, 0:1]).then_inc(S["eL"], 1)
                scalar.wait_ge(S["tL"], b_tL + 2)
                scalar.copy(out=mu_bc[:, :], in_=psG[:, :]).then_inc(S["eL"], 1)
                scalar.copy(out=rs_bc[:, :], in_=psU[:, :]).then_inc(S["eL"], 1)

            def ln_tensor(tensor, xt3, b_sq, b_tql, b_tL, b_eL, b_vL):
                for kb in range(16):
                    tensor.matmul(psL0[:, :], ones_c[:, 0:1], xt3[:, kb, :],
                                  start=(kb == 0), stop=(kb == 15))
                for kb in range(16):
                    tensor.wait_ge(S["sqc"], b_sq + kb + 1)
                    tensor.matmul(psL1[:, :], ones_c[:, 0:1], sq1[:, :],
                                  start=(kb == 0), stop=(kb == 15)
                                  ).then_inc(S["tql"], 1)
                tensor.wait_ge(S["vL"], b_vL + 2)
                tensor.matmul(psG[:, :], ones_r[0:1, :], r_mu, start=True, stop=True
                              ).then_inc(S["tL"], 1)
                tensor.matmul(psU[:, :], ones_r[0:1, :], r_v, start=True, stop=True
                              ).then_inc(S["tL"], 1)

            def ln_vector(vector, xt3, outt3, b_sq, b_tql, b_tL, b_eL, b_vL,
                          inc=None):
                vector.wait_ge(S["eL"], b_eL + 2)
                vector.tensor_scalar_mul(r_mu, r_m, 1.0 / DM)
                vector.tensor_tensor(out=r_v, in0=r_mu, in1=r_mu, op=ALU.mult)
                vector.scalar_tensor_tensor(out=r_v, in0=r_s, scalar=1.0 / DM,
                                            in1=r_v, op0=ALU.mult,
                                            op1=ALU.subtract).then_inc(S["vL"], 1)
                vector.wait_ge(S["eL"], b_eL + 3)
                vector.reciprocal(r_v, r_v).then_inc(S["vL"], 1)
                vector.wait_ge(S["eL"], b_eL + 5)
                mu16 = mu_bc[:, :].unsqueeze(1).broadcast_to([P, 16, TQ])
                rs16 = rs_bc[:, :].unsqueeze(1).broadcast_to([P, 16, TQ])
                vector.tensor_tensor(out=outt3, in0=xt3, in1=mu16, op=ALU.subtract)
                ti = vector.tensor_tensor(out=outt3, in0=outt3, in1=rs16, op=ALU.mult)
                if inc is not None:
                    ti.then_inc(*inc)

            with nc.Block() as blk:
                @blk.sync
                def _(sync):
                    sync.wait_ge(S["phB"], 5)
                    for j in range(B2):
                        hcb = j * HC_J
                        sync.wait_ge(S["ccO"], j + 1)
                        if j > 0:
                            sync.wait_ge(S["vout"], j)
                            sync.wait_ge(S["soc"], 16 * j)
                        sync.dma_start(out=x1_3[:, :, :], in_=hidq_v[j, :, :, :]
                                       ).then_inc(S["hc"], 16)
                        sync.dma_start(out=bufA3[:, :, :], in_=outr_v[j][0]
                                       ).then_inc(S["hc"], 16)
                        for ic in range(NIC):
                            if ic == 0:
                                sync.wait_ge(S["vx1"], j + 1)
                            for gi in range(8):
                                wsrc = gateT_v if gi < 4 else upT_v
                                sub = gi % 4
                                kbs = slice(4 * sub, 4 * sub + 4)
                                slot = gi % 4
                                g_all = 8 * ic + gi
                                if g_all >= 4:
                                    sync.wait_ge(S["vgc"], j * VGC_J + g_all - 3)
                                sync.dma_start(
                                    out=bufA3[:, 4 * slot:4 * slot + 4, :],
                                    in_=wsrc[:, kbs, ic * 512:(ic + 1) * 512]
                                    ).then_inc(S["hc"], 16)
                        for mc in range(4):
                            for sub in range(4):
                                if mc == 0 and sub == 0:
                                    sync.wait_ge(S["vgc"], (j + 1) * VGC_J)
                                sync.wait_ge(S["vdc"], j * VDC_J + 4 * mc + sub)
                                sync.dma_start(
                                    out=dst11,
                                    in_=downT_v[:, sub * 11:(sub + 1) * 11,
                                                mc * 512:(mc + 1) * 512]
                                    ).then_inc(S["hc"], 16)

                @blk.vector
                def _(vector):
                    vector.wait_ge(S["phB"], 5)
                    for j in range(B2):
                        hcb = j * HC_J
                        vector.wait_ge(S["hc"], hcb + 32)
                        vector.tensor_add(x1[:, :], x1[:, :], bufA[:, :]
                                          ).then_inc(S["vx1"], 1)
                        ln_vector(vector, x1_3, x1_3, j * SQ_J, j * TQL_J, j * TL_J,
                                  j * EL_J, j * VL_J)
                        if j > 0:
                            vector.wait_ge(S["toc"], j * TOC_J)
                        vector.tensor_copy(xlnb, x1_3).then_inc(S["vln"], 1)
                        for ic in range(NIC):
                            for gi in range(8):
                                g_all = 8 * ic + gi
                                if g_all >= 8:
                                    vector.wait_ge(S["tic"], j * TIC_J + 8 * ic)
                                vector.wait_ge(S["hc"], hcb + 32 + 16 * (g_all + 1))
                                dst = gbf if gi < 4 else ubf
                                sub = gi % 4
                                slot = gi % 4
                                vector.tensor_copy(dst[:, 4 * sub:4 * sub + 4, :],
                                                   bufA3[:, 4 * slot:4 * slot + 4, :]
                                                   ).then_inc(S["vgc"], 1)
                            for ib in range(4):
                                vector.wait_ge(S["egc"],
                                               j * EGC_J + 8 * ic + 2 * (ib + 1))
                                vector.tensor_tensor(
                                    out=h3[:, ic * 4 + ib, :], in0=gsil[:, :],
                                    in1=uscb[:, :], op=ALU.mult).then_inc(S["vh"], 1)
                        for mc in range(4):
                            for sub in range(4):
                                if mc == 0 and sub == 0:
                                    vector.wait_ge(S["tic"], (j + 1) * TIC_J)
                                else:
                                    vector.wait_ge(S["toc"], j * TOC_J + 4 * mc)
                                vector.wait_ge(S["hc"],
                                               hcb + 32 + 128 * NIC + 16 * (4 * mc + sub + 1))
                                vector.tensor_copy(dbf[:, sub * 11:(sub + 1) * 11, :],
                                                   dst11).then_inc(S["vdc"], 1)
                            for mb in range(4):
                                idx = 4 * mc + mb
                                vector.wait_ge(S["edc"], j * EDC_J + idx + 1)
                                vector.tensor_add(pre2_3[:, idx, :], ffn_t[:, :],
                                                  x1_3[:, idx, :]).then_inc(S["vp"], 1)
                        ln_vector(vector, pre2_3, out3, j * SQ_J + 16, j * TQL_J + 16,
                                  j * TL_J + 2, j * EL_J + 5, j * VL_J + 2,
                                  inc=(S["vout"], 1))

                @blk.tensor
                def _(tensor):
                    tensor.wait_ge(S["phB"], 5)
                    for j in range(B2):
                        tensor.wait_ge(S["vx1"], j + 1)
                        ln_tensor(tensor, x1_3, j * SQ_J, j * TQL_J, j * TL_J,
                                  j * EL_J, j * VL_J)
                        tensor.wait_ge(S["vln"], j + 1)
                        for ic in range(NIC):
                            for ib in range(4):
                                tensor.wait_ge(S["egc"],
                                               max(0, j * EGC_J + 8 * ic + 2 * ib - 2))
                                pg = psG if ib % 2 == 0 else psG2
                                pu = psU if ib % 2 == 0 else psU2
                                for kb in range(16):
                                    if ib == 0 and kb % 4 == 0:
                                        tensor.wait_ge(S["vgc"],
                                                       j * VGC_J + 8 * ic + kb // 4 + 1)
                                    mmi = tensor.matmul(pg[:, :],
                                                        gbf[:, kb, ib * 128:(ib + 1) * 128],
                                                        xlnb[:, kb, :],
                                                        start=(kb == 0), stop=(kb == 15))
                                    if kb == 15:
                                        mmi.then_inc(S["tic"], 1)
                                for kb in range(16):
                                    if ib == 0 and kb % 4 == 0:
                                        tensor.wait_ge(S["vgc"],
                                                       j * VGC_J + 8 * ic + 4 + kb // 4 + 1)
                                    mmi = tensor.matmul(pu[:, :],
                                                        ubf[:, kb, ib * 128:(ib + 1) * 128],
                                                        xlnb[:, kb, :],
                                                        start=(kb == 0), stop=(kb == 15))
                                    if kb == 15:
                                        mmi.then_inc(S["tic"], 1)
                        tensor.wait_ge(S["vh"], (j + 1) * VH_J)
                        tensor.wait_ge(S["egc"], (j + 1) * EGC_J)
                        for mc in range(4):
                            for mb in range(4):
                                idx = 4 * mc + mb
                                if idx >= 2:
                                    tensor.wait_ge(S["edc"], j * EDC_J + idx - 1)
                                pot = psG if idx % 2 == 0 else psU
                                for ib in range(44):
                                    if mb == 0 and ib % 11 == 0:
                                        tensor.wait_ge(S["vdc"],
                                                       j * VDC_J + 4 * mc + ib // 11 + 1)
                                    mmi = tensor.matmul(pot[:, :],
                                                        dbf[:, ib, mb * 128:(mb + 1) * 128],
                                                        h3[:, ib, :],
                                                        start=(ib == 0), stop=(ib == 43))
                                    if ib == 43:
                                        mmi.then_inc(S["toc"], 1)
                        tensor.wait_ge(S["vp"], (j + 1) * VP_J)
                        ln_tensor(tensor, pre2_3, j * SQ_J + 16, j * TQL_J + 16,
                                  j * TL_J + 2, j * EL_J + 5, j * VL_J + 2)

                @blk.scalar
                def _(scalar):
                    scalar.wait_ge(S["phB"], 5)
                    for j in range(B2):
                        scalar.wait_ge(S["vx1"], j + 1)
                        ln_scalar(scalar, x1_3, j * SQ_J, j * TQL_J, j * TL_J,
                                  j * EL_J, j * VL_J)
                        for ic in range(NIC):
                            for ib in range(4):
                                pg = psG if ib % 2 == 0 else psG2
                                pu = psU if ib % 2 == 0 else psU2
                                scalar.wait_ge(S["vh"], j * VH_J + 4 * ic + ib)
                                scalar.wait_ge(S["tic"], j * TIC_J + 8 * ic + 2 * ib + 1)
                                scalar.activation(out=gsil[:, :], in_=pg[:, :],
                                                  func=AF.Silu,
                                                  scale=gs_sb[:, ic * 4 + ib:
                                                              ic * 4 + ib + 1])
                                scalar.wait_ge(S["tic"], j * TIC_J + 8 * ic + 2 * ib + 2)
                                scalar.activation(out=uscb[:, :], in_=pu[:, :],
                                                  func=AF.Copy,
                                                  scale=us_sb[:, ic * 4 + ib:
                                                              ic * 4 + ib + 1]
                                                  ).then_inc(S["egc"], 2)
                        for mc in range(4):
                            for mb in range(4):
                                idx = 4 * mc + mb
                                scalar.wait_ge(S["toc"], j * TOC_J + idx + 1)
                                if idx > 0:
                                    scalar.wait_ge(S["vp"], j * VP_J + idx)
                                pot = psG if idx % 2 == 0 else psU
                                scalar.activation(out=ffn_t[:, :], in_=pot[:, :],
                                                  func=AF.Copy,
                                                  scale=ds_sb[:, idx:idx + 1]
                                                  ).then_inc(S["edc"], 1)
                        scalar.wait_ge(S["vp"], (j + 1) * VP_J)
                        ln_scalar(scalar, pre2_3, j * SQ_J + 16, j * TQL_J + 16,
                                  j * TL_J + 2, j * EL_J + 5, j * VL_J + 2)

                @blk.gpsimd
                def _(gpsimd):
                    gpsimd.wait_ge(S["phB"], 5)
                    for j in range(B2):
                        gpsimd.wait_ge(S["vout"], j + 1)
                        gpsimd.dma_start(out=outy_v[j, :, :, :], in_=out3[:, :, :]
                                         ).then_inc(S["soc"], 16)
                    gpsimd.wait_ge(S["soc"], 16 * B2)

    return nc


def kernel(**inp):
    f32 = np.float32

    def pshape(a, nblk):
        x = np.asarray(a)
        rest = x.shape[1:]
        return np.ascontiguousarray(
            x.reshape(nblk, 128, *rest).transpose(1, 0, *range(2, 2 + len(rest)))
            .reshape(128, -1))

    hid_all = np.asarray(inp["L0_attn_hidden_states"])
    w_in = np.asarray(inp["L0_attn_W_in"])
    conv_w = np.asarray(inp["L0_attn_conv_w"])
    conv_b = np.asarray(inp["L0_attn_conv_b"])
    w_x = np.asarray(inp["L0_attn_W_x"])
    w_dt = np.asarray(inp["L0_attn_W_dt"])
    dt_bias = np.asarray(inp["L0_attn_dt_bias"])
    a_m = np.asarray(inp["L0_attn_A"])
    d_v = np.asarray(inp["L0_attn_D"])
    w_out = np.asarray(inp["L0_attn_W_out"])
    gate_w = np.asarray(inp["L0_ffn_gate_w"])
    up_w = np.asarray(inp["L0_ffn_up_w"])
    down_w = np.asarray(inp["L0_ffn_down_w"])
    gate_s = np.asarray(inp["L0_ffn_gate_s"])
    up_s = np.asarray(inp["L0_ffn_up_s"])
    down_s = np.asarray(inp["L0_ffn_down_s"])

    gateT_h = pshape(np.ascontiguousarray(gate_w.T), 16).astype(f32)
    upT_h = pshape(np.ascontiguousarray(up_w.T), 16).astype(f32)
    downT_h = pshape(np.ascontiguousarray(down_w.T), 44).astype(f32)
    gs_h = pshape(gate_s.reshape(-1, 1), 44).astype(f32)
    us_h = pshape(up_s.reshape(-1, 1), 44).astype(f32)
    ds_h = pshape(down_s.reshape(-1, 1), 16).astype(f32)

    in_maps = []
    for c in range(8):
        p, q = c // 4, c % 4
        dq = slice(1024 * q, 1024 * (q + 1))
        hid_fm = np.ascontiguousarray(hid_all[2 * p:2 * p + 2].transpose(0, 2, 1))
        hid_h = np.stack([pshape(hid_fm[jj], 16) for jj in range(2)])
        hidq_h = np.stack([pshape(np.ascontiguousarray(
            hid_fm[jj][:, 512 * q:512 * (q + 1)]), 16) for jj in range(2)])
        winT_q = np.ascontiguousarray(
            np.concatenate([w_in[dq], w_in[4096 + 1024 * q:4096 + 1024 * (q + 1)]], 0).T)
        im = {
            "hid": hid_h.astype(f32),
            "hidq": hidq_h.astype(f32),
            "winT": pshape(winT_q, 16).astype(f32),
            "wxT": pshape(np.ascontiguousarray(w_x[:, dq].T), 8).astype(f32),
            "wdtT": np.ascontiguousarray(w_dt[dq].T).astype(f32),
            "woutT": pshape(np.ascontiguousarray(w_out[:, dq].T), 8).astype(f32),
            "convw": pshape(conv_w[dq], 8).astype(f32),
            "convb": pshape(conv_b[dq].reshape(-1, 1), 8).astype(f32),
            "dtb": pshape(dt_bias[dq].reshape(-1, 1), 8).astype(f32),
            "a_mat": pshape(a_m[dq], 8).astype(f32),
            "d_vec": pshape(d_v[dq].reshape(-1, 1), 8).astype(f32),
            "gateT": gateT_h, "upT": upT_h, "downT": downT_h,
            "gs": gs_h, "us": us_h, "ds": ds_h,
        }
        in_maps.append(im)

    nc = build_nc()
    res = run_bass_kernel_spmd(nc, in_maps, core_ids=list(range(8)))

    out = np.empty((4, 2048, 2048), f32)
    for c in range(8):
        p, q = c // 4, c % 4
        oc = np.asarray(res.results[c]["outy"])
        for jj in range(2):
            om = oc[jj].reshape(128, 16, 512).transpose(1, 0, 2).reshape(2048, 512)
            out[2 * p + jj, 512 * q:512 * (q + 1), :] = om.T
    return out


# version 24
# speedup vs baseline: 1.0191x; 1.0029x over previous
import numpy as np
import concourse.bass as bass
import concourse.mybir as mybir
from concourse.bass_utils import run_bass_kernel_spmd
from contextlib import ExitStack

F32 = mybir.dt.float32
BF16 = mybir.dt.bfloat16
AF = mybir.ActivationFunctionType
ALU = mybir.AluOpType

P = 128
B2 = 2
LL = 2048
DM = 2048
NB = 8
NST = 16
LCA = 512
NCA = LL // LCA
LCB = 256
NCB = LL // LCB
TQ = 512
NIC = 11
EPS = 1e-5
DQ = 1024


def build_nc():
    nc = bass.Bass("TRN2", target_bir_lowering=False)

    hid = nc.dram_tensor("hid", [B2, P, 16 * LL], F32, kind="ExternalInput").ap()
    hidq = nc.dram_tensor("hidq", [B2, P, 16 * TQ], F32, kind="ExternalInput").ap()
    winT = nc.dram_tensor("winT", [P, 16 * DM], F32, kind="ExternalInput").ap()
    wxT = nc.dram_tensor("wxT", [P, NB * 160], F32, kind="ExternalInput").ap()
    wdtT = nc.dram_tensor("wdtT", [P, DQ], F32, kind="ExternalInput").ap()
    woutT = nc.dram_tensor("woutT", [P, NB * DM], F32, kind="ExternalInput").ap()
    convw = nc.dram_tensor("convw", [P, NB * 4], F32, kind="ExternalInput").ap()
    convb = nc.dram_tensor("convb", [P, NB], F32, kind="ExternalInput").ap()
    dtb = nc.dram_tensor("dtb", [P, NB], F32, kind="ExternalInput").ap()
    a_mat = nc.dram_tensor("a_mat", [P, NB * NST], F32, kind="ExternalInput").ap()
    d_vec = nc.dram_tensor("d_vec", [P, NB], F32, kind="ExternalInput").ap()
    gateT = nc.dram_tensor("gateT", [P, 16 * 5632], F32, kind="ExternalInput").ap()
    upT = nc.dram_tensor("upT", [P, 16 * 5632], F32, kind="ExternalInput").ap()
    downT = nc.dram_tensor("downT", [P, 44 * DM], F32, kind="ExternalInput").ap()
    gs = nc.dram_tensor("gs", [P, 44], F32, kind="ExternalInput").ap()
    us = nc.dram_tensor("us", [P, 44], F32, kind="ExternalInput").ap()
    ds = nc.dram_tensor("ds", [P, 16], F32, kind="ExternalInput").ap()

    outy = nc.dram_tensor("outy", [B2, P, 16 * TQ], F32, kind="ExternalOutput").ap()

    xc_buf = nc.dram_tensor("xc_buf", [B2 * P, NB * LL], BF16)
    z_buf = nc.dram_tensor("z_buf", [B2 * P, NB * LL], BF16)
    ssm_p = [nc.dram_tensor(f"ssm_p{j}", [160, LL], F32) for j in range(B2)]
    ssm_r = [nc.dram_tensor(f"ssm_r{j}", [160, LL], F32) for j in range(B2)]
    outp = [nc.dram_tensor(f"outp{j}", [4, DM * TQ], F32) for j in range(B2)]
    outr = [nc.dram_tensor(f"outr{j}", [1, DM * TQ], F32) for j in range(B2)]

    GROUPS = [[0, 1, 2, 3], [4, 5, 6, 7]]
    NT_A = B2 * NCA
    NT_B = B2 * NCB

    with ExitStack() as ctx0:
        a_sb = ctx0.enter_context(nc.sbuf_tensor("a_sb", [P, NB * NST], F32))
        convw_sb = ctx0.enter_context(nc.sbuf_tensor("convw_sb", [P, NB * 4], F32))
        convb_sb = ctx0.enter_context(nc.sbuf_tensor("convb_sb", [P, NB], F32))
        dtb_sb = ctx0.enter_context(nc.sbuf_tensor("dtb_sb", [P, NB], F32))
        d_sb = ctx0.enter_context(nc.sbuf_tensor("d_sb", [P, NB], F32))
        gs_sb = ctx0.enter_context(nc.sbuf_tensor("gs_sb", [P, 44], F32))
        us_sb = ctx0.enter_context(nc.sbuf_tensor("us_sb", [P, 44], F32))
        ds_sb = ctx0.enter_context(nc.sbuf_tensor("ds_sb", [P, 16], F32))
        wxT_sb = ctx0.enter_context(nc.sbuf_tensor("wxT_sb", [P, NB * 160], BF16))
        wdtT_sb = ctx0.enter_context(nc.sbuf_tensor("wdtT_sb", [P, DQ], BF16))
        ones_c = ctx0.enter_context(nc.sbuf_tensor("ones_c", [P, 1], F32))
        ones_r = ctx0.enter_context(nc.sbuf_tensor("ones_r", [1, P], F32))
        carry = ctx0.enter_context(nc.sbuf_tensor("carry", [P, NST * NB], F32))
        halo = ctx0.enter_context(nc.sbuf_tensor("halo", [P, NB * 3], BF16))
        tmp8 = ctx0.enter_context(nc.sbuf_tensor("tmp8", [P, NB], F32))
        epsv = ctx0.enter_context(nc.sbuf_tensor("epsv", [P, 1], F32))

        S = {}
        for nm in ["wl", "vw", "ha", "va", "mmA", "mmS", "evA", "evS", "cvA",
                   "siA", "stA", "ccS", "phA",
                   "wb", "vwb", "hb", "vb", "mb_", "eb_", "spb", "vn", "sn",
                   "vg", "mo", "eo", "sb_", "ccO", "phB",
                   "hc", "vx1", "sqc", "tql", "tL", "eL", "vL", "vln",
                   "vgc", "tic", "egc", "vh", "vdc", "toc", "edc", "vp",
                   "vout", "soc", "vsf", "vym", "pya", "hd"]:
            S[nm] = ctx0.enter_context(nc.semaphore(nm))

        a3 = a_sb[:, :].rearrange("p (b n) -> p b n", b=NB)
        hid_v = hid.rearrange("j p (b l) -> j p b l", b=16)
        xcb_v = xc_buf.ap().rearrange("(j p) (b l) -> j p b l", j=B2, b=NB)
        zb_v = z_buf.ap().rearrange("(j p) (b l) -> j p b l", j=B2, b=NB)
        outp_v = [o.ap().rearrange("q (mb p l) -> q p mb l", mb=16, p=P) for o in outp]
        outr_v = [o.ap().rearrange("o (mb p l) -> o p mb l", mb=16, p=P) for o in outr]

        with ExitStack() as ctxA:
            winT_sb = ctxA.enter_context(nc.sbuf_tensor("winT_sb", [P, 16 * DM], BF16))
            hidf = ctxA.enter_context(nc.sbuf_tensor("hidf", [P, 16 * LCA], F32))
            hidb = ctxA.enter_context(nc.sbuf_tensor("hidb", [P, 16 * LCA], BF16))
            xpre = ctxA.enter_context(nc.sbuf_tensor("xpre", [P, NB * (LCA + 3)], BF16))
            xc_t = ctxA.enter_context(nc.sbuf_tensor("xc_t", [P, NB * LCA], BF16))
            zs_t = ctxA.enter_context(nc.sbuf_tensor("zs_t", [P, NB * LCA], BF16))
            accv = ctxA.enter_context(nc.sbuf_tensor("accv", [P, NB * LCA], F32))
            sst0 = ctxA.enter_context(nc.sbuf_tensor("sst0", [P, LCA], F32))
            sst1 = ctxA.enter_context(nc.sbuf_tensor("sst1", [32, LCA], F32))
            pa0 = ctxA.enter_context(nc.psum_tensor("pa0", [P, LCA], F32))
            pa1 = ctxA.enter_context(nc.psum_tensor("pa1", [P, LCA], F32))
            psS0 = ctxA.enter_context(nc.psum_tensor("psS0", [P, LCA], F32))
            psS1 = ctxA.enter_context(nc.psum_tensor("psS1", [32, LCA], F32))
            pa = [pa0, pa1]

            xpre3 = xpre[:, :].rearrange("p (b l) -> p b l", b=NB)
            hidf3 = hidf[:, :].rearrange("p (b l) -> p b l", b=16)
            hidb3 = hidb[:, :].rearrange("p (b l) -> p b l", b=16)
            xc3 = xc_t[:, :].rearrange("p (b l) -> p b l", b=NB)
            zs3 = zs_t[:, :].rearrange("p (b l) -> p b l", b=NB)
            acc3 = accv[:, :].rearrange("p (b l) -> p b l", b=NB)
            winT3 = winT_sb[:, :].rearrange("p (b e) -> p b e", b=16)
            wxT3 = wxT_sb[:, :].rearrange("p (b e) -> p b e", b=NB)
            halo3 = halo[:, :].rearrange("p (b k) -> p b k", b=NB)
            winT_v = winT.rearrange("p (b e) -> p b e", b=16)

            with nc.Block() as blk:
                @blk.sync
                def _(sync):
                    for dst, src in [(a_sb, a_mat), (convw_sb, convw), (convb_sb, convb),
                                     (dtb_sb, dtb), (d_sb, d_vec), (gs_sb, gs),
                                     (us_sb, us), (ds_sb, ds)]:
                        sync.dma_start(out=dst[:, :], in_=src).then_inc(S["wl"], 16)
                    sync.dma_start(out=hidf[:, 0:DQ], in_=wdtT).then_inc(S["wl"], 16)
                    sync.wait_ge(S["vw"], 1)
                    sync.dma_start(out=hidf[:, 0:NB * 160], in_=wxT).then_inc(S["wl"], 16)
                    for wc in range(4):
                        sync.wait_ge(S["vw"], 2 + wc)
                        sync.dma_start(out=hidf3[:, :, 0:512],
                                       in_=winT_v[:, :, wc * 512:(wc + 1) * 512]
                                       ).then_inc(S["wl"], 16)
                    for t in range(NT_A):
                        j, Ci = t // NCA, t % NCA
                        if t > 0:
                            sync.wait_ge(S["mmA"], 16 * t)
                        else:
                            sync.wait_ge(S["vw"], 6)
                        sync.dma_start(out=hidf3[:, :, :],
                                       in_=hid_v[j, :, :, Ci * LCA:(Ci + 1) * LCA]
                                       ).then_inc(S["ha"], 16)
                    sync.wait_ge(S["stA"], 64 * NT_A)
                    sync.sem_inc(S["phA"], 1)

                @blk.vector
                def _(vector):
                    vector.memset(ones_c[:, :], 1.0)
                    vector.memset(ones_r[:, :], 1.0)
                    vector.memset(epsv[:, :], EPS)
                    vector.wait_ge(S["wl"], 144)
                    vector.tensor_copy(wdtT_sb[:, :], hidf[:, 0:DQ]).then_inc(S["vw"], 1)
                    vector.wait_ge(S["wl"], 160)
                    vector.tensor_copy(wxT_sb[:, :], hidf[:, 0:NB * 160]).then_inc(S["vw"], 1)
                    for wc in range(4):
                        vector.wait_ge(S["wl"], 176 + 16 * wc)
                        vector.tensor_copy(winT3[:, :, wc * 512:(wc + 1) * 512],
                                           hidf3[:, :, 0:512]).then_inc(S["vw"], 1)
                    for t in range(NT_A):
                        j, Ci = t // NCA, t % NCA
                        vector.wait_ge(S["ha"], 16 * (t + 1))
                        vector.tensor_copy(hidb[:, :], hidf[:, :]).then_inc(S["va"], 1)
                        if Ci == 0:
                            vector.memset(halo[:, :], 0.0)
                        vector.wait_ge(S["evA"], 16 * t + 8)
                        vector.tensor_copy(xpre3[:, :, 0:3], halo3[:, :, :])
                        for db in range(NB):
                            vector.tensor_scalar(
                                out=acc3[:, db, :], in0=xpre3[:, db, 0:LCA],
                                scalar1=convw_sb[:, 4 * db:4 * db + 1],
                                scalar2=convb_sb[:, db:db + 1],
                                op0=ALU.mult, op1=ALU.add)
                        for k in range(1, 4):
                            for db in range(NB):
                                vector.scalar_tensor_tensor(
                                    out=acc3[:, db, :], in0=xpre3[:, db, k:k + LCA],
                                    scalar=convw_sb[:, 4 * db + k:4 * db + k + 1],
                                    in1=acc3[:, db, :], op0=ALU.mult, op1=ALU.add)
                        vector.tensor_copy(halo3[:, :, :], xpre3[:, :, LCA:LCA + 3]
                                           ).then_inc(S["cvA"], 1)
                    vector.wait_ge(S["stA"], 64 * NT_A)
                    vector.sem_inc(S["phA"], 1)

                @blk.tensor
                def _(tensor):
                    for t in range(NT_A):
                        tensor.wait_ge(S["va"], t + 1)
                        tensor.wait_ge(S["evA"], 16 * t)
                        for eb in range(16):
                            if eb >= 2:
                                tensor.wait_ge(S["evA"], 16 * t + eb - 1)
                            for kb in range(16):
                                mmi = tensor.matmul(
                                    pa[eb % 2][:, :],
                                    winT3[:, kb, eb * 128:(eb + 1) * 128],
                                    hidb3[:, kb, :],
                                    start=(kb == 0), stop=(kb == 15))
                                if kb == 15:
                                    mmi.then_inc(S["mmA"], 1)
                        tensor.wait_ge(S["siA"], t + 1)
                        tensor.wait_ge(S["evS"], 2 * t)
                        for kb in range(NB):
                            m0 = tensor.matmul(psS0[:, :], wxT3[:, kb, 0:128],
                                               xc3[:, kb, :], start=(kb == 0), stop=(kb == 7))
                            m1 = tensor.matmul(psS1[:, :], wxT3[:, kb, 128:160],
                                               xc3[:, kb, :], start=(kb == 0), stop=(kb == 7))
                            if kb == 7:
                                m0.then_inc(S["mmS"], 1)
                                m1.then_inc(S["mmS"], 1)
                    tensor.wait_ge(S["stA"], 64 * NT_A)
                    tensor.sem_inc(S["phA"], 1)

                @blk.scalar
                def _(scalar):
                    for t in range(NT_A):
                        scalar.wait_ge(S["cvA"], t)
                        scalar.wait_ge(S["stA"], 64 * t)
                        for eb in range(16):
                            scalar.wait_ge(S["mmA"], 16 * t + eb + 1)
                            if eb < 8:
                                scalar.activation(out=xpre3[:, eb, 3:LCA + 3],
                                                  in_=pa[eb % 2][:, :], func=AF.Copy
                                                  ).then_inc(S["evA"], 1)
                            else:
                                scalar.activation(out=zs3[:, eb - 8, :],
                                                  in_=pa[eb % 2][:, :], func=AF.Silu
                                                  ).then_inc(S["evA"], 1)
                        scalar.wait_ge(S["cvA"], t + 1)
                        scalar.activation(out=xc_t[:, :], in_=accv[:, :], func=AF.Silu
                                          ).then_inc(S["siA"], 1)
                        scalar.wait_ge(S["mmS"], 2 * t + 2)
                        scalar.copy(out=sst0[:, :], in_=psS0[:, :]).then_inc(S["evS"], 1)
                        scalar.copy(out=sst1[:, :], in_=psS1[:, :]).then_inc(S["evS"], 1)
                    scalar.wait_ge(S["stA"], 64 * NT_A)
                    scalar.sem_inc(S["phA"], 1)

                @blk.gpsimd
                def _(gpsimd):
                    for t in range(NT_A):
                        j, Ci = t // NCA, t % NCA
                        ls = slice(Ci * LCA, (Ci + 1) * LCA)
                        gpsimd.wait_ge(S["siA"], t + 1)
                        gpsimd.dma_start(out=xcb_v[j, :, :, ls], in_=xc3[:, :, :]
                                         ).then_inc(S["stA"], 16)
                        gpsimd.wait_ge(S["evA"], 16 * (t + 1))
                        gpsimd.dma_start(out=zb_v[j, :, :, ls], in_=zs3[:, :, :]
                                         ).then_inc(S["stA"], 16)
                        gpsimd.wait_ge(S["evS"], 2 * t + 1)
                        gpsimd.dma_start(out=ssm_p[j].ap()[0:128, ls],
                                         in_=sst0[:, :]).then_inc(S["stA"], 16)
                        gpsimd.wait_ge(S["evS"], 2 * t + 2)
                        gpsimd.dma_start(out=ssm_p[j].ap()[128:160, ls],
                                         in_=sst1[:, :]).then_inc(S["stA"], 16)
                        if Ci == NCA - 1:
                            gpsimd.wait_ge(S["stA"], 64 * (t + 1))
                            gpsimd.collective_compute(
                                "AllReduce", ALU.add, replica_groups=GROUPS,
                                ins=[ssm_p[j].ap().opt()], outs=[ssm_r[j].ap().opt()],
                            ).then_inc(S["ccS"], 1)
                    gpsimd.wait_ge(S["stA"], 64 * NT_A)
                    gpsimd.sem_inc(S["phA"], 1)

        with ExitStack() as ctxB:
            woutT_sb = ctxB.enter_context(nc.sbuf_tensor("woutT_sb", [P, NB * DM], BF16))
            wstageB = ctxB.enter_context(nc.sbuf_tensor("wstageB", [P, NB * 512], F32))
            dtr_f = [ctxB.enter_context(nc.sbuf_tensor(f"dtr_f{i}", [P, LCB], F32))
                     for i in range(2)]
            dtr_b = [ctxB.enter_context(nc.sbuf_tensor(f"dtr_b{i}", [P, LCB], BF16))
                     for i in range(2)]
            b_rep = [ctxB.enter_context(nc.sbuf_tensor(f"b_rep{i}", [P, NST * LCB], F32))
                     for i in range(2)]
            c_rep = [ctxB.enter_context(nc.sbuf_tensor(f"c_rep{i}", [P, NST * LCB], F32))
                     for i in range(2)]

            delta = ctxB.enter_context(nc.sbuf_tensor("delta", [P, NB * LCB], F32))
            at = ctxB.enter_context(nc.sbuf_tensor("at", [P, NB * LCB], F32))
            bt = ctxB.enter_context(nc.sbuf_tensor("bt", [P, NB * LCB], F32))
            st_ = ctxB.enter_context(nc.sbuf_tensor("st_", [P, NB * LCB], F32))
            du_b = ctxB.enter_context(nc.sbuf_tensor("du_b", [P, NB * LCB], BF16))
            yac = ctxB.enter_context(nc.sbuf_tensor("yac", [P, NB * LCB], F32))
            xcB = [ctxB.enter_context(nc.sbuf_tensor(f"xcB{i}", [P, NB * LCB], BF16))
                   for i in range(2)]
            zsB = [ctxB.enter_context(nc.sbuf_tensor(f"zsB{i}", [P, NB * LCB], BF16))
                   for i in range(2)]
            yg = ctxB.enter_context(nc.sbuf_tensor("yg", [P, NB * LCB], BF16))
            wo_st = ctxB.enter_context(nc.sbuf_tensor("wo_st", [P, 16 * LCB], F32))
            pd0 = ctxB.enter_context(nc.psum_tensor("pd0", [P, LCB], F32))
            pd1 = ctxB.enter_context(nc.psum_tensor("pd1", [P, LCB], F32))
            po0 = ctxB.enter_context(nc.psum_tensor("po0", [P, LCB], F32))
            po1 = ctxB.enter_context(nc.psum_tensor("po1", [P, LCB], F32))
            pd = [pd0, pd1]
            po = [po0, po1]

            woutT3 = woutT_sb[:, :].rearrange("p (b m) -> p b m", b=NB)
            wstage3 = wstageB[:, :].rearrange("p (b m) -> p b m", b=NB)
            woutT_v = woutT.rearrange("p (b m) -> p b m", b=NB)
            delta3 = delta[:, :].rearrange("p (b l) -> p b l", b=NB)
            at2f = wstageB[:, 0:NB * LCB]
            ytmp = wstageB[:, NB * LCB:2 * NB * LCB]
            atv = [at[:, :], at2f]
            atv3 = [v.rearrange("p (b l) -> p b l", b=NB) for v in atv]
            at3 = atv3[0]
            bt3 = bt[:, :].rearrange("p (b l) -> p b l", b=NB)
            st3 = st_[:, :].rearrange("p (b l) -> p b l", b=NB)
            du3 = du_b[:, :].rearrange("p (b l) -> p b l", b=NB)
            yac3 = yac[:, :].rearrange("p (b l) -> p b l", b=NB)
            xcB3 = [x[:, :].rearrange("p (b l) -> p b l", b=NB) for x in xcB]
            zsB3 = [x[:, :].rearrange("p (b l) -> p b l", b=NB) for x in zsB]
            yg3 = yg[:, :].rearrange("p (b l) -> p b l", b=NB)
            wo3 = wo_st[:, :].rearrange("p (b l) -> p b l", b=16)

            with nc.Block() as blk:
                @blk.sync
                def _(sync):
                    sync.wait_ge(S["phA"], 5)
                    for wc in range(4):
                        if wc > 0:
                            sync.wait_ge(S["vwb"], wc)
                        sync.dma_start(out=wstage3[:, :, 0:512],
                                       in_=woutT_v[:, :, wc * 512:(wc + 1) * 512]
                                       ).then_inc(S["wb"], 16)
                    for t in range(NT_B):
                        j, Ci = t // NCB, t % NCB
                        pt = t % 2
                        ls = slice(Ci * LCB, (Ci + 1) * LCB)
                        if Ci == 0:
                            sync.wait_ge(S["ccS"], j + 1)
                        if t >= 2:
                            sync.wait_ge(S["vg"], t - 1)
                        sync.dma_start(out=xcB3[pt][:, :, :], in_=xcb_v[j, :, :, ls]
                                       ).then_inc(S["hb"], 16)
                        sync.dma_start(out=zsB3[pt][:, :, :], in_=zb_v[j, :, :, ls]
                                       ).then_inc(S["hb"], 16)
                        sync.dma_start(out=dtr_f[pt][:, :],
                                       in_=ssm_r[j].ap()[0:128, ls]
                                       ).then_inc(S["hb"], 16)
                        sync.dma_start(
                            out=b_rep[pt][:, :].rearrange("p (n l) -> p n l", n=NST),
                            in_=ssm_r[j].ap()[128:144, ls
                                              ].unsqueeze(0).broadcast_to([P, NST, LCB])
                        ).then_inc(S["hb"], 16)
                        sync.dma_start(
                            out=c_rep[pt][:, :].rearrange("p (n l) -> p n l", n=NST),
                            in_=ssm_r[j].ap()[144:160, ls
                                              ].unsqueeze(0).broadcast_to([P, NST, LCB])
                        ).then_inc(S["hb"], 16)
                    sync.wait_ge(S["sb_"], 16 * NT_B)
                    sync.sem_inc(S["phB"], 1)

                @blk.vector
                def _(vector):
                    vector.wait_ge(S["phA"], 5)
                    for wc in range(4):
                        vector.wait_ge(S["wb"], 16 * (wc + 1))
                        vector.tensor_copy(woutT3[:, :, wc * 512:(wc + 1) * 512],
                                           wstage3[:, :, 0:512]).then_inc(S["vwb"], 1)
                    for t in range(NT_B):
                        j, Ci = t // NCB, t % NCB
                        pt = t % 2
                        vector.wait_ge(S["hb"], 80 * (t + 1))
                        if Ci == 0:
                            vector.memset(carry[:, :], 0.0)
                        vector.tensor_copy(dtr_b[pt][:, :], dtr_f[pt][:, :]
                                           ).then_inc(S["vb"], 1)
                        vector.wait_ge(S["spb"], t + 1)
                        vector.tensor_add(delta[:, :], delta[:, :], at[:, :]
                                          ).then_inc(S["vn"], 1)
                        vector.tensor_tensor(out=du_b[:, :], in0=delta[:, :],
                                             in1=xcB[pt][:, :], op=ALU.mult)
                        for n in range(NST):
                            ap3 = atv3[n % 2]
                            vector.wait_ge(S["sn"], NST * t + n + 1)
                            b_bc = b_rep[pt][:, n * LCB:(n + 1) * LCB].unsqueeze(1
                                        ).broadcast_to([P, NB, LCB])
                            vector.tensor_tensor(out=bt3, in0=du3, in1=b_bc, op=ALU.mult)
                            cu = carry[:, n * NB:(n + 1) * NB].unsqueeze(2)
                            for db in range(NB):
                                sci = vector.tensor_tensor_scan(
                                    out=st3[:, db, :], data0=ap3[:, db, :],
                                    data1=bt3[:, db, :],
                                    initial=carry[:, n * NB + db:n * NB + db + 1],
                                    op0=ALU.mult, op1=ALU.add)
                                if db == NB - 1:
                                    sci.then_inc(S["vsf"], 1)
                            vector.wait_ge(S["vsf"], NST * t + n + 1)
                            vector.tensor_copy(cu, st3[:, :, LCB - 1:LCB])
                            c_bc = c_rep[pt][:, n * LCB:(n + 1) * LCB].unsqueeze(1
                                        ).broadcast_to([P, NB, LCB])
                            if n == 0:
                                vector.tensor_tensor(out=yac3, in0=st3, in1=c_bc,
                                                     op=ALU.mult)
                            else:
                                vector.tensor_tensor(
                                    out=ytmp[:, :].rearrange("p (b l) -> p b l", b=NB),
                                    in0=st3, in1=c_bc, op=ALU.mult)
                                vector.tensor_add(yac[:, :], yac[:, :], ytmp[:, :])
                        for db in range(NB):
                            vector.scalar_tensor_tensor(
                                out=yac3[:, db, :], in0=xcB3[pt][:, db, :],
                                scalar=d_sb[:, db:db + 1], in1=yac3[:, db, :],
                                op0=ALU.mult, op1=ALU.add)
                        vector.wait_ge(S["mo"], 16 * t)
                        vector.tensor_tensor(out=yg[:, :], in0=yac[:, :],
                                             in1=zsB[pt][:, :],
                                             op=ALU.mult).then_inc(S["vg"], 1)
                    vector.wait_ge(S["sb_"], 16 * NT_B)
                    vector.sem_inc(S["phB"], 1)

                @blk.tensor
                def _(tensor):
                    tensor.wait_ge(S["phA"], 5)
                    for t in range(NT_B):
                        tensor.wait_ge(S["vb"], t + 1)
                        tensor.wait_ge(S["eb_"], 8 * t)
                        for db in range(NB):
                            if db >= 2:
                                tensor.wait_ge(S["eb_"], 8 * t + db - 1)
                            tensor.matmul(pd[db % 2][:, :],
                                          wdtT_sb[:, db * 128:(db + 1) * 128],
                                          dtr_b[t % 2][:, :], start=True, stop=True
                                          ).then_inc(S["mb_"], 1)
                        tensor.wait_ge(S["vg"], t + 1)
                        tensor.wait_ge(S["eo"], 16 * t)
                        if t == 0:
                            tensor.wait_ge(S["vwb"], 4)
                        for mb in range(16):
                            if mb >= 2:
                                tensor.wait_ge(S["eo"], 16 * t + mb - 1)
                            for kb in range(NB):
                                mmi = tensor.matmul(po[mb % 2][:, :],
                                                    woutT3[:, kb, mb * 128:(mb + 1) * 128],
                                                    yg3[:, kb, :],
                                                    start=(kb == 0), stop=(kb == 7))
                                if kb == 7:
                                    mmi.then_inc(S["mo"], 1)
                    tensor.wait_ge(S["sb_"], 16 * NT_B)
                    tensor.sem_inc(S["phB"], 1)

                @blk.scalar
                def _(scalar):
                    scalar.wait_ge(S["phA"], 5)
                    for t in range(NT_B):
                        for db in range(NB):
                            scalar.wait_ge(S["mb_"], 8 * t + db + 1)
                            scalar.activation(out=at3[:, db, :], in_=pd[db % 2][:, :],
                                              func=AF.Abs, bias=dtb_sb[:, db:db + 1])
                            scalar.activation(out=delta3[:, db, :], in_=pd[db % 2][:, :],
                                              func=AF.Relu, bias=dtb_sb[:, db:db + 1]
                                              ).then_inc(S["eb_"], 1)
                        scalar.activation(out=at[:, :], in_=at[:, :], func=AF.Exp,
                                          scale=-1.0)
                        scalar.activation(out=at[:, :], in_=at[:, :], func=AF.Ln,
                                          bias=1.0).then_inc(S["spb"], 1)
                        scalar.wait_ge(S["vn"], t + 1)
                        for n in range(NST):
                            if n >= 2:
                                scalar.wait_ge(S["vsf"], NST * t + n - 1)
                            for db in range(NB):
                                ai = scalar.activation(
                                    out=atv3[n % 2][:, db, :], in_=delta3[:, db, :],
                                    func=AF.Exp,
                                    scale=a_sb[:, db * NST + n:db * NST + n + 1])
                                if db == NB - 1:
                                    ai.then_inc(S["sn"], 1)
                        scalar.wait_ge(S["sb_"], 16 * t)
                        for mb in range(16):
                            scalar.wait_ge(S["mo"], 16 * t + mb + 1)
                            scalar.copy(out=wo3[:, mb, :], in_=po[mb % 2][:, :]
                                        ).then_inc(S["eo"], 1)
                    scalar.wait_ge(S["sb_"], 16 * NT_B)
                    scalar.sem_inc(S["phB"], 1)

                @blk.gpsimd
                def _(gpsimd):
                    gpsimd.wait_ge(S["phA"], 5)
                    for t in range(NT_B):
                        j, Ci = t // NCB, t % NCB
                        qc, oc = Ci // 2, (Ci % 2) * LCB
                        gpsimd.wait_ge(S["eo"], 16 * (t + 1))
                        gpsimd.dma_start(out=outp_v[j][qc][:, :, oc:oc + LCB],
                                         in_=wo3[:, :, :]).then_inc(S["sb_"], 16)
                    gpsimd.wait_ge(S["sb_"], 16 * NT_B)
                    gpsimd.sem_inc(S["phB"], 1)
                    for j in range(B2):
                        gpsimd.collective_compute(
                            "ReduceScatter", ALU.add, replica_groups=GROUPS,
                            ins=[outp[j].ap().opt()], outs=[outr[j].ap().opt()],
                        ).then_inc(S["ccO"], 1)

        with ExitStack() as ctxC:
            x1 = ctxC.enter_context(nc.sbuf_tensor("x1", [P, 16 * TQ], F32))
            bufA = ctxC.enter_context(nc.sbuf_tensor("bufA", [P, 16 * TQ], F32))
            bufBF = ctxC.enter_context(nc.sbuf_tensor("bufBF", [P, 24576], BF16))
            h_all = ctxC.enter_context(nc.sbuf_tensor("h_all", [P, 44 * TQ], BF16))
            pre2 = ctxC.enter_context(nc.sbuf_tensor("pre2", [P, 16 * TQ], F32))
            sq1 = ctxC.enter_context(nc.sbuf_tensor("sq1", [P, TQ], F32))
            rows = ctxC.enter_context(nc.sbuf_tensor("rows", [1, 2 * TQ], F32))
            mu_bc = ctxC.enter_context(nc.sbuf_tensor("mu_bc", [P, TQ], F32))
            rs_bc = ctxC.enter_context(nc.sbuf_tensor("rs_bc", [P, TQ], F32))
            gsil = ctxC.enter_context(nc.sbuf_tensor("gsil", [P, TQ], BF16))
            uscb = ctxC.enter_context(nc.sbuf_tensor("uscb", [P, TQ], BF16))
            ffn_t = sq1
            psG = ctxC.enter_context(nc.psum_tensor("psG", [P, TQ], F32))
            psU = ctxC.enter_context(nc.psum_tensor("psU", [P, TQ], F32))
            psG2 = ctxC.enter_context(nc.psum_tensor("psG2", [P, TQ], F32))
            psU2 = ctxC.enter_context(nc.psum_tensor("psU2", [P, TQ], F32))
            psL0 = ctxC.enter_context(nc.psum_tensor("psL0", [1, TQ], F32))
            psL1 = ctxC.enter_context(nc.psum_tensor("psL1", [1, TQ], F32))

            x1_3 = x1[:, :].rearrange("p (b l) -> p b l", b=16)
            bufA3 = bufA[:, :].rearrange("p (b l) -> p b l", b=16)
            pre2_3 = pre2[:, :].rearrange("p (b l) -> p b l", b=16)
            dst11 = bufA[:, 0:11 * 512].rearrange("p (g m) -> p g m", g=11)
            gbf = bufBF[:, 0:8192].rearrange("p (b i) -> p b i", b=16)
            ubf = bufBF[:, 8192:16384].rearrange("p (b i) -> p b i", b=16)
            xlnb = bufBF[:, 16384:24576].rearrange("p (b l) -> p b l", b=16)
            dbf = bufBF[:, 0:22528].rearrange("p (b m) -> p b m", b=44)
            h3 = h_all[:, :].rearrange("p (b l) -> p b l", b=44)
            out3 = bufA3
            gateT_v = gateT.rearrange("p (b i) -> p b i", b=16)
            upT_v = upT.rearrange("p (b i) -> p b i", b=16)
            downT_v = downT.rearrange("p (b m) -> p b m", b=44)
            hidq_v = hidq.rearrange("j p (b l) -> j p b l", b=16)
            outy_v = outy.rearrange("j p (b l) -> j p b l", b=16)
            r_m, r_s = rows[:, 0:TQ], rows[:, TQ:2 * TQ]
            r_mu, r_v = mu_bc[0:1, :], rs_bc[0:1, :]

            SQ_J = 32
            TQL_J = 32
            TL_J = 4
            EL_J = 10
            VL_J = 4
            HC_J = 32 + NIC * 128
            VGC_J = 8 * NIC
            TIC_J = 8 * NIC
            EGC_J = 8 * NIC
            VH_J = 4 * NIC
            VDC_J = 16
            TOC_J = 16
            EDC_J = 16
            VP_J = 16

            def ln_scalar(scalar, xt3, b_sq, b_tql, b_tL, b_eL, b_vL):
                for kb in range(16):
                    scalar.wait_ge(S["tql"], b_tql + kb)
                    scalar.activation(out=sq1[:, :], in_=xt3[:, kb, :],
                                      func=AF.Square).then_inc(S["sqc"], 1)
                scalar.wait_ge(S["tql"], b_tql + 16)
                scalar.copy(out=r_m, in_=psL0[:, :]).then_inc(S["eL"], 1)
                scalar.copy(out=r_s, in_=psL1[:, :]).then_inc(S["eL"], 1)
                scalar.wait_ge(S["vL"], b_vL + 1)
                scalar.activation(out=r_v, in_=r_v, func=AF.Sqrt,
                                  bias=epsv[0:1, 0:1]).then_inc(S["eL"], 1)
                scalar.wait_ge(S["tL"], b_tL + 2)
                scalar.copy(out=mu_bc[:, :], in_=psG[:, :]).then_inc(S["eL"], 1)
                scalar.copy(out=rs_bc[:, :], in_=psU[:, :]).then_inc(S["eL"], 1)

            def ln_tensor(tensor, xt3, b_sq, b_tql, b_tL, b_eL, b_vL):
                for kb in range(16):
                    tensor.matmul(psL0[:, :], ones_c[:, 0:1], xt3[:, kb, :],
                                  start=(kb == 0), stop=(kb == 15))
                for kb in range(16):
                    tensor.wait_ge(S["sqc"], b_sq + kb + 1)
                    tensor.matmul(psL1[:, :], ones_c[:, 0:1], sq1[:, :],
                                  start=(kb == 0), stop=(kb == 15)
                                  ).then_inc(S["tql"], 1)
                tensor.wait_ge(S["vL"], b_vL + 2)
                tensor.matmul(psG[:, :], ones_r[0:1, :], r_mu, start=True, stop=True
                              ).then_inc(S["tL"], 1)
                tensor.matmul(psU[:, :], ones_r[0:1, :], r_v, start=True, stop=True
                              ).then_inc(S["tL"], 1)

            def ln_vector(vector, xt3, outt3, b_sq, b_tql, b_tL, b_eL, b_vL,
                          inc=None):
                vector.wait_ge(S["eL"], b_eL + 2)
                vector.tensor_scalar_mul(r_mu, r_m, 1.0 / DM)
                vector.tensor_tensor(out=r_v, in0=r_mu, in1=r_mu, op=ALU.mult)
                vector.scalar_tensor_tensor(out=r_v, in0=r_s, scalar=1.0 / DM,
                                            in1=r_v, op0=ALU.mult,
                                            op1=ALU.subtract).then_inc(S["vL"], 1)
                vector.wait_ge(S["eL"], b_eL + 3)
                vector.reciprocal(r_v, r_v).then_inc(S["vL"], 1)
                vector.wait_ge(S["eL"], b_eL + 5)
                mu16 = mu_bc[:, :].unsqueeze(1).broadcast_to([P, 16, TQ])
                rs16 = rs_bc[:, :].unsqueeze(1).broadcast_to([P, 16, TQ])
                vector.tensor_tensor(out=outt3, in0=xt3, in1=mu16, op=ALU.subtract)
                ti = vector.tensor_tensor(out=outt3, in0=outt3, in1=rs16, op=ALU.mult)
                if inc is not None:
                    ti.then_inc(*inc)

            with nc.Block() as blk:
                @blk.sync
                def _(sync):
                    sync.wait_ge(S["phB"], 5)
                    for j in range(B2):
                        hcb = j * HC_J
                        sync.wait_ge(S["ccO"], j + 1)
                        if j > 0:
                            sync.wait_ge(S["vout"], j)
                            sync.wait_ge(S["soc"], 16 * j)
                        sync.dma_start(out=x1_3[:, :, :], in_=hidq_v[j, :, :, :]
                                       ).then_inc(S["hc"], 16)
                        sync.dma_start(out=bufA3[:, :, :], in_=outr_v[j][0]
                                       ).then_inc(S["hc"], 16)
                        for ic in range(NIC):
                            if ic == 0:
                                sync.wait_ge(S["vx1"], j + 1)
                            for gi in range(8):
                                wsrc = gateT_v if gi < 4 else upT_v
                                sub = gi % 4
                                kbs = slice(4 * sub, 4 * sub + 4)
                                slot = gi % 4
                                g_all = 8 * ic + gi
                                if g_all >= 4:
                                    sync.wait_ge(S["vgc"], j * VGC_J + g_all - 3)
                                sync.dma_start(
                                    out=bufA3[:, 4 * slot:4 * slot + 4, :],
                                    in_=wsrc[:, kbs, ic * 512:(ic + 1) * 512]
                                    ).then_inc(S["hc"], 16)


                @blk.vector
                def _(vector):
                    vector.wait_ge(S["phB"], 5)
                    for j in range(B2):
                        hcb = j * HC_J
                        vector.wait_ge(S["hc"], hcb + 32)
                        vector.tensor_add(x1[:, :], x1[:, :], bufA[:, :]
                                          ).then_inc(S["vx1"], 1)
                        ln_vector(vector, x1_3, x1_3, j * SQ_J, j * TQL_J, j * TL_J,
                                  j * EL_J, j * VL_J)
                        if j > 0:
                            vector.wait_ge(S["toc"], j * TOC_J)
                        vector.tensor_copy(xlnb, x1_3).then_inc(S["vln"], 1)
                        for ic in range(NIC):
                            for gi in range(8):
                                g_all = 8 * ic + gi
                                if g_all >= 8:
                                    vector.wait_ge(S["tic"], j * TIC_J + 8 * ic)
                                vector.wait_ge(S["hc"], hcb + 32 + 16 * (g_all + 1))
                                dst = gbf if gi < 4 else ubf
                                sub = gi % 4
                                slot = gi % 4
                                vector.tensor_copy(dst[:, 4 * sub:4 * sub + 4, :],
                                                   bufA3[:, 4 * slot:4 * slot + 4, :]
                                                   ).then_inc(S["vgc"], 1)
                            for ib in range(4):
                                vector.wait_ge(S["egc"],
                                               j * EGC_J + 8 * ic + 2 * (ib + 1))
                                vector.tensor_tensor(
                                    out=h3[:, ic * 4 + ib, :], in0=gsil[:, :],
                                    in1=uscb[:, :], op=ALU.mult).then_inc(S["vh"], 1)
                        for mc in range(4):
                            for sub in range(4):
                                if mc == 0 and sub == 0:
                                    vector.wait_ge(S["tic"], (j + 1) * TIC_J)
                                else:
                                    vector.wait_ge(S["toc"], j * TOC_J + 4 * mc)
                                vector.wait_ge(S["hd"],
                                               j * 256 + 16 * (4 * mc + sub + 1))
                                vector.tensor_copy(dbf[:, sub * 11:(sub + 1) * 11, :],
                                                   dst11).then_inc(S["vdc"], 1)
                            for mb in range(4):
                                idx = 4 * mc + mb
                                vector.wait_ge(S["edc"], j * EDC_J + idx + 1)
                                vector.tensor_add(pre2_3[:, idx, :], ffn_t[:, :],
                                                  x1_3[:, idx, :]).then_inc(S["vp"], 1)
                        ln_vector(vector, pre2_3, out3, j * SQ_J + 16, j * TQL_J + 16,
                                  j * TL_J + 2, j * EL_J + 5, j * VL_J + 2,
                                  inc=(S["vout"], 1))

                @blk.tensor
                def _(tensor):
                    tensor.wait_ge(S["phB"], 5)
                    for j in range(B2):
                        tensor.wait_ge(S["vx1"], j + 1)
                        ln_tensor(tensor, x1_3, j * SQ_J, j * TQL_J, j * TL_J,
                                  j * EL_J, j * VL_J)
                        tensor.wait_ge(S["vln"], j + 1)
                        for ic in range(NIC):
                            for ib in range(4):
                                tensor.wait_ge(S["egc"],
                                               max(0, j * EGC_J + 8 * ic + 2 * ib - 2))
                                pg = psG if ib % 2 == 0 else psG2
                                pu = psU if ib % 2 == 0 else psU2
                                for kb in range(16):
                                    if ib == 0 and kb % 4 == 0:
                                        tensor.wait_ge(S["vgc"],
                                                       j * VGC_J + 8 * ic + kb // 4 + 1)
                                    mmi = tensor.matmul(pg[:, :],
                                                        gbf[:, kb, ib * 128:(ib + 1) * 128],
                                                        xlnb[:, kb, :],
                                                        start=(kb == 0), stop=(kb == 15))
                                    if kb == 15:
                                        mmi.then_inc(S["tic"], 1)
                                for kb in range(16):
                                    if ib == 0 and kb % 4 == 0:
                                        tensor.wait_ge(S["vgc"],
                                                       j * VGC_J + 8 * ic + 4 + kb // 4 + 1)
                                    mmi = tensor.matmul(pu[:, :],
                                                        ubf[:, kb, ib * 128:(ib + 1) * 128],
                                                        xlnb[:, kb, :],
                                                        start=(kb == 0), stop=(kb == 15))
                                    if kb == 15:
                                        mmi.then_inc(S["tic"], 1)
                        tensor.wait_ge(S["vh"], (j + 1) * VH_J)
                        tensor.wait_ge(S["egc"], (j + 1) * EGC_J)
                        for mc in range(4):
                            for mb in range(4):
                                idx = 4 * mc + mb
                                if idx >= 2:
                                    tensor.wait_ge(S["edc"], j * EDC_J + idx - 1)
                                pot = psG if idx % 2 == 0 else psU
                                for ib in range(44):
                                    if mb == 0 and ib % 11 == 0:
                                        tensor.wait_ge(S["vdc"],
                                                       j * VDC_J + 4 * mc + ib // 11 + 1)
                                    mmi = tensor.matmul(pot[:, :],
                                                        dbf[:, ib, mb * 128:(mb + 1) * 128],
                                                        h3[:, ib, :],
                                                        start=(ib == 0), stop=(ib == 43))
                                    if ib == 43:
                                        mmi.then_inc(S["toc"], 1)
                        tensor.wait_ge(S["vp"], (j + 1) * VP_J)
                        ln_tensor(tensor, pre2_3, j * SQ_J + 16, j * TQL_J + 16,
                                  j * TL_J + 2, j * EL_J + 5, j * VL_J + 2)

                @blk.scalar
                def _(scalar):
                    scalar.wait_ge(S["phB"], 5)
                    for j in range(B2):
                        scalar.wait_ge(S["vx1"], j + 1)
                        ln_scalar(scalar, x1_3, j * SQ_J, j * TQL_J, j * TL_J,
                                  j * EL_J, j * VL_J)
                        for ic in range(NIC):
                            for ib in range(4):
                                pg = psG if ib % 2 == 0 else psG2
                                pu = psU if ib % 2 == 0 else psU2
                                scalar.wait_ge(S["vh"], j * VH_J + 4 * ic + ib)
                                scalar.wait_ge(S["tic"], j * TIC_J + 8 * ic + 2 * ib + 1)
                                scalar.activation(out=gsil[:, :], in_=pg[:, :],
                                                  func=AF.Silu,
                                                  scale=gs_sb[:, ic * 4 + ib:
                                                              ic * 4 + ib + 1])
                                scalar.wait_ge(S["tic"], j * TIC_J + 8 * ic + 2 * ib + 2)
                                scalar.activation(out=uscb[:, :], in_=pu[:, :],
                                                  func=AF.Copy,
                                                  scale=us_sb[:, ic * 4 + ib:
                                                              ic * 4 + ib + 1]
                                                  ).then_inc(S["egc"], 2)
                        for mc in range(4):
                            for mb in range(4):
                                idx = 4 * mc + mb
                                scalar.wait_ge(S["toc"], j * TOC_J + idx + 1)
                                if idx > 0:
                                    scalar.wait_ge(S["vp"], j * VP_J + idx)
                                pot = psG if idx % 2 == 0 else psU
                                scalar.activation(out=ffn_t[:, :], in_=pot[:, :],
                                                  func=AF.Copy,
                                                  scale=ds_sb[:, idx:idx + 1]
                                                  ).then_inc(S["edc"], 1)
                        scalar.wait_ge(S["vp"], (j + 1) * VP_J)
                        ln_scalar(scalar, pre2_3, j * SQ_J + 16, j * TQL_J + 16,
                                  j * TL_J + 2, j * EL_J + 5, j * VL_J + 2)

                @blk.gpsimd
                def _(gpsimd):
                    gpsimd.wait_ge(S["phB"], 5)
                    for j in range(B2):
                        for mc in range(4):
                            for sub in range(4):
                                if mc == 0 and sub == 0:
                                    gpsimd.wait_ge(S["vgc"], (j + 1) * VGC_J)
                                gpsimd.wait_ge(S["vdc"], j * VDC_J + 4 * mc + sub)
                                gpsimd.dma_start(
                                    out=dst11,
                                    in_=downT_v[:, sub * 11:(sub + 1) * 11,
                                                mc * 512:(mc + 1) * 512]
                                    ).then_inc(S["hd"], 16)
                        gpsimd.wait_ge(S["vout"], j + 1)
                        gpsimd.dma_start(out=outy_v[j, :, :, :], in_=out3[:, :, :]
                                         ).then_inc(S["soc"], 16)
                    gpsimd.wait_ge(S["soc"], 16 * B2)

    return nc


def kernel(**inp):
    f32 = np.float32

    def pshape(a, nblk):
        x = np.asarray(a)
        rest = x.shape[1:]
        return np.ascontiguousarray(
            x.reshape(nblk, 128, *rest).transpose(1, 0, *range(2, 2 + len(rest)))
            .reshape(128, -1))

    hid_all = np.asarray(inp["L0_attn_hidden_states"])
    w_in = np.asarray(inp["L0_attn_W_in"])
    conv_w = np.asarray(inp["L0_attn_conv_w"])
    conv_b = np.asarray(inp["L0_attn_conv_b"])
    w_x = np.asarray(inp["L0_attn_W_x"])
    w_dt = np.asarray(inp["L0_attn_W_dt"])
    dt_bias = np.asarray(inp["L0_attn_dt_bias"])
    a_m = np.asarray(inp["L0_attn_A"])
    d_v = np.asarray(inp["L0_attn_D"])
    w_out = np.asarray(inp["L0_attn_W_out"])
    gate_w = np.asarray(inp["L0_ffn_gate_w"])
    up_w = np.asarray(inp["L0_ffn_up_w"])
    down_w = np.asarray(inp["L0_ffn_down_w"])
    gate_s = np.asarray(inp["L0_ffn_gate_s"])
    up_s = np.asarray(inp["L0_ffn_up_s"])
    down_s = np.asarray(inp["L0_ffn_down_s"])

    gateT_h = pshape(np.ascontiguousarray(gate_w.T), 16).astype(f32)
    upT_h = pshape(np.ascontiguousarray(up_w.T), 16).astype(f32)
    downT_h = pshape(np.ascontiguousarray(down_w.T), 44).astype(f32)
    gs_h = pshape(gate_s.reshape(-1, 1), 44).astype(f32)
    us_h = pshape(up_s.reshape(-1, 1), 44).astype(f32)
    ds_h = pshape(down_s.reshape(-1, 1), 16).astype(f32)

    in_maps = []
    for c in range(8):
        p, q = c // 4, c % 4
        dq = slice(1024 * q, 1024 * (q + 1))
        hid_fm = np.ascontiguousarray(hid_all[2 * p:2 * p + 2].transpose(0, 2, 1))
        hid_h = np.stack([pshape(hid_fm[jj], 16) for jj in range(2)])
        hidq_h = np.stack([pshape(np.ascontiguousarray(
            hid_fm[jj][:, 512 * q:512 * (q + 1)]), 16) for jj in range(2)])
        winT_q = np.ascontiguousarray(
            np.concatenate([w_in[dq], w_in[4096 + 1024 * q:4096 + 1024 * (q + 1)]], 0).T)
        im = {
            "hid": hid_h.astype(f32),
            "hidq": hidq_h.astype(f32),
            "winT": pshape(winT_q, 16).astype(f32),
            "wxT": pshape(np.ascontiguousarray(w_x[:, dq].T), 8).astype(f32),
            "wdtT": np.ascontiguousarray(w_dt[dq].T).astype(f32),
            "woutT": pshape(np.ascontiguousarray(w_out[:, dq].T), 8).astype(f32),
            "convw": pshape(conv_w[dq], 8).astype(f32),
            "convb": pshape(conv_b[dq].reshape(-1, 1), 8).astype(f32),
            "dtb": pshape(dt_bias[dq].reshape(-1, 1), 8).astype(f32),
            "a_mat": pshape(a_m[dq], 8).astype(f32),
            "d_vec": pshape(d_v[dq].reshape(-1, 1), 8).astype(f32),
            "gateT": gateT_h, "upT": upT_h, "downT": downT_h,
            "gs": gs_h, "us": us_h, "ds": ds_h,
        }
        in_maps.append(im)

    nc = build_nc()
    res = run_bass_kernel_spmd(nc, in_maps, core_ids=list(range(8)))

    out = np.empty((4, 2048, 2048), f32)
    for c in range(8):
        p, q = c // 4, c % 4
        oc = np.asarray(res.results[c]["outy"])
        for jj in range(2):
            om = oc[jj].reshape(128, 16, 512).transpose(1, 0, 2).reshape(2048, 512)
            out[2 * p + jj, 512 * q:512 * (q + 1), :] = om.T
    return out


# version 25 (final)
# speedup vs baseline: 1.0519x; 1.0322x over previous
import numpy as np
import concourse.bass as bass
import concourse.mybir as mybir
from concourse.bass_utils import run_bass_kernel_spmd
from contextlib import ExitStack

F32 = mybir.dt.float32
BF16 = mybir.dt.bfloat16
AF = mybir.ActivationFunctionType
ALU = mybir.AluOpType

P = 128
B2 = 2
LL = 2048
DM = 2048
NB = 8
NST = 16
LCA = 512
NCA = LL // LCA
LCB = 256
NCB = LL // LCB
TQ = 512
NIC = 11
EPS = 1e-5
DQ = 1024


def build_nc():
    nc = bass.Bass("TRN2", target_bir_lowering=False)

    hid = nc.dram_tensor("hid", [B2, P, 16 * LL], F32, kind="ExternalInput").ap()
    hidq = nc.dram_tensor("hidq", [B2, P, 16 * TQ], F32, kind="ExternalInput").ap()
    winT = nc.dram_tensor("winT", [P, 16 * DM], F32, kind="ExternalInput").ap()
    wxT = nc.dram_tensor("wxT", [P, NB * 160], F32, kind="ExternalInput").ap()
    wdtT = nc.dram_tensor("wdtT", [P, DQ], F32, kind="ExternalInput").ap()
    woutT = nc.dram_tensor("woutT", [P, NB * DM], F32, kind="ExternalInput").ap()
    convw = nc.dram_tensor("convw", [P, NB * 4], F32, kind="ExternalInput").ap()
    convb = nc.dram_tensor("convb", [P, NB], F32, kind="ExternalInput").ap()
    dtb = nc.dram_tensor("dtb", [P, NB], F32, kind="ExternalInput").ap()
    a_mat = nc.dram_tensor("a_mat", [P, NB * NST], F32, kind="ExternalInput").ap()
    d_vec = nc.dram_tensor("d_vec", [P, NB], F32, kind="ExternalInput").ap()
    gateT = nc.dram_tensor("gateT", [P, 16 * 5632], F32, kind="ExternalInput").ap()
    upT = nc.dram_tensor("upT", [P, 16 * 5632], F32, kind="ExternalInput").ap()
    downT = nc.dram_tensor("downT", [P, 44 * DM], F32, kind="ExternalInput").ap()
    gs = nc.dram_tensor("gs", [P, 44], F32, kind="ExternalInput").ap()
    us = nc.dram_tensor("us", [P, 44], F32, kind="ExternalInput").ap()
    ds = nc.dram_tensor("ds", [P, 16], F32, kind="ExternalInput").ap()

    outy = nc.dram_tensor("outy", [B2, P, 16 * TQ], F32, kind="ExternalOutput").ap()

    xc_buf = nc.dram_tensor("xc_buf", [B2 * P, NB * LL], BF16)
    z_buf = nc.dram_tensor("z_buf", [B2 * P, NB * LL], BF16)
    ssm_p = [nc.dram_tensor(f"ssm_p{j}", [160, LL], F32) for j in range(B2)]
    ssm_r = [nc.dram_tensor(f"ssm_r{j}", [160, LL], F32) for j in range(B2)]
    outp = [nc.dram_tensor(f"outp{j}", [4, DM * TQ], F32) for j in range(B2)]
    outr = [nc.dram_tensor(f"outr{j}", [1, DM * TQ], F32) for j in range(B2)]

    GROUPS = [[0, 1, 2, 3], [4, 5, 6, 7]]
    NT_A = B2 * NCA
    NT_B = B2 * NCB

    with ExitStack() as ctx0:
        a_sb = ctx0.enter_context(nc.sbuf_tensor("a_sb", [P, NB * NST], F32))
        convw_sb = ctx0.enter_context(nc.sbuf_tensor("convw_sb", [P, NB * 4], F32))
        convb_sb = ctx0.enter_context(nc.sbuf_tensor("convb_sb", [P, NB], F32))
        dtb_sb = ctx0.enter_context(nc.sbuf_tensor("dtb_sb", [P, NB], F32))
        d_sb = ctx0.enter_context(nc.sbuf_tensor("d_sb", [P, NB], F32))
        gs_sb = ctx0.enter_context(nc.sbuf_tensor("gs_sb", [P, 44], F32))
        us_sb = ctx0.enter_context(nc.sbuf_tensor("us_sb", [P, 44], F32))
        ds_sb = ctx0.enter_context(nc.sbuf_tensor("ds_sb", [P, 16], F32))
        wxT_sb = ctx0.enter_context(nc.sbuf_tensor("wxT_sb", [P, NB * 160], BF16))
        wdtT_sb = ctx0.enter_context(nc.sbuf_tensor("wdtT_sb", [P, DQ], BF16))
        ones_c = ctx0.enter_context(nc.sbuf_tensor("ones_c", [P, 1], F32))
        ones_r = ctx0.enter_context(nc.sbuf_tensor("ones_r", [1, P], F32))
        carry = ctx0.enter_context(nc.sbuf_tensor("carry", [P, NST * NB], F32))
        halo = ctx0.enter_context(nc.sbuf_tensor("halo", [P, NB * 3], BF16))
        tmp8 = ctx0.enter_context(nc.sbuf_tensor("tmp8", [P, NB], F32))
        epsv = ctx0.enter_context(nc.sbuf_tensor("epsv", [P, 1], F32))

        S = {}
        for nm in ["wl", "vw", "ha", "va", "mmA", "mmS", "evA", "evS", "cvA",
                   "siA", "stA", "ccS", "phA",
                   "wb", "vwb", "hb", "vb", "mb_", "eb_", "spb", "vn", "sn",
                   "vg", "mo", "eo", "sb_", "ccO", "phB",
                   "hc", "vx1", "sqc", "tql", "tL", "eL", "vL", "vln",
                   "vgc", "tic", "egc", "vh", "vdc", "toc", "edc", "vp",
                   "vout", "soc", "vsf", "vym", "pya"]:
            S[nm] = ctx0.enter_context(nc.semaphore(nm))

        a3 = a_sb[:, :].rearrange("p (b n) -> p b n", b=NB)
        hid_v = hid.rearrange("j p (b l) -> j p b l", b=16)
        xcb_v = xc_buf.ap().rearrange("(j p) (b l) -> j p b l", j=B2, b=NB)
        zb_v = z_buf.ap().rearrange("(j p) (b l) -> j p b l", j=B2, b=NB)
        outp_v = [o.ap().rearrange("q (mb p l) -> q p mb l", mb=16, p=P) for o in outp]
        outr_v = [o.ap().rearrange("o (mb p l) -> o p mb l", mb=16, p=P) for o in outr]

        with ExitStack() as ctxA:
            winT_sb = ctxA.enter_context(nc.sbuf_tensor("winT_sb", [P, 16 * DM], BF16))
            hidf = ctxA.enter_context(nc.sbuf_tensor("hidf", [P, 16 * LCA], F32))
            hidb = ctxA.enter_context(nc.sbuf_tensor("hidb", [P, 16 * LCA], BF16))
            xpre = ctxA.enter_context(nc.sbuf_tensor("xpre", [P, NB * (LCA + 3)], BF16))
            xc_t = ctxA.enter_context(nc.sbuf_tensor("xc_t", [P, NB * LCA], BF16))
            zs_t = ctxA.enter_context(nc.sbuf_tensor("zs_t", [P, NB * LCA], BF16))
            accv = ctxA.enter_context(nc.sbuf_tensor("accv", [P, NB * LCA], F32))
            sst0 = ctxA.enter_context(nc.sbuf_tensor("sst0", [P, LCA], F32))
            sst1 = ctxA.enter_context(nc.sbuf_tensor("sst1", [32, LCA], F32))
            pa0 = ctxA.enter_context(nc.psum_tensor("pa0", [P, LCA], F32))
            pa1 = ctxA.enter_context(nc.psum_tensor("pa1", [P, LCA], F32))
            psS0 = ctxA.enter_context(nc.psum_tensor("psS0", [P, LCA], F32))
            psS1 = ctxA.enter_context(nc.psum_tensor("psS1", [32, LCA], F32))
            pa = [pa0, pa1]

            xpre3 = xpre[:, :].rearrange("p (b l) -> p b l", b=NB)
            hidf3 = hidf[:, :].rearrange("p (b l) -> p b l", b=16)
            hidb3 = hidb[:, :].rearrange("p (b l) -> p b l", b=16)
            xc3 = xc_t[:, :].rearrange("p (b l) -> p b l", b=NB)
            zs3 = zs_t[:, :].rearrange("p (b l) -> p b l", b=NB)
            acc3 = accv[:, :].rearrange("p (b l) -> p b l", b=NB)
            winT3 = winT_sb[:, :].rearrange("p (b e) -> p b e", b=16)
            wxT3 = wxT_sb[:, :].rearrange("p (b e) -> p b e", b=NB)
            halo3 = halo[:, :].rearrange("p (b k) -> p b k", b=NB)
            winT_v = winT.rearrange("p (b e) -> p b e", b=16)

            with nc.Block() as blk:
                @blk.sync
                def _(sync):
                    for dst, src in [(a_sb, a_mat), (convw_sb, convw), (convb_sb, convb),
                                     (dtb_sb, dtb), (d_sb, d_vec), (gs_sb, gs),
                                     (us_sb, us), (ds_sb, ds)]:
                        sync.dma_start(out=dst[:, :], in_=src).then_inc(S["wl"], 16)
                    sync.dma_start(out=hidf[:, 0:DQ], in_=wdtT).then_inc(S["wl"], 16)
                    sync.wait_ge(S["vw"], 1)
                    sync.dma_start(out=hidf[:, 0:NB * 160], in_=wxT).then_inc(S["wl"], 16)
                    for wc in range(4):
                        sync.wait_ge(S["vw"], 2 + wc)
                        sync.dma_start(out=hidf3[:, :, 0:512],
                                       in_=winT_v[:, :, wc * 512:(wc + 1) * 512]
                                       ).then_inc(S["wl"], 16)
                    for t in range(NT_A):
                        j, Ci = t // NCA, t % NCA
                        if t > 0:
                            sync.wait_ge(S["mmA"], 16 * t)
                        else:
                            sync.wait_ge(S["vw"], 6)
                        sync.dma_start(out=hidf3[:, :, :],
                                       in_=hid_v[j, :, :, Ci * LCA:(Ci + 1) * LCA]
                                       ).then_inc(S["ha"], 16)
                    sync.wait_ge(S["stA"], 64 * NT_A)
                    sync.sem_inc(S["phA"], 1)

                @blk.vector
                def _(vector):
                    vector.memset(ones_c[:, :], 1.0)
                    vector.memset(ones_r[:, :], 1.0)
                    vector.memset(epsv[:, :], EPS)
                    vector.wait_ge(S["wl"], 144)
                    vector.tensor_copy(wdtT_sb[:, :], hidf[:, 0:DQ]).then_inc(S["vw"], 1)
                    vector.wait_ge(S["wl"], 160)
                    vector.tensor_copy(wxT_sb[:, :], hidf[:, 0:NB * 160]).then_inc(S["vw"], 1)
                    for wc in range(4):
                        vector.wait_ge(S["wl"], 176 + 16 * wc)
                        vector.tensor_copy(winT3[:, :, wc * 512:(wc + 1) * 512],
                                           hidf3[:, :, 0:512]).then_inc(S["vw"], 1)
                    for t in range(NT_A):
                        j, Ci = t // NCA, t % NCA
                        vector.wait_ge(S["ha"], 16 * (t + 1))
                        vector.tensor_copy(hidb[:, :], hidf[:, :]).then_inc(S["va"], 1)
                        if Ci == 0:
                            vector.memset(halo[:, :], 0.0)
                        vector.wait_ge(S["evA"], 16 * t + 8)
                        vector.tensor_copy(xpre3[:, :, 0:3], halo3[:, :, :])
                        for db in range(NB):
                            vector.tensor_scalar(
                                out=acc3[:, db, :], in0=xpre3[:, db, 0:LCA],
                                scalar1=convw_sb[:, 4 * db:4 * db + 1],
                                scalar2=convb_sb[:, db:db + 1],
                                op0=ALU.mult, op1=ALU.add)
                        for k in range(1, 4):
                            for db in range(NB):
                                vector.scalar_tensor_tensor(
                                    out=acc3[:, db, :], in0=xpre3[:, db, k:k + LCA],
                                    scalar=convw_sb[:, 4 * db + k:4 * db + k + 1],
                                    in1=acc3[:, db, :], op0=ALU.mult, op1=ALU.add)
                        vector.tensor_copy(halo3[:, :, :], xpre3[:, :, LCA:LCA + 3]
                                           ).then_inc(S["cvA"], 1)
                    vector.wait_ge(S["stA"], 64 * NT_A)
                    vector.sem_inc(S["phA"], 1)

                @blk.tensor
                def _(tensor):
                    for t in range(NT_A):
                        tensor.wait_ge(S["va"], t + 1)
                        tensor.wait_ge(S["evA"], 16 * t)
                        for eb in range(16):
                            if eb >= 2:
                                tensor.wait_ge(S["evA"], 16 * t + eb - 1)
                            for kb in range(16):
                                mmi = tensor.matmul(
                                    pa[eb % 2][:, :],
                                    winT3[:, kb, eb * 128:(eb + 1) * 128],
                                    hidb3[:, kb, :],
                                    start=(kb == 0), stop=(kb == 15))
                                if kb == 15:
                                    mmi.then_inc(S["mmA"], 1)
                        tensor.wait_ge(S["siA"], t + 1)
                        tensor.wait_ge(S["evS"], 2 * t)
                        for kb in range(NB):
                            m0 = tensor.matmul(psS0[:, :], wxT3[:, kb, 0:128],
                                               xc3[:, kb, :], start=(kb == 0), stop=(kb == 7))
                            m1 = tensor.matmul(psS1[:, :], wxT3[:, kb, 128:160],
                                               xc3[:, kb, :], start=(kb == 0), stop=(kb == 7))
                            if kb == 7:
                                m0.then_inc(S["mmS"], 1)
                                m1.then_inc(S["mmS"], 1)
                    tensor.wait_ge(S["stA"], 64 * NT_A)
                    tensor.sem_inc(S["phA"], 1)

                @blk.scalar
                def _(scalar):
                    for t in range(NT_A):
                        scalar.wait_ge(S["cvA"], t)
                        scalar.wait_ge(S["stA"], 64 * t)
                        for eb in range(16):
                            scalar.wait_ge(S["mmA"], 16 * t + eb + 1)
                            if eb < 8:
                                scalar.activation(out=xpre3[:, eb, 3:LCA + 3],
                                                  in_=pa[eb % 2][:, :], func=AF.Copy
                                                  ).then_inc(S["evA"], 1)
                            else:
                                scalar.activation(out=zs3[:, eb - 8, :],
                                                  in_=pa[eb % 2][:, :], func=AF.Silu
                                                  ).then_inc(S["evA"], 1)
                        scalar.wait_ge(S["cvA"], t + 1)
                        scalar.activation(out=xc_t[:, :], in_=accv[:, :], func=AF.Silu
                                          ).then_inc(S["siA"], 1)
                        scalar.wait_ge(S["mmS"], 2 * t + 2)
                        scalar.copy(out=sst0[:, :], in_=psS0[:, :]).then_inc(S["evS"], 1)
                        scalar.copy(out=sst1[:, :], in_=psS1[:, :]).then_inc(S["evS"], 1)
                    scalar.wait_ge(S["stA"], 64 * NT_A)
                    scalar.sem_inc(S["phA"], 1)

                @blk.gpsimd
                def _(gpsimd):
                    for t in range(NT_A):
                        j, Ci = t // NCA, t % NCA
                        ls = slice(Ci * LCA, (Ci + 1) * LCA)
                        gpsimd.wait_ge(S["siA"], t + 1)
                        gpsimd.dma_start(out=xcb_v[j, :, :, ls], in_=xc3[:, :, :]
                                         ).then_inc(S["stA"], 16)
                        gpsimd.wait_ge(S["evA"], 16 * (t + 1))
                        gpsimd.dma_start(out=zb_v[j, :, :, ls], in_=zs3[:, :, :]
                                         ).then_inc(S["stA"], 16)
                        gpsimd.wait_ge(S["evS"], 2 * t + 1)
                        gpsimd.dma_start(out=ssm_p[j].ap()[0:128, ls],
                                         in_=sst0[:, :]).then_inc(S["stA"], 16)
                        gpsimd.wait_ge(S["evS"], 2 * t + 2)
                        gpsimd.dma_start(out=ssm_p[j].ap()[128:160, ls],
                                         in_=sst1[:, :]).then_inc(S["stA"], 16)
                        if Ci == NCA - 1:
                            gpsimd.wait_ge(S["stA"], 64 * (t + 1))
                            gpsimd.collective_compute(
                                "AllReduce", ALU.add, replica_groups=GROUPS,
                                ins=[ssm_p[j].ap().opt()], outs=[ssm_r[j].ap().opt()],
                            ).then_inc(S["ccS"], 1)
                    gpsimd.wait_ge(S["stA"], 64 * NT_A)
                    gpsimd.sem_inc(S["phA"], 1)

        with ExitStack() as ctxB:
            woutT_sb = ctxB.enter_context(nc.sbuf_tensor("woutT_sb", [P, NB * DM], BF16))
            wstageB = ctxB.enter_context(nc.sbuf_tensor("wstageB", [P, NB * 512], F32))
            dtr_f = [ctxB.enter_context(nc.sbuf_tensor(f"dtr_f{i}", [P, LCB], F32))
                     for i in range(2)]
            dtr_b = [ctxB.enter_context(nc.sbuf_tensor(f"dtr_b{i}", [P, LCB], BF16))
                     for i in range(2)]
            b_rep = [ctxB.enter_context(nc.sbuf_tensor(f"b_rep{i}", [P, NST * LCB], F32))
                     for i in range(2)]
            c_rep = [ctxB.enter_context(nc.sbuf_tensor(f"c_rep{i}", [P, NST * LCB], F32))
                     for i in range(2)]

            delta = ctxB.enter_context(nc.sbuf_tensor("delta", [P, NB * LCB], F32))
            at = ctxB.enter_context(nc.sbuf_tensor("at", [P, NB * LCB], F32))
            bt = ctxB.enter_context(nc.sbuf_tensor("bt", [P, NB * LCB], F32))
            st_ = ctxB.enter_context(nc.sbuf_tensor("st_", [P, NB * LCB], F32))
            du_b = ctxB.enter_context(nc.sbuf_tensor("du_b", [P, NB * LCB], BF16))
            yac = ctxB.enter_context(nc.sbuf_tensor("yac", [P, NB * LCB], F32))
            xcB = [ctxB.enter_context(nc.sbuf_tensor(f"xcB{i}", [P, NB * LCB], BF16))
                   for i in range(2)]
            zsB = [ctxB.enter_context(nc.sbuf_tensor(f"zsB{i}", [P, NB * LCB], BF16))
                   for i in range(2)]
            yg = ctxB.enter_context(nc.sbuf_tensor("yg", [P, NB * LCB], BF16))
            wo_st = ctxB.enter_context(nc.sbuf_tensor("wo_st", [P, 16 * LCB], F32))
            pd0 = ctxB.enter_context(nc.psum_tensor("pd0", [P, LCB], F32))
            pd1 = ctxB.enter_context(nc.psum_tensor("pd1", [P, LCB], F32))
            po0 = ctxB.enter_context(nc.psum_tensor("po0", [P, LCB], F32))
            po1 = ctxB.enter_context(nc.psum_tensor("po1", [P, LCB], F32))
            pd = [pd0, pd1]
            po = [po0, po1]

            woutT3 = woutT_sb[:, :].rearrange("p (b m) -> p b m", b=NB)
            wstage3 = wstageB[:, :].rearrange("p (b m) -> p b m", b=NB)
            woutT_v = woutT.rearrange("p (b m) -> p b m", b=NB)
            pass
            at2f = wstageB[:, 0:NB * LCB]
            ytmp = bt[:, :]
            delta = [delta[:, :], wstageB[:, NB * LCB:2 * NB * LCB]]
            delta3 = [d.rearrange("p (b l) -> p b l", b=NB) for d in delta]
            atv = [at[:, :], at2f]
            atv3 = [v.rearrange("p (b l) -> p b l", b=NB) for v in atv]
            at3 = atv3[0]
            bt3 = bt[:, :].rearrange("p (b l) -> p b l", b=NB)
            st3 = st_[:, :].rearrange("p (b l) -> p b l", b=NB)
            du3 = du_b[:, :].rearrange("p (b l) -> p b l", b=NB)
            yac3 = yac[:, :].rearrange("p (b l) -> p b l", b=NB)
            xcB3 = [x[:, :].rearrange("p (b l) -> p b l", b=NB) for x in xcB]
            zsB3 = [x[:, :].rearrange("p (b l) -> p b l", b=NB) for x in zsB]
            yg3 = yg[:, :].rearrange("p (b l) -> p b l", b=NB)
            wo3 = wo_st[:, :].rearrange("p (b l) -> p b l", b=16)

            with nc.Block() as blk:
                @blk.sync
                def _(sync):
                    sync.wait_ge(S["phA"], 5)
                    for wc in range(4):
                        if wc > 0:
                            sync.wait_ge(S["vwb"], wc)
                        sync.dma_start(out=wstage3[:, :, 0:512],
                                       in_=woutT_v[:, :, wc * 512:(wc + 1) * 512]
                                       ).then_inc(S["wb"], 16)
                    for t in range(NT_B):
                        j, Ci = t // NCB, t % NCB
                        pt = t % 2
                        ls = slice(Ci * LCB, (Ci + 1) * LCB)
                        if Ci == 0:
                            sync.wait_ge(S["ccS"], j + 1)
                        if t >= 2:
                            sync.wait_ge(S["vg"], t - 1)
                        sync.dma_start(out=xcB3[pt][:, :, :], in_=xcb_v[j, :, :, ls]
                                       ).then_inc(S["hb"], 16)
                        sync.dma_start(out=zsB3[pt][:, :, :], in_=zb_v[j, :, :, ls]
                                       ).then_inc(S["hb"], 16)
                        sync.dma_start(out=dtr_f[pt][:, :],
                                       in_=ssm_r[j].ap()[0:128, ls]
                                       ).then_inc(S["hb"], 16)
                        sync.dma_start(
                            out=b_rep[pt][:, :].rearrange("p (n l) -> p n l", n=NST),
                            in_=ssm_r[j].ap()[128:144, ls
                                              ].unsqueeze(0).broadcast_to([P, NST, LCB])
                        ).then_inc(S["hb"], 16)
                        sync.dma_start(
                            out=c_rep[pt][:, :].rearrange("p (n l) -> p n l", n=NST),
                            in_=ssm_r[j].ap()[144:160, ls
                                              ].unsqueeze(0).broadcast_to([P, NST, LCB])
                        ).then_inc(S["hb"], 16)
                    sync.wait_ge(S["sb_"], 16 * NT_B)
                    sync.sem_inc(S["phB"], 1)

                @blk.vector
                def _(vector):
                    vector.wait_ge(S["phA"], 5)
                    for wc in range(4):
                        vector.wait_ge(S["wb"], 16 * (wc + 1))
                        vector.tensor_copy(woutT3[:, :, wc * 512:(wc + 1) * 512],
                                           wstage3[:, :, 0:512]).then_inc(S["vwb"], 1)
                    for t in range(NT_B):
                        j, Ci = t // NCB, t % NCB
                        pt = t % 2
                        dp = delta[pt]
                        if t == 0:
                            vector.wait_ge(S["hb"], 48)
                            vector.tensor_copy(dtr_b[0][:, :], dtr_f[0][:, :]
                                               ).then_inc(S["vb"], 1)
                        vector.wait_ge(S["hb"], 80 * (t + 1))
                        if Ci == 0:
                            vector.memset(carry[:, :], 0.0)
                        vector.wait_ge(S["spb"], t + 1)
                        vector.tensor_add(dp, dp, at[:, :]).then_inc(S["vn"], 1)
                        vector.tensor_tensor(out=du_b[:, :], in0=dp,
                                             in1=xcB[pt][:, :], op=ALU.mult)
                        if t + 1 < NT_B:
                            vector.wait_ge(S["hb"], 80 * (t + 1) + 48)
                            vector.tensor_copy(dtr_b[(t + 1) % 2][:, :],
                                               dtr_f[(t + 1) % 2][:, :]
                                               ).then_inc(S["vb"], 1)
                        for n in range(NST):
                            ap3 = atv3[n % 2]
                            vector.wait_ge(S["sn"], NST * t + n + 1)
                            b_bc = b_rep[pt][:, n * LCB:(n + 1) * LCB].unsqueeze(1
                                        ).broadcast_to([P, NB, LCB])
                            vector.tensor_tensor(out=bt3, in0=du3, in1=b_bc, op=ALU.mult)
                            cu = carry[:, n * NB:(n + 1) * NB].unsqueeze(2)
                            for db in range(NB):
                                sci = vector.tensor_tensor_scan(
                                    out=st3[:, db, :], data0=ap3[:, db, :],
                                    data1=bt3[:, db, :],
                                    initial=carry[:, n * NB + db:n * NB + db + 1],
                                    op0=ALU.mult, op1=ALU.add)
                                if db == NB - 1:
                                    sci.then_inc(S["vsf"], 1)
                            vector.wait_ge(S["vsf"], NST * t + n + 1)
                            vector.tensor_copy(cu, st3[:, :, LCB - 1:LCB])
                            c_bc = c_rep[pt][:, n * LCB:(n + 1) * LCB].unsqueeze(1
                                        ).broadcast_to([P, NB, LCB])
                            if n == 0:
                                vector.tensor_tensor(out=yac3, in0=st3, in1=c_bc,
                                                     op=ALU.mult)
                            else:
                                vector.tensor_tensor(
                                    out=ytmp[:, :].rearrange("p (b l) -> p b l", b=NB),
                                    in0=st3, in1=c_bc, op=ALU.mult)
                                vector.tensor_add(yac[:, :], yac[:, :], ytmp[:, :])
                        for db in range(NB):
                            vector.scalar_tensor_tensor(
                                out=yac3[:, db, :], in0=xcB3[pt][:, db, :],
                                scalar=d_sb[:, db:db + 1], in1=yac3[:, db, :],
                                op0=ALU.mult, op1=ALU.add)
                        vector.wait_ge(S["mo"], 16 * t)
                        vector.tensor_tensor(out=yg[:, :], in0=yac[:, :],
                                             in1=zsB[pt][:, :],
                                             op=ALU.mult).then_inc(S["vg"], 1)
                    vector.wait_ge(S["sb_"], 16 * NT_B)
                    vector.sem_inc(S["phB"], 1)

                @blk.tensor
                def _(tensor):
                    tensor.wait_ge(S["phA"], 5)

                    def delta_mm(tensor, k):
                        tensor.wait_ge(S["vb"], k + 1)
                        tensor.wait_ge(S["eb_"], 8 * k)
                        for db in range(NB):
                            if db >= 2:
                                tensor.wait_ge(S["eb_"], 8 * k + db - 1)
                            tensor.matmul(pd[db % 2][:, :],
                                          wdtT_sb[:, db * 128:(db + 1) * 128],
                                          dtr_b[k % 2][:, :], start=True, stop=True
                                          ).then_inc(S["mb_"], 1)

                    delta_mm(tensor, 0)
                    for t in range(NT_B):
                        if t + 1 < NT_B:
                            delta_mm(tensor, t + 1)
                        tensor.wait_ge(S["vg"], t + 1)
                        tensor.wait_ge(S["eo"], 16 * t)
                        if t == 0:
                            tensor.wait_ge(S["vwb"], 4)
                        for mb in range(16):
                            if mb >= 2:
                                tensor.wait_ge(S["eo"], 16 * t + mb - 1)
                            for kb in range(NB):
                                mmi = tensor.matmul(po[mb % 2][:, :],
                                                    woutT3[:, kb, mb * 128:(mb + 1) * 128],
                                                    yg3[:, kb, :],
                                                    start=(kb == 0), stop=(kb == 7))
                                if kb == 7:
                                    mmi.then_inc(S["mo"], 1)
                    tensor.wait_ge(S["sb_"], 16 * NT_B)
                    tensor.sem_inc(S["phB"], 1)

                @blk.scalar
                def _(scalar):
                    scalar.wait_ge(S["phA"], 5)

                    def softplus(scalar, k):
                        if k > 0:
                            scalar.wait_ge(S["vsf"], NST * (k - 1) + 15)
                        dk3 = delta3[k % 2]
                        for db in range(NB):
                            scalar.wait_ge(S["mb_"], 8 * k + db + 1)
                            scalar.activation(out=at3[:, db, :], in_=pd[db % 2][:, :],
                                              func=AF.Abs, bias=dtb_sb[:, db:db + 1])
                            scalar.activation(out=dk3[:, db, :], in_=pd[db % 2][:, :],
                                              func=AF.Relu, bias=dtb_sb[:, db:db + 1]
                                              ).then_inc(S["eb_"], 1)
                        scalar.activation(out=at[:, :], in_=at[:, :], func=AF.Exp,
                                          scale=-1.0)
                        scalar.activation(out=at[:, :], in_=at[:, :], func=AF.Ln,
                                          bias=1.0).then_inc(S["spb"], 1)

                    softplus(scalar, 0)
                    for t in range(NT_B):
                        dp3s = delta3[t % 2]
                        scalar.wait_ge(S["vn"], t + 1)
                        for n in range(NST):
                            if n >= 2:
                                scalar.wait_ge(S["vsf"], NST * t + n - 1)
                            for db in range(NB):
                                ai = scalar.activation(
                                    out=atv3[n % 2][:, db, :], in_=dp3s[:, db, :],
                                    func=AF.Exp,
                                    scale=a_sb[:, db * NST + n:db * NST + n + 1])
                                if db == NB - 1:
                                    ai.then_inc(S["sn"], 1)
                        if t + 1 < NT_B:
                            softplus(scalar, t + 1)
                        scalar.wait_ge(S["sb_"], 16 * t)
                        for mb in range(16):
                            scalar.wait_ge(S["mo"], 16 * t + mb + 1)
                            scalar.copy(out=wo3[:, mb, :], in_=po[mb % 2][:, :]
                                        ).then_inc(S["eo"], 1)
                    scalar.wait_ge(S["sb_"], 16 * NT_B)
                    scalar.sem_inc(S["phB"], 1)

                @blk.gpsimd
                def _(gpsimd):
                    gpsimd.wait_ge(S["phA"], 5)
                    for t in range(NT_B):
                        j, Ci = t // NCB, t % NCB
                        qc, oc = Ci // 2, (Ci % 2) * LCB
                        gpsimd.wait_ge(S["eo"], 16 * (t + 1))
                        gpsimd.dma_start(out=outp_v[j][qc][:, :, oc:oc + LCB],
                                         in_=wo3[:, :, :]).then_inc(S["sb_"], 16)
                    gpsimd.wait_ge(S["sb_"], 16 * NT_B)
                    gpsimd.sem_inc(S["phB"], 1)
                    for j in range(B2):
                        gpsimd.collective_compute(
                            "ReduceScatter", ALU.add, replica_groups=GROUPS,
                            ins=[outp[j].ap().opt()], outs=[outr[j].ap().opt()],
                        ).then_inc(S["ccO"], 1)

        with ExitStack() as ctxC:
            x1 = ctxC.enter_context(nc.sbuf_tensor("x1", [P, 16 * TQ], F32))
            bufA = ctxC.enter_context(nc.sbuf_tensor("bufA", [P, 16 * TQ], F32))
            bufBF = ctxC.enter_context(nc.sbuf_tensor("bufBF", [P, 24576], BF16))
            h_all = ctxC.enter_context(nc.sbuf_tensor("h_all", [P, 44 * TQ], BF16))
            pre2 = ctxC.enter_context(nc.sbuf_tensor("pre2", [P, 16 * TQ], F32))
            sq1 = ctxC.enter_context(nc.sbuf_tensor("sq1", [P, TQ], F32))
            rows = ctxC.enter_context(nc.sbuf_tensor("rows", [1, 2 * TQ], F32))
            mu_bc = ctxC.enter_context(nc.sbuf_tensor("mu_bc", [P, TQ], F32))
            rs_bc = ctxC.enter_context(nc.sbuf_tensor("rs_bc", [P, TQ], F32))
            gsil = ctxC.enter_context(nc.sbuf_tensor("gsil", [P, TQ], BF16))
            uscb = ctxC.enter_context(nc.sbuf_tensor("uscb", [P, TQ], BF16))
            ffn_t = sq1
            psG = ctxC.enter_context(nc.psum_tensor("psG", [P, TQ], F32))
            psU = ctxC.enter_context(nc.psum_tensor("psU", [P, TQ], F32))
            psL0 = ctxC.enter_context(nc.psum_tensor("psL0", [1, TQ], F32))
            psL1 = ctxC.enter_context(nc.psum_tensor("psL1", [1, TQ], F32))

            x1_3 = x1[:, :].rearrange("p (b l) -> p b l", b=16)
            bufA3 = bufA[:, :].rearrange("p (b l) -> p b l", b=16)
            pre2_3 = pre2[:, :].rearrange("p (b l) -> p b l", b=16)
            dst11 = bufA[:, 0:11 * 512].rearrange("p (g m) -> p g m", g=11)
            gbf = bufBF[:, 0:8192].rearrange("p (b i) -> p b i", b=16)
            ubf = bufBF[:, 8192:16384].rearrange("p (b i) -> p b i", b=16)
            xlnb = bufBF[:, 16384:24576].rearrange("p (b l) -> p b l", b=16)
            dbf = bufBF[:, 0:22528].rearrange("p (b m) -> p b m", b=44)
            h3 = h_all[:, :].rearrange("p (b l) -> p b l", b=44)
            out3 = bufA3
            gateT_v = gateT.rearrange("p (b i) -> p b i", b=16)
            upT_v = upT.rearrange("p (b i) -> p b i", b=16)
            downT_v = downT.rearrange("p (b m) -> p b m", b=44)
            hidq_v = hidq.rearrange("j p (b l) -> j p b l", b=16)
            outy_v = outy.rearrange("j p (b l) -> j p b l", b=16)
            r_m, r_s = rows[:, 0:TQ], rows[:, TQ:2 * TQ]
            r_mu, r_v = mu_bc[0:1, :], rs_bc[0:1, :]

            SQ_J = 32
            TQL_J = 32
            TL_J = 4
            EL_J = 10
            VL_J = 4
            HC_J = 32 + NIC * 128 + 16 * 16
            VGC_J = 8 * NIC
            TIC_J = 8 * NIC
            EGC_J = 8 * NIC
            VH_J = 4 * NIC
            VDC_J = 16
            TOC_J = 16
            EDC_J = 16
            VP_J = 16

            def ln_scalar(scalar, xt3, b_sq, b_tql, b_tL, b_eL, b_vL):
                for kb in range(16):
                    scalar.wait_ge(S["tql"], b_tql + kb)
                    scalar.activation(out=sq1[:, :], in_=xt3[:, kb, :],
                                      func=AF.Square).then_inc(S["sqc"], 1)
                scalar.wait_ge(S["tql"], b_tql + 16)
                scalar.copy(out=r_m, in_=psL0[:, :]).then_inc(S["eL"], 1)
                scalar.copy(out=r_s, in_=psL1[:, :]).then_inc(S["eL"], 1)
                scalar.wait_ge(S["vL"], b_vL + 1)
                scalar.activation(out=r_v, in_=r_v, func=AF.Sqrt,
                                  bias=epsv[0:1, 0:1]).then_inc(S["eL"], 1)
                scalar.wait_ge(S["tL"], b_tL + 2)
                scalar.copy(out=mu_bc[:, :], in_=psG[:, :]).then_inc(S["eL"], 1)
                scalar.copy(out=rs_bc[:, :], in_=psU[:, :]).then_inc(S["eL"], 1)

            def ln_tensor(tensor, xt3, b_sq, b_tql, b_tL, b_eL, b_vL):
                for kb in range(16):
                    tensor.matmul(psL0[:, :], ones_c[:, 0:1], xt3[:, kb, :],
                                  start=(kb == 0), stop=(kb == 15))
                for kb in range(16):
                    tensor.wait_ge(S["sqc"], b_sq + kb + 1)
                    tensor.matmul(psL1[:, :], ones_c[:, 0:1], sq1[:, :],
                                  start=(kb == 0), stop=(kb == 15)
                                  ).then_inc(S["tql"], 1)
                tensor.wait_ge(S["vL"], b_vL + 2)
                tensor.matmul(psG[:, :], ones_r[0:1, :], r_mu, start=True, stop=True
                              ).then_inc(S["tL"], 1)
                tensor.matmul(psU[:, :], ones_r[0:1, :], r_v, start=True, stop=True
                              ).then_inc(S["tL"], 1)

            def ln_vector(vector, xt3, outt3, b_sq, b_tql, b_tL, b_eL, b_vL,
                          inc=None):
                vector.wait_ge(S["eL"], b_eL + 2)
                vector.tensor_scalar_mul(r_mu, r_m, 1.0 / DM)
                vector.tensor_tensor(out=r_v, in0=r_mu, in1=r_mu, op=ALU.mult)
                vector.scalar_tensor_tensor(out=r_v, in0=r_s, scalar=1.0 / DM,
                                            in1=r_v, op0=ALU.mult,
                                            op1=ALU.subtract).then_inc(S["vL"], 1)
                vector.wait_ge(S["eL"], b_eL + 3)
                vector.reciprocal(r_v, r_v).then_inc(S["vL"], 1)
                vector.wait_ge(S["eL"], b_eL + 5)
                mu16 = mu_bc[:, :].unsqueeze(1).broadcast_to([P, 16, TQ])
                rs16 = rs_bc[:, :].unsqueeze(1).broadcast_to([P, 16, TQ])
                vector.tensor_tensor(out=outt3, in0=xt3, in1=mu16, op=ALU.subtract)
                ti = vector.tensor_tensor(out=outt3, in0=outt3, in1=rs16, op=ALU.mult)
                if inc is not None:
                    ti.then_inc(*inc)

            with nc.Block() as blk:
                @blk.sync
                def _(sync):
                    sync.wait_ge(S["phB"], 5)
                    for j in range(B2):
                        hcb = j * HC_J
                        sync.wait_ge(S["ccO"], j + 1)
                        if j > 0:
                            sync.wait_ge(S["vout"], j)
                            sync.wait_ge(S["soc"], 16 * j)
                        sync.dma_start(out=x1_3[:, :, :], in_=hidq_v[j, :, :, :]
                                       ).then_inc(S["hc"], 16)
                        sync.dma_start(out=bufA3[:, :, :], in_=outr_v[j][0]
                                       ).then_inc(S["hc"], 16)
                        for ic in range(NIC):
                            if ic == 0:
                                sync.wait_ge(S["vx1"], j + 1)
                            for gi in range(8):
                                wsrc = gateT_v if gi < 4 else upT_v
                                sub = gi % 4
                                kbs = slice(4 * sub, 4 * sub + 4)
                                slot = gi % 4
                                g_all = 8 * ic + gi
                                if g_all >= 4:
                                    sync.wait_ge(S["vgc"], j * VGC_J + g_all - 3)
                                sync.dma_start(
                                    out=bufA3[:, 4 * slot:4 * slot + 4, :],
                                    in_=wsrc[:, kbs, ic * 512:(ic + 1) * 512]
                                    ).then_inc(S["hc"], 16)
                        for mc in range(4):
                            for sub in range(4):
                                if mc == 0 and sub == 0:
                                    sync.wait_ge(S["vgc"], (j + 1) * VGC_J)
                                sync.wait_ge(S["vdc"], j * VDC_J + 4 * mc + sub)
                                sync.dma_start(
                                    out=dst11,
                                    in_=downT_v[:, sub * 11:(sub + 1) * 11,
                                                mc * 512:(mc + 1) * 512]
                                    ).then_inc(S["hc"], 16)

                @blk.vector
                def _(vector):
                    vector.wait_ge(S["phB"], 5)
                    for j in range(B2):
                        hcb = j * HC_J
                        vector.wait_ge(S["hc"], hcb + 32)
                        vector.tensor_add(x1[:, :], x1[:, :], bufA[:, :]
                                          ).then_inc(S["vx1"], 1)
                        ln_vector(vector, x1_3, x1_3, j * SQ_J, j * TQL_J, j * TL_J,
                                  j * EL_J, j * VL_J)
                        if j > 0:
                            vector.wait_ge(S["toc"], j * TOC_J)
                        vector.tensor_copy(xlnb, x1_3).then_inc(S["vln"], 1)
                        for ic in range(NIC):
                            for gi in range(8):
                                g_all = 8 * ic + gi
                                if g_all >= 8:
                                    vector.wait_ge(S["tic"], j * TIC_J + 8 * ic)
                                vector.wait_ge(S["hc"], hcb + 32 + 16 * (g_all + 1))
                                dst = gbf if gi < 4 else ubf
                                sub = gi % 4
                                slot = gi % 4
                                vector.tensor_copy(dst[:, 4 * sub:4 * sub + 4, :],
                                                   bufA3[:, 4 * slot:4 * slot + 4, :]
                                                   ).then_inc(S["vgc"], 1)
                            for ib in range(4):
                                vector.wait_ge(S["egc"],
                                               j * EGC_J + 8 * ic + 2 * (ib + 1))
                                vector.tensor_tensor(
                                    out=h3[:, ic * 4 + ib, :], in0=gsil[:, :],
                                    in1=uscb[:, :], op=ALU.mult).then_inc(S["vh"], 1)
                        for mc in range(4):
                            for sub in range(4):
                                if mc == 0 and sub == 0:
                                    vector.wait_ge(S["tic"], (j + 1) * TIC_J)
                                else:
                                    vector.wait_ge(S["toc"], j * TOC_J + 4 * mc)
                                vector.wait_ge(S["hc"],
                                               hcb + 32 + 128 * NIC + 16 * (4 * mc + sub + 1))
                                vector.tensor_copy(dbf[:, sub * 11:(sub + 1) * 11, :],
                                                   dst11).then_inc(S["vdc"], 1)
                            for mb in range(4):
                                idx = 4 * mc + mb
                                vector.wait_ge(S["edc"], j * EDC_J + idx + 1)
                                vector.tensor_add(pre2_3[:, idx, :], ffn_t[:, :],
                                                  x1_3[:, idx, :]).then_inc(S["vp"], 1)
                        ln_vector(vector, pre2_3, out3, j * SQ_J + 16, j * TQL_J + 16,
                                  j * TL_J + 2, j * EL_J + 5, j * VL_J + 2,
                                  inc=(S["vout"], 1))

                @blk.tensor
                def _(tensor):
                    tensor.wait_ge(S["phB"], 5)
                    for j in range(B2):
                        tensor.wait_ge(S["vx1"], j + 1)
                        ln_tensor(tensor, x1_3, j * SQ_J, j * TQL_J, j * TL_J,
                                  j * EL_J, j * VL_J)
                        tensor.wait_ge(S["vln"], j + 1)
                        for ic in range(NIC):
                            for ib in range(4):
                                tensor.wait_ge(S["egc"], j * EGC_J + 8 * ic + 2 * ib)
                                for kb in range(16):
                                    if ib == 0 and kb % 4 == 0:
                                        tensor.wait_ge(S["vgc"],
                                                       j * VGC_J + 8 * ic + kb // 4 + 1)
                                    mmi = tensor.matmul(psG[:, :],
                                                        gbf[:, kb, ib * 128:(ib + 1) * 128],
                                                        xlnb[:, kb, :],
                                                        start=(kb == 0), stop=(kb == 15))
                                    if kb == 15:
                                        mmi.then_inc(S["tic"], 1)
                                for kb in range(16):
                                    if ib == 0 and kb % 4 == 0:
                                        tensor.wait_ge(S["vgc"],
                                                       j * VGC_J + 8 * ic + 4 + kb // 4 + 1)
                                    mmi = tensor.matmul(psU[:, :],
                                                        ubf[:, kb, ib * 128:(ib + 1) * 128],
                                                        xlnb[:, kb, :],
                                                        start=(kb == 0), stop=(kb == 15))
                                    if kb == 15:
                                        mmi.then_inc(S["tic"], 1)
                        tensor.wait_ge(S["vh"], (j + 1) * VH_J)
                        tensor.wait_ge(S["egc"], (j + 1) * EGC_J)
                        for mc in range(4):
                            for mb in range(4):
                                idx = 4 * mc + mb
                                if idx >= 2:
                                    tensor.wait_ge(S["edc"], j * EDC_J + idx - 1)
                                pot = psG if idx % 2 == 0 else psU
                                for ib in range(44):
                                    if mb == 0 and ib % 11 == 0:
                                        tensor.wait_ge(S["vdc"],
                                                       j * VDC_J + 4 * mc + ib // 11 + 1)
                                    mmi = tensor.matmul(pot[:, :],
                                                        dbf[:, ib, mb * 128:(mb + 1) * 128],
                                                        h3[:, ib, :],
                                                        start=(ib == 0), stop=(ib == 43))
                                    if ib == 43:
                                        mmi.then_inc(S["toc"], 1)
                        tensor.wait_ge(S["vp"], (j + 1) * VP_J)
                        ln_tensor(tensor, pre2_3, j * SQ_J + 16, j * TQL_J + 16,
                                  j * TL_J + 2, j * EL_J + 5, j * VL_J + 2)

                @blk.scalar
                def _(scalar):
                    scalar.wait_ge(S["phB"], 5)
                    for j in range(B2):
                        scalar.wait_ge(S["vx1"], j + 1)
                        ln_scalar(scalar, x1_3, j * SQ_J, j * TQL_J, j * TL_J,
                                  j * EL_J, j * VL_J)
                        for ic in range(NIC):
                            for ib in range(4):
                                scalar.wait_ge(S["vh"], j * VH_J + 4 * ic + ib)
                                scalar.wait_ge(S["tic"], j * TIC_J + 8 * ic + 2 * ib + 1)
                                scalar.activation(out=gsil[:, :], in_=psG[:, :],
                                                  func=AF.Silu,
                                                  scale=gs_sb[:, ic * 4 + ib:
                                                              ic * 4 + ib + 1])
                                scalar.wait_ge(S["tic"], j * TIC_J + 8 * ic + 2 * ib + 2)
                                scalar.activation(out=uscb[:, :], in_=psU[:, :],
                                                  func=AF.Copy,
                                                  scale=us_sb[:, ic * 4 + ib:
                                                              ic * 4 + ib + 1]
                                                  ).then_inc(S["egc"], 2)
                        for mc in range(4):
                            for mb in range(4):
                                idx = 4 * mc + mb
                                scalar.wait_ge(S["toc"], j * TOC_J + idx + 1)
                                if idx > 0:
                                    scalar.wait_ge(S["vp"], j * VP_J + idx)
                                pot = psG if idx % 2 == 0 else psU
                                scalar.activation(out=ffn_t[:, :], in_=pot[:, :],
                                                  func=AF.Copy,
                                                  scale=ds_sb[:, idx:idx + 1]
                                                  ).then_inc(S["edc"], 1)
                        scalar.wait_ge(S["vp"], (j + 1) * VP_J)
                        ln_scalar(scalar, pre2_3, j * SQ_J + 16, j * TQL_J + 16,
                                  j * TL_J + 2, j * EL_J + 5, j * VL_J + 2)

                @blk.gpsimd
                def _(gpsimd):
                    gpsimd.wait_ge(S["phB"], 5)
                    for j in range(B2):
                        gpsimd.wait_ge(S["vout"], j + 1)
                        gpsimd.dma_start(out=outy_v[j, :, :, :], in_=out3[:, :, :]
                                         ).then_inc(S["soc"], 16)
                    gpsimd.wait_ge(S["soc"], 16 * B2)

    return nc


def kernel(**inp):
    f32 = np.float32

    def pshape(a, nblk):
        x = np.asarray(a)
        rest = x.shape[1:]
        return np.ascontiguousarray(
            x.reshape(nblk, 128, *rest).transpose(1, 0, *range(2, 2 + len(rest)))
            .reshape(128, -1))

    hid_all = np.asarray(inp["L0_attn_hidden_states"])
    w_in = np.asarray(inp["L0_attn_W_in"])
    conv_w = np.asarray(inp["L0_attn_conv_w"])
    conv_b = np.asarray(inp["L0_attn_conv_b"])
    w_x = np.asarray(inp["L0_attn_W_x"])
    w_dt = np.asarray(inp["L0_attn_W_dt"])
    dt_bias = np.asarray(inp["L0_attn_dt_bias"])
    a_m = np.asarray(inp["L0_attn_A"])
    d_v = np.asarray(inp["L0_attn_D"])
    w_out = np.asarray(inp["L0_attn_W_out"])
    gate_w = np.asarray(inp["L0_ffn_gate_w"])
    up_w = np.asarray(inp["L0_ffn_up_w"])
    down_w = np.asarray(inp["L0_ffn_down_w"])
    gate_s = np.asarray(inp["L0_ffn_gate_s"])
    up_s = np.asarray(inp["L0_ffn_up_s"])
    down_s = np.asarray(inp["L0_ffn_down_s"])

    gateT_h = pshape(np.ascontiguousarray(gate_w.T), 16).astype(f32)
    upT_h = pshape(np.ascontiguousarray(up_w.T), 16).astype(f32)
    downT_h = pshape(np.ascontiguousarray(down_w.T), 44).astype(f32)
    gs_h = pshape(gate_s.reshape(-1, 1), 44).astype(f32)
    us_h = pshape(up_s.reshape(-1, 1), 44).astype(f32)
    ds_h = pshape(down_s.reshape(-1, 1), 16).astype(f32)

    in_maps = []
    for c in range(8):
        p, q = c // 4, c % 4
        dq = slice(1024 * q, 1024 * (q + 1))
        hid_fm = np.ascontiguousarray(hid_all[2 * p:2 * p + 2].transpose(0, 2, 1))
        hid_h = np.stack([pshape(hid_fm[jj], 16) for jj in range(2)])
        hidq_h = np.stack([pshape(np.ascontiguousarray(
            hid_fm[jj][:, 512 * q:512 * (q + 1)]), 16) for jj in range(2)])
        winT_q = np.ascontiguousarray(
            np.concatenate([w_in[dq], w_in[4096 + 1024 * q:4096 + 1024 * (q + 1)]], 0).T)
        im = {
            "hid": hid_h.astype(f32),
            "hidq": hidq_h.astype(f32),
            "winT": pshape(winT_q, 16).astype(f32),
            "wxT": pshape(np.ascontiguousarray(w_x[:, dq].T), 8).astype(f32),
            "wdtT": np.ascontiguousarray(w_dt[dq].T).astype(f32),
            "woutT": pshape(np.ascontiguousarray(w_out[:, dq].T), 8).astype(f32),
            "convw": pshape(conv_w[dq], 8).astype(f32),
            "convb": pshape(conv_b[dq].reshape(-1, 1), 8).astype(f32),
            "dtb": pshape(dt_bias[dq].reshape(-1, 1), 8).astype(f32),
            "a_mat": pshape(a_m[dq], 8).astype(f32),
            "d_vec": pshape(d_v[dq].reshape(-1, 1), 8).astype(f32),
            "gateT": gateT_h, "upT": upT_h, "downT": downT_h,
            "gs": gs_h, "us": us_h, "ds": ds_h,
        }
        in_maps.append(im)

    nc = build_nc()
    res = run_bass_kernel_spmd(nc, in_maps, core_ids=list(range(8)))

    out = np.empty((4, 2048, 2048), f32)
    for c in range(8):
        p, q = c // 4, c % 4
        oc = np.asarray(res.results[c]["outy"])
        for jj in range(2):
            om = oc[jj].reshape(128, 16, 512).transpose(1, 0, 2).reshape(2048, 512)
            out[2 * p + jj, 512 * q:512 * (q + 1), :] = om.T
    return out


# version 26
# speedup vs baseline: 1.0609x; 1.0085x over previous
import numpy as np
import concourse.bass as bass
import concourse.mybir as mybir
from concourse.bass_utils import run_bass_kernel_spmd
from contextlib import ExitStack

F32 = mybir.dt.float32
BF16 = mybir.dt.bfloat16
AF = mybir.ActivationFunctionType
ALU = mybir.AluOpType

P = 128
B2 = 2
LL = 2048
DM = 2048
NB = 8
NST = 16
LCA = 512
NCA = LL // LCA
LCB = 256
NCB = LL // LCB
TQ = 512
NIC = 11
EPS = 1e-5
DQ = 1024


def build_nc():
    nc = bass.Bass("TRN2", target_bir_lowering=False)

    hid = nc.dram_tensor("hid", [B2, P, 16 * LL], F32, kind="ExternalInput").ap()
    hidq = nc.dram_tensor("hidq", [B2, P, 16 * TQ], F32, kind="ExternalInput").ap()
    winT = nc.dram_tensor("winT", [P, 16 * DM], F32, kind="ExternalInput").ap()
    wxT = nc.dram_tensor("wxT", [P, NB * 160], F32, kind="ExternalInput").ap()
    wdtT = nc.dram_tensor("wdtT", [P, DQ], F32, kind="ExternalInput").ap()
    woutT = nc.dram_tensor("woutT", [P, NB * DM], F32, kind="ExternalInput").ap()
    convw = nc.dram_tensor("convw", [P, NB * 4], F32, kind="ExternalInput").ap()
    convb = nc.dram_tensor("convb", [P, NB], F32, kind="ExternalInput").ap()
    dtb = nc.dram_tensor("dtb", [P, NB], F32, kind="ExternalInput").ap()
    a_mat = nc.dram_tensor("a_mat", [P, NB * NST], F32, kind="ExternalInput").ap()
    d_vec = nc.dram_tensor("d_vec", [P, NB], F32, kind="ExternalInput").ap()
    gateT = nc.dram_tensor("gateT", [P, 16 * 5632], F32, kind="ExternalInput").ap()
    upT = nc.dram_tensor("upT", [P, 16 * 5632], F32, kind="ExternalInput").ap()
    downT = nc.dram_tensor("downT", [P, 44 * DM], F32, kind="ExternalInput").ap()
    gs = nc.dram_tensor("gs", [P, 44], F32, kind="ExternalInput").ap()
    us = nc.dram_tensor("us", [P, 44], F32, kind="ExternalInput").ap()
    ds = nc.dram_tensor("ds", [P, 16], F32, kind="ExternalInput").ap()

    outy = nc.dram_tensor("outy", [B2, P, 16 * TQ], F32, kind="ExternalOutput").ap()

    xc_buf = nc.dram_tensor("xc_buf", [B2 * P, NB * LL], BF16)
    z_buf = nc.dram_tensor("z_buf", [B2 * P, NB * LL], BF16)
    ssm_p = [nc.dram_tensor(f"ssm_p{j}", [160, LL], F32) for j in range(B2)]
    ssm_r = [nc.dram_tensor(f"ssm_r{j}", [160, LL], F32) for j in range(B2)]
    outp = [nc.dram_tensor(f"outp{j}", [4, DM * TQ], F32) for j in range(B2)]
    outr = [nc.dram_tensor(f"outr{j}", [1, DM * TQ], F32) for j in range(B2)]

    GROUPS = [[0, 1, 2, 3], [4, 5, 6, 7]]
    NT_A = B2 * NCA
    NT_B = B2 * NCB

    with ExitStack() as ctx0:
        a_sb = ctx0.enter_context(nc.sbuf_tensor("a_sb", [P, NB * NST], F32))
        convw_sb = ctx0.enter_context(nc.sbuf_tensor("convw_sb", [P, NB * 4], F32))
        convb_sb = ctx0.enter_context(nc.sbuf_tensor("convb_sb", [P, NB], F32))
        dtb_sb = ctx0.enter_context(nc.sbuf_tensor("dtb_sb", [P, NB], F32))
        d_sb = ctx0.enter_context(nc.sbuf_tensor("d_sb", [P, NB], F32))
        gs_sb = ctx0.enter_context(nc.sbuf_tensor("gs_sb", [P, 44], F32))
        us_sb = ctx0.enter_context(nc.sbuf_tensor("us_sb", [P, 44], F32))
        ds_sb = ctx0.enter_context(nc.sbuf_tensor("ds_sb", [P, 16], F32))
        wxT_sb = ctx0.enter_context(nc.sbuf_tensor("wxT_sb", [P, NB * 160], BF16))
        wdtT_sb = ctx0.enter_context(nc.sbuf_tensor("wdtT_sb", [P, DQ], BF16))
        ones_c = ctx0.enter_context(nc.sbuf_tensor("ones_c", [P, 1], F32))
        ones_r = ctx0.enter_context(nc.sbuf_tensor("ones_r", [1, P], F32))
        carry = ctx0.enter_context(nc.sbuf_tensor("carry", [P, NST * NB], F32))
        halo = ctx0.enter_context(nc.sbuf_tensor("halo", [P, NB * 3], BF16))
        tmp8 = ctx0.enter_context(nc.sbuf_tensor("tmp8", [P, NB], F32))
        epsv = ctx0.enter_context(nc.sbuf_tensor("epsv", [P, 1], F32))

        S = {}
        for nm in ["wl", "vw", "ha", "va", "mmA", "mmS", "evA", "evS", "cvA",
                   "siA", "stA", "ccS", "phA",
                   "wb", "vwb", "hb", "vb", "mb_", "eb_", "spb", "vn", "sn",
                   "vg", "mo", "eo", "sb_", "ccO", "phB",
                   "hc", "vx1", "sqc", "tql", "tL", "eL", "vL", "vln",
                   "vgc", "tic", "egc", "vh", "vdc", "toc", "edc", "vp",
                   "vout", "soc", "vsf", "vym", "pya"]:
            S[nm] = ctx0.enter_context(nc.semaphore(nm))

        a3 = a_sb[:, :].rearrange("p (b n) -> p b n", b=NB)
        hid_v = hid.rearrange("j p (b l) -> j p b l", b=16)
        xcb_v = xc_buf.ap().rearrange("(j p) (b l) -> j p b l", j=B2, b=NB)
        zb_v = z_buf.ap().rearrange("(j p) (b l) -> j p b l", j=B2, b=NB)
        outp_v = [o.ap().rearrange("q (mb p l) -> q p mb l", mb=16, p=P) for o in outp]
        outr_v = [o.ap().rearrange("o (mb p l) -> o p mb l", mb=16, p=P) for o in outr]

        with ExitStack() as ctxA:
            winT_sb = ctxA.enter_context(nc.sbuf_tensor("winT_sb", [P, 16 * DM], BF16))
            hidf = ctxA.enter_context(nc.sbuf_tensor("hidf", [P, 16 * LCA], F32))
            hidb = ctxA.enter_context(nc.sbuf_tensor("hidb", [P, 16 * LCA], BF16))
            xpre = ctxA.enter_context(nc.sbuf_tensor("xpre", [P, NB * (LCA + 3)], BF16))
            xc_t = ctxA.enter_context(nc.sbuf_tensor("xc_t", [P, NB * LCA], BF16))
            zs_t = ctxA.enter_context(nc.sbuf_tensor("zs_t", [P, NB * LCA], BF16))
            accv = ctxA.enter_context(nc.sbuf_tensor("accv", [P, NB * LCA], F32))
            sst0 = ctxA.enter_context(nc.sbuf_tensor("sst0", [P, LCA], F32))
            sst1 = ctxA.enter_context(nc.sbuf_tensor("sst1", [32, LCA], F32))
            pa0 = ctxA.enter_context(nc.psum_tensor("pa0", [P, LCA], F32))
            pa1 = ctxA.enter_context(nc.psum_tensor("pa1", [P, LCA], F32))
            psS0 = ctxA.enter_context(nc.psum_tensor("psS0", [P, LCA], F32))
            psS1 = ctxA.enter_context(nc.psum_tensor("psS1", [32, LCA], F32))
            pa = [pa0, pa1]

            xpre3 = xpre[:, :].rearrange("p (b l) -> p b l", b=NB)
            hidf3 = hidf[:, :].rearrange("p (b l) -> p b l", b=16)
            hidb3 = hidb[:, :].rearrange("p (b l) -> p b l", b=16)
            xc3 = xc_t[:, :].rearrange("p (b l) -> p b l", b=NB)
            zs3 = zs_t[:, :].rearrange("p (b l) -> p b l", b=NB)
            acc3 = accv[:, :].rearrange("p (b l) -> p b l", b=NB)
            winT3 = winT_sb[:, :].rearrange("p (b e) -> p b e", b=16)
            wxT3 = wxT_sb[:, :].rearrange("p (b e) -> p b e", b=NB)
            halo3 = halo[:, :].rearrange("p (b k) -> p b k", b=NB)
            winT_v = winT.rearrange("p (b e) -> p b e", b=16)

            with nc.Block() as blk:
                @blk.sync
                def _(sync):
                    for dst, src in [(a_sb, a_mat), (convw_sb, convw), (convb_sb, convb),
                                     (dtb_sb, dtb), (d_sb, d_vec), (gs_sb, gs),
                                     (us_sb, us), (ds_sb, ds)]:
                        sync.dma_start(out=dst[:, :], in_=src).then_inc(S["wl"], 16)
                    sync.dma_start(out=hidf[:, 0:DQ], in_=wdtT).then_inc(S["wl"], 16)
                    sync.wait_ge(S["vw"], 1)
                    sync.dma_start(out=hidf[:, 0:NB * 160], in_=wxT).then_inc(S["wl"], 16)
                    for wc in range(4):
                        sync.wait_ge(S["vw"], 2 + wc)
                        sync.dma_start(out=hidf3[:, :, 0:512],
                                       in_=winT_v[:, :, wc * 512:(wc + 1) * 512]
                                       ).then_inc(S["wl"], 16)
                    for t in range(NT_A):
                        j, Ci = t // NCA, t % NCA
                        if t > 0:
                            sync.wait_ge(S["mmA"], 16 * t)
                        else:
                            sync.wait_ge(S["vw"], 6)
                        sync.dma_start(out=hidf3[:, :, :],
                                       in_=hid_v[j, :, :, Ci * LCA:(Ci + 1) * LCA]
                                       ).then_inc(S["ha"], 16)
                    sync.wait_ge(S["stA"], 64 * NT_A)
                    sync.sem_inc(S["phA"], 1)

                @blk.vector
                def _(vector):
                    vector.memset(ones_c[:, :], 1.0)
                    vector.memset(ones_r[:, :], 1.0)
                    vector.memset(epsv[:, :], EPS)
                    vector.wait_ge(S["wl"], 144)
                    vector.tensor_copy(wdtT_sb[:, :], hidf[:, 0:DQ]).then_inc(S["vw"], 1)
                    vector.wait_ge(S["wl"], 160)
                    vector.tensor_copy(wxT_sb[:, :], hidf[:, 0:NB * 160]).then_inc(S["vw"], 1)
                    for wc in range(4):
                        vector.wait_ge(S["wl"], 176 + 16 * wc)
                        vector.tensor_copy(winT3[:, :, wc * 512:(wc + 1) * 512],
                                           hidf3[:, :, 0:512]).then_inc(S["vw"], 1)
                    for t in range(NT_A):
                        j, Ci = t // NCA, t % NCA
                        vector.wait_ge(S["ha"], 16 * (t + 1))
                        vector.tensor_copy(hidb[:, :], hidf[:, :]).then_inc(S["va"], 1)
                        if Ci == 0:
                            vector.memset(halo[:, :], 0.0)
                        vector.wait_ge(S["evA"], 16 * t + 8)
                        vector.tensor_copy(xpre3[:, :, 0:3], halo3[:, :, :])
                        for db in range(NB):
                            vector.tensor_scalar(
                                out=acc3[:, db, :], in0=xpre3[:, db, 0:LCA],
                                scalar1=convw_sb[:, 4 * db:4 * db + 1],
                                scalar2=convb_sb[:, db:db + 1],
                                op0=ALU.mult, op1=ALU.add)
                        for k in range(1, 4):
                            for db in range(NB):
                                vector.scalar_tensor_tensor(
                                    out=acc3[:, db, :], in0=xpre3[:, db, k:k + LCA],
                                    scalar=convw_sb[:, 4 * db + k:4 * db + k + 1],
                                    in1=acc3[:, db, :], op0=ALU.mult, op1=ALU.add)
                        vector.tensor_copy(halo3[:, :, :], xpre3[:, :, LCA:LCA + 3]
                                           ).then_inc(S["cvA"], 1)
                    vector.wait_ge(S["stA"], 64 * NT_A)
                    vector.sem_inc(S["phA"], 1)

                @blk.tensor
                def _(tensor):
                    for t in range(NT_A):
                        tensor.wait_ge(S["va"], t + 1)
                        tensor.wait_ge(S["evA"], 16 * t)
                        for eb in range(16):
                            if eb >= 2:
                                tensor.wait_ge(S["evA"], 16 * t + eb - 1)
                            for kb in range(16):
                                mmi = tensor.matmul(
                                    pa[eb % 2][:, :],
                                    winT3[:, kb, eb * 128:(eb + 1) * 128],
                                    hidb3[:, kb, :],
                                    start=(kb == 0), stop=(kb == 15))
                                if kb == 15:
                                    mmi.then_inc(S["mmA"], 1)
                        tensor.wait_ge(S["siA"], t + 1)
                        tensor.wait_ge(S["evS"], 2 * t)
                        for kb in range(NB):
                            m0 = tensor.matmul(psS0[:, :], wxT3[:, kb, 0:128],
                                               xc3[:, kb, :], start=(kb == 0), stop=(kb == 7))
                            m1 = tensor.matmul(psS1[:, :], wxT3[:, kb, 128:160],
                                               xc3[:, kb, :], start=(kb == 0), stop=(kb == 7))
                            if kb == 7:
                                m0.then_inc(S["mmS"], 1)
                                m1.then_inc(S["mmS"], 1)
                    tensor.wait_ge(S["stA"], 64 * NT_A)
                    tensor.sem_inc(S["phA"], 1)

                @blk.scalar
                def _(scalar):
                    for t in range(NT_A):
                        scalar.wait_ge(S["cvA"], t)
                        scalar.wait_ge(S["stA"], 64 * t)
                        for eb in range(16):
                            scalar.wait_ge(S["mmA"], 16 * t + eb + 1)
                            if eb < 8:
                                scalar.activation(out=xpre3[:, eb, 3:LCA + 3],
                                                  in_=pa[eb % 2][:, :], func=AF.Copy
                                                  ).then_inc(S["evA"], 1)
                            else:
                                scalar.activation(out=zs3[:, eb - 8, :],
                                                  in_=pa[eb % 2][:, :], func=AF.Silu
                                                  ).then_inc(S["evA"], 1)
                        scalar.wait_ge(S["cvA"], t + 1)
                        scalar.activation(out=xc_t[:, :], in_=accv[:, :], func=AF.Silu
                                          ).then_inc(S["siA"], 1)
                        scalar.wait_ge(S["mmS"], 2 * t + 2)
                        scalar.copy(out=sst0[:, :], in_=psS0[:, :]).then_inc(S["evS"], 1)
                        scalar.copy(out=sst1[:, :], in_=psS1[:, :]).then_inc(S["evS"], 1)
                    scalar.wait_ge(S["stA"], 64 * NT_A)
                    scalar.sem_inc(S["phA"], 1)

                @blk.gpsimd
                def _(gpsimd):
                    for t in range(NT_A):
                        j, Ci = t // NCA, t % NCA
                        ls = slice(Ci * LCA, (Ci + 1) * LCA)
                        gpsimd.wait_ge(S["siA"], t + 1)
                        gpsimd.dma_start(out=xcb_v[j, :, :, ls], in_=xc3[:, :, :]
                                         ).then_inc(S["stA"], 16)
                        gpsimd.wait_ge(S["evA"], 16 * (t + 1))
                        gpsimd.dma_start(out=zb_v[j, :, :, ls], in_=zs3[:, :, :]
                                         ).then_inc(S["stA"], 16)
                        gpsimd.wait_ge(S["evS"], 2 * t + 1)
                        gpsimd.dma_start(out=ssm_p[j].ap()[0:128, ls],
                                         in_=sst0[:, :]).then_inc(S["stA"], 16)
                        gpsimd.wait_ge(S["evS"], 2 * t + 2)
                        gpsimd.dma_start(out=ssm_p[j].ap()[128:160, ls],
                                         in_=sst1[:, :]).then_inc(S["stA"], 16)
                        if Ci == NCA - 1:
                            gpsimd.wait_ge(S["stA"], 64 * (t + 1))
                            gpsimd.collective_compute(
                                "AllReduce", ALU.add, replica_groups=GROUPS,
                                ins=[ssm_p[j].ap().opt()], outs=[ssm_r[j].ap().opt()],
                            ).then_inc(S["ccS"], 1)
                    gpsimd.wait_ge(S["stA"], 64 * NT_A)
                    gpsimd.sem_inc(S["phA"], 1)

        with ExitStack() as ctxB:
            woutT_sb = ctxB.enter_context(nc.sbuf_tensor("woutT_sb", [P, NB * DM], BF16))
            wstageB = ctxB.enter_context(nc.sbuf_tensor("wstageB", [P, NB * 512], F32))
            dtr_f = [ctxB.enter_context(nc.sbuf_tensor(f"dtr_f{i}", [P, LCB], F32))
                     for i in range(2)]
            dtr_b = [ctxB.enter_context(nc.sbuf_tensor(f"dtr_b{i}", [P, LCB], BF16))
                     for i in range(2)]
            b_rep = [ctxB.enter_context(nc.sbuf_tensor(f"b_rep{i}", [P, NST * LCB], F32))
                     for i in range(2)]
            c_rep = [ctxB.enter_context(nc.sbuf_tensor(f"c_rep{i}", [P, NST * LCB], F32))
                     for i in range(2)]

            delta = ctxB.enter_context(nc.sbuf_tensor("delta", [P, NB * LCB], F32))
            at = ctxB.enter_context(nc.sbuf_tensor("at", [P, NB * LCB], F32))
            bt = ctxB.enter_context(nc.sbuf_tensor("bt", [P, NB * LCB], F32))
            st_ = ctxB.enter_context(nc.sbuf_tensor("st_", [P, NB * LCB], F32))
            du_b = ctxB.enter_context(nc.sbuf_tensor("du_b", [P, NB * LCB], BF16))
            yac = ctxB.enter_context(nc.sbuf_tensor("yac", [P, NB * LCB], F32))
            xcB = [ctxB.enter_context(nc.sbuf_tensor(f"xcB{i}", [P, NB * LCB], BF16))
                   for i in range(2)]
            zsB = [ctxB.enter_context(nc.sbuf_tensor(f"zsB{i}", [P, NB * LCB], BF16))
                   for i in range(2)]
            yg = ctxB.enter_context(nc.sbuf_tensor("yg", [P, NB * LCB], BF16))
            wo_st = ctxB.enter_context(nc.sbuf_tensor("wo_st", [P, 16 * LCB], F32))
            pd0 = ctxB.enter_context(nc.psum_tensor("pd0", [P, LCB], F32))
            pd1 = ctxB.enter_context(nc.psum_tensor("pd1", [P, LCB], F32))
            po0 = ctxB.enter_context(nc.psum_tensor("po0", [P, LCB], F32))
            po1 = ctxB.enter_context(nc.psum_tensor("po1", [P, LCB], F32))
            pd = [pd0, pd1]
            po = [po0, po1]

            woutT3 = woutT_sb[:, :].rearrange("p (b m) -> p b m", b=NB)
            wstage3 = wstageB[:, :].rearrange("p (b m) -> p b m", b=NB)
            woutT_v = woutT.rearrange("p (b m) -> p b m", b=NB)
            pass
            at2f = wstageB[:, 0:NB * LCB]
            ytmp = bt[:, :]
            delta = [delta[:, :], wstageB[:, NB * LCB:2 * NB * LCB]]
            delta3 = [d.rearrange("p (b l) -> p b l", b=NB) for d in delta]
            atv = [at[:, :], at2f]
            atv3 = [v.rearrange("p (b l) -> p b l", b=NB) for v in atv]
            at3 = atv3[0]
            bt3 = bt[:, :].rearrange("p (b l) -> p b l", b=NB)
            st3 = st_[:, :].rearrange("p (b l) -> p b l", b=NB)
            du3 = du_b[:, :].rearrange("p (b l) -> p b l", b=NB)
            yac3 = yac[:, :].rearrange("p (b l) -> p b l", b=NB)
            xcB3 = [x[:, :].rearrange("p (b l) -> p b l", b=NB) for x in xcB]
            zsB3 = [x[:, :].rearrange("p (b l) -> p b l", b=NB) for x in zsB]
            yg3 = yg[:, :].rearrange("p (b l) -> p b l", b=NB)
            wo3 = wo_st[:, :].rearrange("p (b l) -> p b l", b=16)

            with nc.Block() as blk:
                @blk.sync
                def _(sync):
                    sync.wait_ge(S["phA"], 5)
                    for wc in range(4):
                        if wc > 0:
                            sync.wait_ge(S["vwb"], wc)
                        sync.dma_start(out=wstage3[:, :, 0:512],
                                       in_=woutT_v[:, :, wc * 512:(wc + 1) * 512]
                                       ).then_inc(S["wb"], 16)
                    for t in range(NT_B):
                        j, Ci = t // NCB, t % NCB
                        pt = t % 2
                        ls = slice(Ci * LCB, (Ci + 1) * LCB)
                        if Ci == 0:
                            sync.wait_ge(S["ccS"], j + 1)
                        if t >= 2:
                            sync.wait_ge(S["vg"], t - 1)
                        sync.dma_start(out=xcB3[pt][:, :, :], in_=xcb_v[j, :, :, ls]
                                       ).then_inc(S["hb"], 16)
                        sync.dma_start(out=zsB3[pt][:, :, :], in_=zb_v[j, :, :, ls]
                                       ).then_inc(S["hb"], 16)
                        sync.dma_start(out=dtr_f[pt][:, :],
                                       in_=ssm_r[j].ap()[0:128, ls]
                                       ).then_inc(S["hb"], 16)
                        sync.dma_start(
                            out=b_rep[pt][:, :].rearrange("p (n l) -> p n l", n=NST),
                            in_=ssm_r[j].ap()[128:144, ls
                                              ].unsqueeze(0).broadcast_to([P, NST, LCB])
                        ).then_inc(S["hb"], 16)
                        sync.dma_start(
                            out=c_rep[pt][:, :].rearrange("p (n l) -> p n l", n=NST),
                            in_=ssm_r[j].ap()[144:160, ls
                                              ].unsqueeze(0).broadcast_to([P, NST, LCB])
                        ).then_inc(S["hb"], 16)
                    sync.wait_ge(S["sb_"], 16 * NT_B)
                    sync.sem_inc(S["phB"], 1)

                @blk.vector
                def _(vector):
                    vector.wait_ge(S["phA"], 5)
                    for wc in range(4):
                        vector.wait_ge(S["wb"], 16 * (wc + 1))
                        vector.tensor_copy(woutT3[:, :, wc * 512:(wc + 1) * 512],
                                           wstage3[:, :, 0:512]).then_inc(S["vwb"], 1)
                    for t in range(NT_B):
                        j, Ci = t // NCB, t % NCB
                        pt = t % 2
                        dp = delta[pt]
                        if t == 0:
                            vector.wait_ge(S["hb"], 48)
                            vector.tensor_copy(dtr_b[0][:, :], dtr_f[0][:, :]
                                               ).then_inc(S["vb"], 1)
                        vector.wait_ge(S["hb"], 80 * (t + 1))
                        if Ci == 0:
                            vector.memset(carry[:, :], 0.0)
                        vector.wait_ge(S["spb"], t + 1)
                        vector.tensor_add(dp, dp, at[:, :]).then_inc(S["vn"], 1)
                        vector.tensor_tensor(out=du_b[:, :], in0=dp,
                                             in1=xcB[pt][:, :], op=ALU.mult)
                        if t + 1 < NT_B:
                            vector.wait_ge(S["hb"], 80 * (t + 1) + 48)
                            vector.tensor_copy(dtr_b[(t + 1) % 2][:, :],
                                               dtr_f[(t + 1) % 2][:, :]
                                               ).then_inc(S["vb"], 1)
                        for n in range(NST):
                            ap3 = atv3[n % 2]
                            vector.wait_ge(S["sn"], NST * t + n + 1)
                            b_bc = b_rep[pt][:, n * LCB:(n + 1) * LCB].unsqueeze(1
                                        ).broadcast_to([P, NB, LCB])
                            vector.tensor_tensor(out=bt3, in0=du3, in1=b_bc, op=ALU.mult)
                            cu = carry[:, n * NB:(n + 1) * NB].unsqueeze(2)
                            for db in range(NB):
                                sci = vector.tensor_tensor_scan(
                                    out=st3[:, db, :], data0=ap3[:, db, :],
                                    data1=bt3[:, db, :],
                                    initial=carry[:, n * NB + db:n * NB + db + 1],
                                    op0=ALU.mult, op1=ALU.add)
                                if db == NB - 1:
                                    sci.then_inc(S["vsf"], 1)
                            vector.wait_ge(S["vsf"], NST * t + n + 1)
                            vector.tensor_copy(cu, st3[:, :, LCB - 1:LCB])
                            c_bc = c_rep[pt][:, n * LCB:(n + 1) * LCB].unsqueeze(1
                                        ).broadcast_to([P, NB, LCB])
                            if n == 0:
                                vector.tensor_tensor(out=yac3, in0=st3, in1=c_bc,
                                                     op=ALU.mult)
                            else:
                                vector.tensor_tensor(
                                    out=ytmp[:, :].rearrange("p (b l) -> p b l", b=NB),
                                    in0=st3, in1=c_bc, op=ALU.mult)
                                vector.tensor_add(yac[:, :], yac[:, :], ytmp[:, :])
                        for db in range(NB):
                            vector.scalar_tensor_tensor(
                                out=yac3[:, db, :], in0=xcB3[pt][:, db, :],
                                scalar=d_sb[:, db:db + 1], in1=yac3[:, db, :],
                                op0=ALU.mult, op1=ALU.add)
                        vector.wait_ge(S["mo"], 16 * t)
                        vector.tensor_tensor(out=yg[:, :], in0=yac[:, :],
                                             in1=zsB[pt][:, :],
                                             op=ALU.mult).then_inc(S["vg"], 1)
                    vector.wait_ge(S["sb_"], 16 * NT_B)
                    vector.sem_inc(S["phB"], 1)

                @blk.tensor
                def _(tensor):
                    tensor.wait_ge(S["phA"], 5)

                    def delta_mm(tensor, k):
                        tensor.wait_ge(S["vb"], k + 1)
                        tensor.wait_ge(S["eb_"], 8 * k)
                        for db in range(NB):
                            if db >= 2:
                                tensor.wait_ge(S["eb_"], 8 * k + db - 1)
                            tensor.matmul(pd[db % 2][:, :],
                                          wdtT_sb[:, db * 128:(db + 1) * 128],
                                          dtr_b[k % 2][:, :], start=True, stop=True
                                          ).then_inc(S["mb_"], 1)

                    delta_mm(tensor, 0)
                    for t in range(NT_B):
                        if t + 1 < NT_B:
                            delta_mm(tensor, t + 1)
                        tensor.wait_ge(S["vg"], t + 1)
                        tensor.wait_ge(S["eo"], 16 * t)
                        if t == 0:
                            tensor.wait_ge(S["vwb"], 4)
                        for mb in range(16):
                            if mb >= 2:
                                tensor.wait_ge(S["eo"], 16 * t + mb - 1)
                            for kb in range(NB):
                                mmi = tensor.matmul(po[mb % 2][:, :],
                                                    woutT3[:, kb, mb * 128:(mb + 1) * 128],
                                                    yg3[:, kb, :],
                                                    start=(kb == 0), stop=(kb == 7))
                                if kb == 7:
                                    mmi.then_inc(S["mo"], 1)
                    tensor.wait_ge(S["sb_"], 16 * NT_B)
                    tensor.sem_inc(S["phB"], 1)

                @blk.scalar
                def _(scalar):
                    scalar.wait_ge(S["phA"], 5)

                    def softplus(scalar, k):
                        if k > 0:
                            scalar.wait_ge(S["vsf"], NST * (k - 1) + 15)
                        dk3 = delta3[k % 2]
                        for db in range(NB):
                            scalar.wait_ge(S["mb_"], 8 * k + db + 1)
                            scalar.activation(out=at3[:, db, :], in_=pd[db % 2][:, :],
                                              func=AF.Abs, bias=dtb_sb[:, db:db + 1])
                            scalar.activation(out=dk3[:, db, :], in_=pd[db % 2][:, :],
                                              func=AF.Relu, bias=dtb_sb[:, db:db + 1]
                                              ).then_inc(S["eb_"], 1)
                        scalar.activation(out=at[:, :], in_=at[:, :], func=AF.Exp,
                                          scale=-1.0)
                        scalar.activation(out=at[:, :], in_=at[:, :], func=AF.Ln,
                                          bias=1.0).then_inc(S["spb"], 1)

                    softplus(scalar, 0)
                    for t in range(NT_B):
                        dp3s = delta3[t % 2]
                        scalar.wait_ge(S["vn"], t + 1)
                        for n in range(NST):
                            if n >= 2:
                                scalar.wait_ge(S["vsf"], NST * t + n - 1)
                            for db in range(NB):
                                ai = scalar.activation(
                                    out=atv3[n % 2][:, db, :], in_=dp3s[:, db, :],
                                    func=AF.Exp,
                                    scale=a_sb[:, db * NST + n:db * NST + n + 1])
                                if db == NB - 1:
                                    ai.then_inc(S["sn"], 1)
                        if t + 1 < NT_B:
                            softplus(scalar, t + 1)
                        scalar.wait_ge(S["sb_"], 16 * t)
                        for mb in range(16):
                            scalar.wait_ge(S["mo"], 16 * t + mb + 1)
                            scalar.copy(out=wo3[:, mb, :], in_=po[mb % 2][:, :]
                                        ).then_inc(S["eo"], 1)
                    scalar.wait_ge(S["sb_"], 16 * NT_B)
                    scalar.sem_inc(S["phB"], 1)

                @blk.gpsimd
                def _(gpsimd):
                    gpsimd.wait_ge(S["phA"], 5)
                    for t in range(NT_B):
                        j, Ci = t // NCB, t % NCB
                        qc, oc = Ci // 2, (Ci % 2) * LCB
                        gpsimd.wait_ge(S["eo"], 16 * (t + 1))
                        gpsimd.dma_start(out=outp_v[j][qc][:, :, oc:oc + LCB],
                                         in_=wo3[:, :, :]).then_inc(S["sb_"], 16)
                    gpsimd.wait_ge(S["sb_"], 16 * NT_B)
                    gpsimd.sem_inc(S["phB"], 1)
                    for j in range(B2):
                        gpsimd.collective_compute(
                            "ReduceScatter", ALU.add, replica_groups=GROUPS,
                            ins=[outp[j].ap().opt()], outs=[outr[j].ap().opt()],
                        ).then_inc(S["ccO"], 1)

        with ExitStack() as ctxC:
            x1 = ctxC.enter_context(nc.sbuf_tensor("x1", [P, 16 * TQ], F32))
            bufA = ctxC.enter_context(nc.sbuf_tensor("bufA", [P, 16 * TQ], F32))
            bufBF = ctxC.enter_context(nc.sbuf_tensor("bufBF", [P, 24576], BF16))
            h_all = ctxC.enter_context(nc.sbuf_tensor("h_all", [P, 44 * TQ], BF16))
            pre2 = ctxC.enter_context(nc.sbuf_tensor("pre2", [P, 16 * TQ], F32))
            sq1 = ctxC.enter_context(nc.sbuf_tensor("sq1", [P, TQ], F32))
            rows = ctxC.enter_context(nc.sbuf_tensor("rows", [1, 2 * TQ], F32))
            mu_bc = ctxC.enter_context(nc.sbuf_tensor("mu_bc", [P, TQ], F32))
            rs_bc = ctxC.enter_context(nc.sbuf_tensor("rs_bc", [P, TQ], F32))
            gsil = ctxC.enter_context(nc.sbuf_tensor("gsil", [P, TQ], BF16))
            uscb = ctxC.enter_context(nc.sbuf_tensor("uscb", [P, TQ], BF16))
            ffn_t = sq1
            psG = ctxC.enter_context(nc.psum_tensor("psG", [P, TQ], F32))
            psU = ctxC.enter_context(nc.psum_tensor("psU", [P, TQ], F32))
            psG2 = ctxC.enter_context(nc.psum_tensor("psG2", [P, TQ], F32))
            psU2 = ctxC.enter_context(nc.psum_tensor("psU2", [P, TQ], F32))
            psL0 = ctxC.enter_context(nc.psum_tensor("psL0", [1, TQ], F32))
            psL1 = ctxC.enter_context(nc.psum_tensor("psL1", [1, TQ], F32))

            x1_3 = x1[:, :].rearrange("p (b l) -> p b l", b=16)
            bufA3 = bufA[:, :].rearrange("p (b l) -> p b l", b=16)
            pre2_3 = pre2[:, :].rearrange("p (b l) -> p b l", b=16)
            dst11 = bufA[:, 0:11 * 512].rearrange("p (g m) -> p g m", g=11)
            gbf = bufBF[:, 0:8192].rearrange("p (b i) -> p b i", b=16)
            ubf = bufBF[:, 8192:16384].rearrange("p (b i) -> p b i", b=16)
            xlnb = bufBF[:, 16384:24576].rearrange("p (b l) -> p b l", b=16)
            dbf = bufBF[:, 0:22528].rearrange("p (b m) -> p b m", b=44)
            h3 = h_all[:, :].rearrange("p (b l) -> p b l", b=44)
            out3 = bufA3
            gateT_v = gateT.rearrange("p (b i) -> p b i", b=16)
            upT_v = upT.rearrange("p (b i) -> p b i", b=16)
            downT_v = downT.rearrange("p (b m) -> p b m", b=44)
            hidq_v = hidq.rearrange("j p (b l) -> j p b l", b=16)
            outy_v = outy.rearrange("j p (b l) -> j p b l", b=16)
            r_m, r_s = rows[:, 0:TQ], rows[:, TQ:2 * TQ]
            r_mu, r_v = mu_bc[0:1, :], rs_bc[0:1, :]

            SQ_J = 32
            TQL_J = 32
            TL_J = 4
            EL_J = 10
            VL_J = 4
            HC_J = 32 + NIC * 128 + 16 * 16
            VGC_J = 8 * NIC
            TIC_J = 8 * NIC
            EGC_J = 8 * NIC
            VH_J = 4 * NIC
            VDC_J = 16
            TOC_J = 16
            EDC_J = 16
            VP_J = 16

            def ln_scalar(scalar, xt3, b_sq, b_tql, b_tL, b_eL, b_vL):
                for kb in range(16):
                    scalar.wait_ge(S["tql"], b_tql + kb)
                    scalar.activation(out=sq1[:, :], in_=xt3[:, kb, :],
                                      func=AF.Square).then_inc(S["sqc"], 1)
                scalar.wait_ge(S["tql"], b_tql + 16)
                scalar.copy(out=r_m, in_=psL0[:, :]).then_inc(S["eL"], 1)
                scalar.copy(out=r_s, in_=psL1[:, :]).then_inc(S["eL"], 1)
                scalar.wait_ge(S["vL"], b_vL + 1)
                scalar.activation(out=r_v, in_=r_v, func=AF.Sqrt,
                                  bias=epsv[0:1, 0:1]).then_inc(S["eL"], 1)
                scalar.wait_ge(S["tL"], b_tL + 2)
                scalar.copy(out=mu_bc[:, :], in_=psG[:, :]).then_inc(S["eL"], 1)
                scalar.copy(out=rs_bc[:, :], in_=psU[:, :]).then_inc(S["eL"], 1)

            def ln_tensor(tensor, xt3, b_sq, b_tql, b_tL, b_eL, b_vL):
                for kb in range(16):
                    tensor.matmul(psL0[:, :], ones_c[:, 0:1], xt3[:, kb, :],
                                  start=(kb == 0), stop=(kb == 15))
                for kb in range(16):
                    tensor.wait_ge(S["sqc"], b_sq + kb + 1)
                    tensor.matmul(psL1[:, :], ones_c[:, 0:1], sq1[:, :],
                                  start=(kb == 0), stop=(kb == 15)
                                  ).then_inc(S["tql"], 1)
                tensor.wait_ge(S["vL"], b_vL + 2)
                tensor.matmul(psG[:, :], ones_r[0:1, :], r_mu, start=True, stop=True
                              ).then_inc(S["tL"], 1)
                tensor.matmul(psU[:, :], ones_r[0:1, :], r_v, start=True, stop=True
                              ).then_inc(S["tL"], 1)

            def ln_vector(vector, xt3, outt3, b_sq, b_tql, b_tL, b_eL, b_vL,
                          inc=None):
                vector.wait_ge(S["eL"], b_eL + 2)
                vector.tensor_scalar_mul(r_mu, r_m, 1.0 / DM)
                vector.tensor_tensor(out=r_v, in0=r_mu, in1=r_mu, op=ALU.mult)
                vector.scalar_tensor_tensor(out=r_v, in0=r_s, scalar=1.0 / DM,
                                            in1=r_v, op0=ALU.mult,
                                            op1=ALU.subtract).then_inc(S["vL"], 1)
                vector.wait_ge(S["eL"], b_eL + 3)
                vector.reciprocal(r_v, r_v).then_inc(S["vL"], 1)
                vector.wait_ge(S["eL"], b_eL + 5)
                mu16 = mu_bc[:, :].unsqueeze(1).broadcast_to([P, 16, TQ])
                rs16 = rs_bc[:, :].unsqueeze(1).broadcast_to([P, 16, TQ])
                vector.tensor_tensor(out=outt3, in0=xt3, in1=mu16, op=ALU.subtract)
                ti = vector.tensor_tensor(out=outt3, in0=outt3, in1=rs16, op=ALU.mult)
                if inc is not None:
                    ti.then_inc(*inc)

            with nc.Block() as blk:
                @blk.sync
                def _(sync):
                    sync.wait_ge(S["phB"], 5)
                    for j in range(B2):
                        hcb = j * HC_J
                        sync.wait_ge(S["ccO"], j + 1)
                        if j > 0:
                            sync.wait_ge(S["vout"], j)
                            sync.wait_ge(S["soc"], 16 * j)
                        sync.dma_start(out=x1_3[:, :, :], in_=hidq_v[j, :, :, :]
                                       ).then_inc(S["hc"], 16)
                        sync.dma_start(out=bufA3[:, :, :], in_=outr_v[j][0]
                                       ).then_inc(S["hc"], 16)
                        for ic in range(NIC):
                            if ic == 0:
                                sync.wait_ge(S["vx1"], j + 1)
                            for gi in range(8):
                                wsrc = gateT_v if gi < 4 else upT_v
                                sub = gi % 4
                                kbs = slice(4 * sub, 4 * sub + 4)
                                slot = gi % 4
                                g_all = 8 * ic + gi
                                if g_all >= 4:
                                    sync.wait_ge(S["vgc"], j * VGC_J + g_all - 3)
                                sync.dma_start(
                                    out=bufA3[:, 4 * slot:4 * slot + 4, :],
                                    in_=wsrc[:, kbs, ic * 512:(ic + 1) * 512]
                                    ).then_inc(S["hc"], 16)
                        for mc in range(4):
                            for sub in range(4):
                                if mc == 0 and sub == 0:
                                    sync.wait_ge(S["vgc"], (j + 1) * VGC_J)
                                sync.wait_ge(S["vdc"], j * VDC_J + 4 * mc + sub)
                                sync.dma_start(
                                    out=dst11,
                                    in_=downT_v[:, sub * 11:(sub + 1) * 11,
                                                mc * 512:(mc + 1) * 512]
                                    ).then_inc(S["hc"], 16)

                @blk.vector
                def _(vector):
                    vector.wait_ge(S["phB"], 5)
                    for j in range(B2):
                        hcb = j * HC_J
                        vector.wait_ge(S["hc"], hcb + 32)
                        vector.tensor_add(x1[:, :], x1[:, :], bufA[:, :]
                                          ).then_inc(S["vx1"], 1)
                        ln_vector(vector, x1_3, x1_3, j * SQ_J, j * TQL_J, j * TL_J,
                                  j * EL_J, j * VL_J)
                        if j > 0:
                            vector.wait_ge(S["toc"], j * TOC_J)
                        vector.tensor_copy(xlnb, x1_3).then_inc(S["vln"], 1)
                        for ic in range(NIC):
                            for gi in range(8):
                                g_all = 8 * ic + gi
                                if g_all >= 8:
                                    vector.wait_ge(S["tic"], j * TIC_J + 8 * ic)
                                vector.wait_ge(S["hc"], hcb + 32 + 16 * (g_all + 1))
                                dst = gbf if gi < 4 else ubf
                                sub = gi % 4
                                slot = gi % 4
                                vector.tensor_copy(dst[:, 4 * sub:4 * sub + 4, :],
                                                   bufA3[:, 4 * slot:4 * slot + 4, :]
                                                   ).then_inc(S["vgc"], 1)
                            for ib in range(4):
                                vector.wait_ge(S["egc"],
                                               j * EGC_J + 8 * ic + 2 * (ib + 1))
                                vector.tensor_tensor(
                                    out=h3[:, ic * 4 + ib, :], in0=gsil[:, :],
                                    in1=uscb[:, :], op=ALU.mult).then_inc(S["vh"], 1)
                        for mc in range(4):
                            for sub in range(4):
                                if mc == 0 and sub == 0:
                                    vector.wait_ge(S["tic"], (j + 1) * TIC_J)
                                else:
                                    vector.wait_ge(S["toc"], j * TOC_J + 4 * mc)
                                vector.wait_ge(S["hc"],
                                               hcb + 32 + 128 * NIC + 16 * (4 * mc + sub + 1))
                                vector.tensor_copy(dbf[:, sub * 11:(sub + 1) * 11, :],
                                                   dst11).then_inc(S["vdc"], 1)
                            for mb in range(4):
                                idx = 4 * mc + mb
                                vector.wait_ge(S["edc"], j * EDC_J + idx + 1)
                                vector.tensor_add(pre2_3[:, idx, :], ffn_t[:, :],
                                                  x1_3[:, idx, :]).then_inc(S["vp"], 1)
                        ln_vector(vector, pre2_3, out3, j * SQ_J + 16, j * TQL_J + 16,
                                  j * TL_J + 2, j * EL_J + 5, j * VL_J + 2,
                                  inc=(S["vout"], 1))

                @blk.tensor
                def _(tensor):
                    tensor.wait_ge(S["phB"], 5)
                    for j in range(B2):
                        tensor.wait_ge(S["vx1"], j + 1)
                        ln_tensor(tensor, x1_3, j * SQ_J, j * TQL_J, j * TL_J,
                                  j * EL_J, j * VL_J)
                        tensor.wait_ge(S["vln"], j + 1)
                        for ic in range(NIC):
                            for ib in range(4):
                                tensor.wait_ge(S["egc"],
                                               max(0, j * EGC_J + 8 * ic + 2 * ib - 2))
                                pg = psG if ib % 2 == 0 else psG2
                                pu = psU if ib % 2 == 0 else psU2
                                for kb in range(16):
                                    if ib == 0 and kb % 4 == 0:
                                        tensor.wait_ge(S["vgc"],
                                                       j * VGC_J + 8 * ic + kb // 4 + 1)
                                    mmi = tensor.matmul(pg[:, :],
                                                        gbf[:, kb, ib * 128:(ib + 1) * 128],
                                                        xlnb[:, kb, :],
                                                        start=(kb == 0), stop=(kb == 15))
                                    if kb == 15:
                                        mmi.then_inc(S["tic"], 1)
                                for kb in range(16):
                                    if ib == 0 and kb % 4 == 0:
                                        tensor.wait_ge(S["vgc"],
                                                       j * VGC_J + 8 * ic + 4 + kb // 4 + 1)
                                    mmi = tensor.matmul(pu[:, :],
                                                        ubf[:, kb, ib * 128:(ib + 1) * 128],
                                                        xlnb[:, kb, :],
                                                        start=(kb == 0), stop=(kb == 15))
                                    if kb == 15:
                                        mmi.then_inc(S["tic"], 1)
                        tensor.wait_ge(S["vh"], (j + 1) * VH_J)
                        tensor.wait_ge(S["egc"], (j + 1) * EGC_J)
                        for mc in range(4):
                            for mb in range(4):
                                idx = 4 * mc + mb
                                if idx >= 2:
                                    tensor.wait_ge(S["edc"], j * EDC_J + idx - 1)
                                pot = psG if idx % 2 == 0 else psU
                                for ib in range(44):
                                    if mb == 0 and ib % 11 == 0:
                                        tensor.wait_ge(S["vdc"],
                                                       j * VDC_J + 4 * mc + ib // 11 + 1)
                                    mmi = tensor.matmul(pot[:, :],
                                                        dbf[:, ib, mb * 128:(mb + 1) * 128],
                                                        h3[:, ib, :],
                                                        start=(ib == 0), stop=(ib == 43))
                                    if ib == 43:
                                        mmi.then_inc(S["toc"], 1)
                        tensor.wait_ge(S["vp"], (j + 1) * VP_J)
                        ln_tensor(tensor, pre2_3, j * SQ_J + 16, j * TQL_J + 16,
                                  j * TL_J + 2, j * EL_J + 5, j * VL_J + 2)

                @blk.scalar
                def _(scalar):
                    scalar.wait_ge(S["phB"], 5)
                    for j in range(B2):
                        scalar.wait_ge(S["vx1"], j + 1)
                        ln_scalar(scalar, x1_3, j * SQ_J, j * TQL_J, j * TL_J,
                                  j * EL_J, j * VL_J)
                        for ic in range(NIC):
                            for ib in range(4):
                                pg = psG if ib % 2 == 0 else psG2
                                pu = psU if ib % 2 == 0 else psU2
                                scalar.wait_ge(S["vh"], j * VH_J + 4 * ic + ib)
                                scalar.wait_ge(S["tic"], j * TIC_J + 8 * ic + 2 * ib + 1)
                                scalar.activation(out=gsil[:, :], in_=pg[:, :],
                                                  func=AF.Silu,
                                                  scale=gs_sb[:, ic * 4 + ib:
                                                              ic * 4 + ib + 1])
                                scalar.wait_ge(S["tic"], j * TIC_J + 8 * ic + 2 * ib + 2)
                                scalar.activation(out=uscb[:, :], in_=pu[:, :],
                                                  func=AF.Copy,
                                                  scale=us_sb[:, ic * 4 + ib:
                                                              ic * 4 + ib + 1]
                                                  ).then_inc(S["egc"], 2)
                        for mc in range(4):
                            for mb in range(4):
                                idx = 4 * mc + mb
                                scalar.wait_ge(S["toc"], j * TOC_J + idx + 1)
                                if idx > 0:
                                    scalar.wait_ge(S["vp"], j * VP_J + idx)
                                pot = psG if idx % 2 == 0 else psU
                                scalar.activation(out=ffn_t[:, :], in_=pot[:, :],
                                                  func=AF.Copy,
                                                  scale=ds_sb[:, idx:idx + 1]
                                                  ).then_inc(S["edc"], 1)
                        scalar.wait_ge(S["vp"], (j + 1) * VP_J)
                        ln_scalar(scalar, pre2_3, j * SQ_J + 16, j * TQL_J + 16,
                                  j * TL_J + 2, j * EL_J + 5, j * VL_J + 2)

                @blk.gpsimd
                def _(gpsimd):
                    gpsimd.wait_ge(S["phB"], 5)
                    for j in range(B2):
                        gpsimd.wait_ge(S["vout"], j + 1)
                        gpsimd.dma_start(out=outy_v[j, :, :, :], in_=out3[:, :, :]
                                         ).then_inc(S["soc"], 16)
                    gpsimd.wait_ge(S["soc"], 16 * B2)

    return nc


def kernel(**inp):
    f32 = np.float32

    def pshape(a, nblk):
        x = np.asarray(a)
        rest = x.shape[1:]
        return np.ascontiguousarray(
            x.reshape(nblk, 128, *rest).transpose(1, 0, *range(2, 2 + len(rest)))
            .reshape(128, -1))

    hid_all = np.asarray(inp["L0_attn_hidden_states"])
    w_in = np.asarray(inp["L0_attn_W_in"])
    conv_w = np.asarray(inp["L0_attn_conv_w"])
    conv_b = np.asarray(inp["L0_attn_conv_b"])
    w_x = np.asarray(inp["L0_attn_W_x"])
    w_dt = np.asarray(inp["L0_attn_W_dt"])
    dt_bias = np.asarray(inp["L0_attn_dt_bias"])
    a_m = np.asarray(inp["L0_attn_A"])
    d_v = np.asarray(inp["L0_attn_D"])
    w_out = np.asarray(inp["L0_attn_W_out"])
    gate_w = np.asarray(inp["L0_ffn_gate_w"])
    up_w = np.asarray(inp["L0_ffn_up_w"])
    down_w = np.asarray(inp["L0_ffn_down_w"])
    gate_s = np.asarray(inp["L0_ffn_gate_s"])
    up_s = np.asarray(inp["L0_ffn_up_s"])
    down_s = np.asarray(inp["L0_ffn_down_s"])

    gateT_h = pshape(np.ascontiguousarray(gate_w.T), 16).astype(f32)
    upT_h = pshape(np.ascontiguousarray(up_w.T), 16).astype(f32)
    downT_h = pshape(np.ascontiguousarray(down_w.T), 44).astype(f32)
    gs_h = pshape(gate_s.reshape(-1, 1), 44).astype(f32)
    us_h = pshape(up_s.reshape(-1, 1), 44).astype(f32)
    ds_h = pshape(down_s.reshape(-1, 1), 16).astype(f32)

    in_maps = []
    for c in range(8):
        p, q = c // 4, c % 4
        dq = slice(1024 * q, 1024 * (q + 1))
        hid_fm = np.ascontiguousarray(hid_all[2 * p:2 * p + 2].transpose(0, 2, 1))
        hid_h = np.stack([pshape(hid_fm[jj], 16) for jj in range(2)])
        hidq_h = np.stack([pshape(np.ascontiguousarray(
            hid_fm[jj][:, 512 * q:512 * (q + 1)]), 16) for jj in range(2)])
        winT_q = np.ascontiguousarray(
            np.concatenate([w_in[dq], w_in[4096 + 1024 * q:4096 + 1024 * (q + 1)]], 0).T)
        im = {
            "hid": hid_h.astype(f32),
            "hidq": hidq_h.astype(f32),
            "winT": pshape(winT_q, 16).astype(f32),
            "wxT": pshape(np.ascontiguousarray(w_x[:, dq].T), 8).astype(f32),
            "wdtT": np.ascontiguousarray(w_dt[dq].T).astype(f32),
            "woutT": pshape(np.ascontiguousarray(w_out[:, dq].T), 8).astype(f32),
            "convw": pshape(conv_w[dq], 8).astype(f32),
            "convb": pshape(conv_b[dq].reshape(-1, 1), 8).astype(f32),
            "dtb": pshape(dt_bias[dq].reshape(-1, 1), 8).astype(f32),
            "a_mat": pshape(a_m[dq], 8).astype(f32),
            "d_vec": pshape(d_v[dq].reshape(-1, 1), 8).astype(f32),
            "gateT": gateT_h, "upT": upT_h, "downT": downT_h,
            "gs": gs_h, "us": us_h, "ds": ds_h,
        }
        in_maps.append(im)

    nc = build_nc()
    res = run_bass_kernel_spmd(nc, in_maps, core_ids=list(range(8)))

    out = np.empty((4, 2048, 2048), f32)
    for c in range(8):
        p, q = c // 4, c % 4
        oc = np.asarray(res.results[c]["outy"])
        for jj in range(2):
            om = oc[jj].reshape(128, 16, 512).transpose(1, 0, 2).reshape(2048, 512)
            out[2 * p + jj, 512 * q:512 * (q + 1), :] = om.T
    return out
